# version 01
# baseline (speedup 1.0000x reference)
import numpy as np
import ml_dtypes
from contextlib import ExitStack

import concourse.bass as bass
import concourse.mybir as mybir
from concourse.bass_utils import run_bass_kernel_spmd

F32 = mybir.dt.float32
BF16 = mybir.dt.bfloat16
AF = mybir.ActivationFunctionType
ALU = mybir.AluOpType
AX = mybir.AxisListType

E_MODEL = 1024
S = 4096
HIDDEN = 4096
NEXP = 8
LN_EPS = 1e-5
NORM_EPS = 1e-12
ROPE_BASE = 10000.0

ENGINES = ("pe", "act", "dve", "sp", "gp")
PAIRS = [(0, 7), (1, 6), (2, 5), (3, 4)]


def host_constants():
    e = E_MODEL
    inv = ROPE_BASE ** (-(np.arange(0, e, 2, dtype=np.float64) / e))
    emb = np.repeat(inv, 2)
    sin_v = np.sin(S * emb).astype(np.float32)
    cos_v = np.cos(S * emb).astype(np.float32)
    s1 = sin_v.copy()
    s1[: e // 2] = -s1[: e // 2]
    cosB = np.broadcast_to(cos_v, (128, e)).copy()
    s1B = np.broadcast_to(s1, (128, e)).copy()
    tri = np.zeros((128, 4 * 512), dtype=ml_dtypes.bfloat16)
    pp = np.arange(128)[:, None]
    ff = np.arange(512)[None, :]
    for i in range(4):
        tri[:, i * 512:(i + 1) * 512] = (ff >= pp + 128 * i).astype(ml_dtypes.bfloat16)
    idf32 = np.eye(128, dtype=np.float32)
    idbf = np.eye(128, dtype=ml_dtypes.bfloat16)
    ones1x128 = np.ones((1, 128), dtype=ml_dtypes.bfloat16)
    onesd = np.ones((128, 1), dtype=ml_dtypes.bfloat16)
    indh = np.zeros((128, 2), dtype=ml_dtypes.bfloat16)
    indh[0:64, 0] = 1
    indh[64:128, 1] = 1
    ind2 = np.zeros((2, 128), dtype=ml_dtypes.bfloat16)
    ind2[0, 0:64] = 1
    ind2[1, 64:128] = 1
    ind_lo = np.zeros((1, 128), dtype=ml_dtypes.bfloat16)
    ind_lo[0, 0:64] = 1
    ind_hi = np.zeros((1, 128), dtype=ml_dtypes.bfloat16)
    ind_hi[0, 64:128] = 1
    onehot = np.zeros((8, 8 * 128), dtype=ml_dtypes.bfloat16)
    for ee in range(8):
        onehot[ee, ee * 128:(ee + 1) * 128] = 1
    return dict(cosB=cosB, s1B=s1B, tri=tri, idf32=idf32, idbf=idbf,
                ones1x128=ones1x128, onesd=onesd, indh=indh, ind2=ind2,
                ind_lo=ind_lo, ind_hi=ind_hi, onehot=onehot)


class Prog:

    GUARDS = {
        "pe": ("act", "dve"),
        "act": ("pe", "dve"),
        "dve": ("pe", "act"),
        "sp": ("pe", "act", "dve", "gp"),
        "gp": ("pe", "act", "dve", "sp"),
    }

    def __init__(self):
        self.ops = []
        self.cnt = {e: 0 for e in ENGINES}
        self.floors = {}

    def set_floor(self, eng, val):
        if val:
            self.floors[eng] = max(self.floors.get(eng, 0), val)

    def op(self, eng, fn, inc=0, waits=(), guard=True):
        w = dict()
        for (fe, fv) in self.floors.items():
            if fe != eng and fv:
                w[fe] = fv
        if guard:
            for ge in self.GUARDS[eng]:
                if self.cnt[ge]:
                    w[ge] = self.cnt[ge]
        for (we, wv) in waits:
            if wv is None:
                continue
            assert wv <= self.cnt[we], f"wait on future {we}:{wv} > {self.cnt[we]}"
            w[we] = max(w.get(we, 0), wv)
        self.ops.append((eng, fn, inc, sorted(w.items())))
        if inc:
            self.cnt[eng] += inc
            return self.cnt[eng]
        return None

    def pe(self, fn, waits=(), guard=True):
        return self.op("pe", fn, 1, waits, guard)

    def act(self, fn, waits=(), guard=True):
        return self.op("act", fn, 1, waits, guard)

    def dve(self, fn, waits=(), guard=True):
        return self.op("dve", fn, 1, waits, guard)

    def sp_dma(self, fn, waits=(), guard=True):
        return self.op("sp", fn, 16, waits, guard)

    def gp_dma(self, fn, waits=(), guard=True):
        return self.op("gp", fn, 16, waits, guard)

    def gp(self, fn, waits=(), guard=True, inc=1):
        return self.op("gp", fn, inc, waits, guard)

    def now(self, eng):
        return self.cnt[eng]

    def replay(self, block, sems):
        def make(engname):
            def emit(eng):
                seen = {e: 0 for e in ENGINES}
                for (e, fn, inc, waits) in self.ops:
                    if e != engname:
                        continue
                    for (we, wv) in waits:
                        if wv > seen[we]:
                            eng.wait_ge(sems[we], wv)
                            seen[we] = wv
                    inst = fn(eng)
                    if inc:
                        assert inst is not None
                        inst.then_inc(sems[engname], inc)
            return emit
        block.tensor(make("pe"))
        block.scalar(make("act"))
        block.vector(make("dve"))
        block.sync(make("sp"))
        block.gpsimd(make("gp"))


def build_program(debug=(), phases=5):
    nc = bass.Bass("TRN2", target_bir_lowering=False)
    p = Prog()

    d_x = nc.dram_tensor("x_batch", [S, E_MODEL], F32, kind="ExternalInput")
    d_xown = nc.dram_tensor("x_own", [2048, E_MODEL], F32, kind="ExternalInput")
    d_wqkv = nc.dram_tensor("w_qkv", [E_MODEL, 3 * E_MODEL], F32, kind="ExternalInput")
    d_wout = nc.dram_tensor("w_out", [E_MODEL, E_MODEL], F32, kind="ExternalInput")
    d_gamma = nc.dram_tensor("gamma", [128, 8], F32, kind="ExternalInput")
    d_beta = nc.dram_tensor("beta", [128, 8], F32, kind="ExternalInput")
    d_qks = nc.dram_tensor("qkscale", [128, 1], F32, kind="ExternalInput")
    d_sel = nc.dram_tensor("sel", [128, 2], F32, kind="ExternalInput")
    d_rw = nc.dram_tensor("router_w", [128, 64], F32, kind="ExternalInput")
    d_rb = nc.dram_tensor("router_b", [8, 1], F32, kind="ExternalInput")
    d_w1 = nc.dram_tensor("w1", [NEXP, E_MODEL, 2 * HIDDEN], F32, kind="ExternalInput")
    d_b1a = nc.dram_tensor("b1a", [128, NEXP * 32], F32, kind="ExternalInput")
    d_b1g = nc.dram_tensor("b1g", [128, NEXP * 32], F32, kind="ExternalInput")
    d_w2 = nc.dram_tensor("w2", [NEXP, HIDDEN, E_MODEL], F32, kind="ExternalInput")
    d_b2 = nc.dram_tensor("b2", [8, E_MODEL], F32, kind="ExternalInput")
    CON = host_constants()
    d_con = {k: nc.dram_tensor("c_" + k, list(CON[k].shape),
                               BF16 if CON[k].dtype == ml_dtypes.bfloat16 else F32,
                               kind="ExternalInput") for k in CON}
    d_y = nc.dram_tensor("y", [2048, E_MODEL], F32, kind="ExternalOutput")
    d_scr_tn = nc.dram_tensor("scr_tn", [128, 8 * 4096], BF16, kind="Internal")
    d_scr_oo = nc.dram_tensor("scr_oo", [128, 8 * 2048], BF16, kind="Internal")
    d_scr_t1 = nc.dram_tensor("scr_t1", [128, 8 * 2048], BF16, kind="Internal")

    dbg = {}
    for name, shape, dt in debug:
        dbg[name] = nc.dram_tensor("dbg_" + name, shape, dt, kind="ExternalOutput")

    ctx = ExitStack()
    sb = lambda name, shape, dt: ctx.enter_context(nc.sbuf_tensor(name, shape, dt))

    c = {}
    for k in CON:
        if k in ("tri", "onehot"):
            continue
        shp = list(CON[k].shape)
        c[k] = sb("sc_" + k, shp, BF16 if CON[k].dtype == ml_dtypes.bfloat16 else F32)
    t_gamma = sb("t_gamma", [128, 8], F32)
    t_beta = sb("t_beta", [128, 8], F32)
    t_qks = sb("t_qks", [128, 1], F32)
    t_sel = sb("t_sel", [128, 2], F32)
    t_rb = sb("t_rb", [8, 1], F32)
    t_eps = sb("t_eps", [128, 1], F32)
    nc.const_aps.register(F32, LN_EPS, t_eps[:, 0:1])
    t_betak = sb("t_betak", [128, 8], F32)
    t_betaq = sb("t_betaq", [128, 8], F32)
    t_betav = sb("t_betav", [1, 1024], BF16)

    for (dst, src) in ([(c[k], d_con[k]) for k in CON if k not in ("tri", "onehot")] +
                       [(t_gamma, d_gamma), (t_beta, d_beta), (t_qks, d_qks),
                        (t_sel, d_sel), (t_rb, d_rb)]):
        p.sp_dma(lambda e, dst=dst, src=src: e.dma_start(out=dst[:, :], in_=src.ap()[:, :]),
                 guard=False)
    p.set_floor("sp", p.now("sp"))
    p.dve(lambda e: e.memset(t_eps[:, 0:1], LN_EPS), guard=False)
    p.set_floor("dve", p.now("dve"))

    def prep_wpart(stk, cols_off, dest, do_rope, beta_cols, beta_row, tag):
        with ExitStack() as ps:
            asbp = lambda n, shp, dt: ps.enter_context(nc.sbuf_tensor(n + "_" + tag, shp, dt))
            wld = asbp("wld", [128, 1024], F32)
            wg = asbp("wg", [128, 1024], F32)
            prm = asbp("prm", [128, 1024], F32)
            m1 = asbp("m1", [128, 1024], F32)
            brow = asbp("brow", [1, 1024], F32)
            brr = asbp("brr", [1, 1024], F32)
            bp1 = ps.enter_context(nc.psum_tensor("bp1_" + tag, [1, 512], F32))
            bp2 = ps.enter_context(nc.psum_tensor("bp2_" + tag, [1, 512], F32))
            btp = ps.enter_context(nc.psum_tensor("btp_" + tag, [128, 8], F32))
            for dt in range(8):
                tok_ld = p.sp_dma(lambda e, dt=dt: e.dma_start(
                    out=wld[:, :],
                    in_=d_wqkv.ap()[dt * 128:(dt + 1) * 128, cols_off:cols_off + 1024]))
                for half in range(2):
                    p.pe(lambda e, dt=dt, half=half: e.matmul(
                        (bp1 if half == 0 else bp2)[:, :], t_beta[:, dt:dt + 1],
                        wld[:, half * 512:(half + 1) * 512],
                        start=(dt == 0), stop=(dt == 7)), waits=[("sp", tok_ld)])
                p.dve(lambda e, dt=dt: e.tensor_scalar(
                    wg[:, :], wld[:, :], t_gamma[:, dt:dt + 1], None, ALU.mult),
                    waits=[("sp", tok_ld)])
                if do_rope:
                    p.dve(lambda e: e.tensor_copy(prm[:, 0:512], bass.AP(wg, 1, [[1024, 128], [2, 512]])))
                    p.dve(lambda e: e.tensor_copy(prm[:, 512:1024], bass.AP(wg, 0, [[1024, 128], [2, 512]])))
                    p.dve(lambda e: e.tensor_tensor(m1[:, :], wg[:, :], c["cosB"][:, :], ALU.mult))
                    p.dve(lambda e: e.tensor_tensor(prm[:, :], prm[:, :], c["s1B"][:, :], ALU.mult))
                    p.dve(lambda e, dt=dt: e.tensor_tensor(
                        dest[:, dt * 1024:(dt + 1) * 1024], m1[:, :], prm[:, :], ALU.add))
                else:
                    p.dve(lambda e, dt=dt: e.tensor_copy(dest[:, dt * 1024:(dt + 1) * 1024], wg[:, :]))
            p.dve(lambda e: e.tensor_copy(brow[0:1, 0:512], bp1[:, :]))
            tok = p.dve(lambda e: e.tensor_copy(brow[0:1, 512:1024], bp2[:, :]))
            if do_rope:
                p.dve(lambda e: e.tensor_copy(brr[0:1, 0:512], bass.AP(brow, 1, [[1024, 1], [2, 512]])))
                p.dve(lambda e: e.tensor_copy(brr[0:1, 512:1024], bass.AP(brow, 0, [[1024, 1], [2, 512]])))
                p.dve(lambda e: e.tensor_tensor(wg[0:1, :], brow[0:1, :], c["cosB"][0:1, :], ALU.mult))
                p.dve(lambda e: e.tensor_tensor(wld[0:1, :], brr[0:1, :], c["s1B"][0:1, :], ALU.mult))
                tok = p.dve(lambda e: e.tensor_tensor(brow[0:1, :], wg[0:1, :], wld[0:1, :], ALU.add))
            if beta_row is not None:
                p.dve(lambda e: e.tensor_copy(beta_row[0:1, :], brow[0:1, :]))
            if beta_cols is not None:
                for fc in range(8):
                    p.pe(lambda e, fc=fc: e.transpose(
                        btp[:, fc:fc + 1], brow[0:1, fc * 128:(fc + 1) * 128], c["idf32"][0:1, 0:1]),
                        waits=[("dve", tok)])
                p.dve(lambda e: e.tensor_copy(beta_cols[:, :], btp[:, :]))

    knctx = ExitStack()
    kn = knctx.enter_context(nc.sbuf_tensor("kn", [128, 8 * 4096], BF16))
    vaug = knctx.enter_context(nc.sbuf_tensor("vaug", [128, 32 * 1040], BF16))

    with ExitStack() as actx:
        asb = lambda n, shp, dt: actx.enter_context(nc.sbuf_tensor(n, shp, dt))
        wmodkv = asb("wmodkv", [128, 8 * 2048], BF16)
        with ExitStack() as ps:
            wld = ps.enter_context(nc.sbuf_tensor("wld_kv", [128, 1024], F32))
            wg = ps.enter_context(nc.sbuf_tensor("wg_kv", [128, 1024], F32))
            prm = ps.enter_context(nc.sbuf_tensor("prm_kv", [128, 1024], F32))
            m1 = ps.enter_context(nc.sbuf_tensor("m1_kv", [128, 1024], F32))
            bk1 = ps.enter_context(nc.psum_tensor("bk1", [1, 512], F32))
            bk2 = ps.enter_context(nc.psum_tensor("bk2", [1, 512], F32))
            bv1 = ps.enter_context(nc.psum_tensor("bv1", [1, 512], F32))
            bv2 = ps.enter_context(nc.psum_tensor("bv2", [1, 512], F32))
            btp = ps.enter_context(nc.psum_tensor("btp_kv", [128, 8], F32))
            brow = m1[0:1, 0:1024]
            brr = prm[0:1, 0:1024]
            for dt in range(8):
                tok_ld = p.sp_dma(lambda e, dt=dt: e.dma_start(
                    out=wld[:, :], in_=d_wqkv.ap()[dt * 128:(dt + 1) * 128, 1024:2048]))
                for half in range(2):
                    p.pe(lambda e, dt=dt, half=half: e.matmul(
                        (bk1 if half == 0 else bk2)[:, :], t_beta[:, dt:dt + 1],
                        wld[:, half * 512:(half + 1) * 512],
                        start=(dt == 0), stop=(dt == 7)), waits=[("sp", tok_ld)])
                p.dve(lambda e, dt=dt: e.tensor_scalar(
                    wg[:, :], wld[:, :], t_gamma[:, dt:dt + 1], None, ALU.mult),
                    waits=[("sp", tok_ld)])
                p.dve(lambda e: e.tensor_copy(prm[:, 0:512], bass.AP(wg, 1, [[1024, 128], [2, 512]])))
                p.dve(lambda e: e.tensor_copy(prm[:, 512:1024], bass.AP(wg, 0, [[1024, 128], [2, 512]])))
                p.dve(lambda e: e.tensor_tensor(m1[:, :], wg[:, :], c["cosB"][:, :], ALU.mult))
                p.dve(lambda e: e.tensor_tensor(prm[:, :], prm[:, :], c["s1B"][:, :], ALU.mult))
                p.dve(lambda e, dt=dt: e.tensor_tensor(
                    wmodkv[:, dt * 2048:dt * 2048 + 1024], m1[:, :], prm[:, :], ALU.add))
                tok_ld = p.sp_dma(lambda e, dt=dt: e.dma_start(
                    out=wld[:, :], in_=d_wqkv.ap()[dt * 128:(dt + 1) * 128, 2048:3072]))
                for half in range(2):
                    p.pe(lambda e, dt=dt, half=half: e.matmul(
                        (bv1 if half == 0 else bv2)[:, :], t_beta[:, dt:dt + 1],
                        wld[:, half * 512:(half + 1) * 512],
                        start=(dt == 0), stop=(dt == 7)), waits=[("sp", tok_ld)])
                p.dve(lambda e, dt=dt: e.tensor_scalar(
                    wmodkv[:, dt * 2048 + 1024:(dt + 1) * 2048], wld[:, :],
                    t_gamma[:, dt:dt + 1], None, ALU.mult), waits=[("sp", tok_ld)])
            p.dve(lambda e: e.tensor_copy(brr[0:1, :], bk1[:, :]) if False else e.tensor_copy(brow[0:1, 0:512], bk1[:, :]))
            p.dve(lambda e: e.tensor_copy(brow[0:1, 512:1024], bk2[:, :]))
            p.dve(lambda e: e.tensor_copy(brr[0:1, 0:512], bass.AP(m1, 1, [[1024, 1], [2, 512]])))
            p.dve(lambda e: e.tensor_copy(brr[0:1, 512:1024], bass.AP(m1, 0, [[1024, 1], [2, 512]])))
            p.dve(lambda e: e.tensor_tensor(wg[0:1, :], brow[0:1, :], c["cosB"][0:1, :], ALU.mult))
            p.dve(lambda e: e.tensor_tensor(wld[0:1, :], brr[0:1, :], c["s1B"][0:1, :], ALU.mult))
            tok = p.dve(lambda e: e.tensor_tensor(brow[0:1, :], wg[0:1, :], wld[0:1, :], ALU.add))
            for fc in range(8):
                p.pe(lambda e, fc=fc: e.transpose(
                    btp[:, fc:fc + 1], brow[0:1, fc * 128:(fc + 1) * 128], c["idf32"][0:1, 0:1]),
                    waits=[("dve", tok)])
            p.dve(lambda e: e.tensor_copy(t_betak[:, :], btp[:, :]))
            p.dve(lambda e: e.tensor_copy(brow[0:1, 0:512], bv1[:, :]))
            p.dve(lambda e: e.tensor_copy(brow[0:1, 512:1024], bv2[:, :]))
            p.dve(lambda e: e.tensor_copy(t_betav[0:1, :], brow[0:1, :]))

        p.dve(lambda e: e.memset(
            bass.AP(vaug, 64, [[32 * 1040, 128], [1040, 32], [65, 16], [1, 1]]), 1.0))

        xld = asb("xld", [128, 1024], F32)
        xsq = asb("xsq", [128, 512], F32)
        scol = asb("scol", [128, 12], F32)
        tnt = asb("tnt", [128, 1024], BF16)
        tnf = asb("tnf", [128, 8 * 512], BF16)
        ksb = asb("ksb", [128, 512], F32)
        k2 = asb("k2", [128, 512], BF16)
        nrow = asb("nrow", [2, 512], F32)
        nrow2 = asb("nrow2", [2, 512], F32)
        nrb = asb("nrb", [2, 512], BF16)
        tp_ps = actx.enter_context(nc.psum_tensor("tp_ps", [128, 128], BF16))
        k_ps = actx.enter_context(nc.psum_tensor("k_ps", [128, 512], F32))
        v_ps = actx.enter_context(nc.psum_tensor("v_ps", [128, 512], F32))
        n2_ps = actx.enter_context(nc.psum_tensor("n2_ps", [2, 512], F32))
        rn_ps = actx.enter_context(nc.psum_tensor("rn_ps", [128, 512], F32))

        def ln_stats_cols(tok_mu, tok_s2):
            p.dve(lambda e: e.tensor_scalar(scol[:, 2:3], scol[:, 0:1], 1.0 / 1024, None, ALU.mult),
                  waits=[("dve", tok_mu)])
            p.dve(lambda e: e.tensor_scalar(scol[:, 3:4], scol[:, 7:8], 1.0 / 1024, None, ALU.mult),
                  waits=[("dve", tok_s2)])
            p.dve(lambda e: e.tensor_tensor(scol[:, 4:5], scol[:, 2:3], scol[:, 2:3], ALU.mult))
            p.dve(lambda e: e.tensor_tensor(scol[:, 5:6], scol[:, 3:4], scol[:, 4:5], ALU.subtract))
            tk = p.act(lambda e: e.activation(scol[:, 9:10], scol[:, 5:6], AF.Sqrt, bias=LN_EPS))
            p.dve(lambda e: e.reciprocal(scol[:, 8:9], scol[:, 9:10]), waits=[("act", tk)])
            p.dve(lambda e: e.drain(), guard=False)
            p.dve(lambda e: e.tensor_tensor(scol[:, 9:10], scol[:, 5:6], scol[:, 8:9], ALU.mult))
            p.dve(lambda e: e.tensor_tensor(scol[:, 10:11], scol[:, 9:10], scol[:, 8:9], ALU.mult))
            tk = p.act(lambda e: e.activation(scol[:, 9:10], scol[:, 10:11], AF.Sqrt, bias=LN_EPS))
            p.dve(lambda e: e.reciprocal(scol[:, 10:11], scol[:, 9:10]), waits=[("act", tk)])
            p.dve(lambda e: e.drain(), guard=False)
            p.dve(lambda e: e.tensor_tensor(scol[:, 11:12], scol[:, 10:11], scol[:, 8:9], ALU.mult))

        for tc in range(8):
            for i in range(4):
                r0 = tc * 512 + i * 128
                tok_x = p.sp_dma(lambda e, r0=r0: e.dma_start(
                    out=xld[:, :], in_=d_x.ap()[r0:r0 + 128, :]))
                tok_mu = p.dve(lambda e: e.tensor_reduce(
                    scol[:, 0:1], xld[:, :], AX.X, ALU.add), waits=[("sp", tok_x)])
                tok_sq = p.act(lambda e: e.activation(xsq[:, :], xld[:, 0:512], AF.Square),
                               waits=[("sp", tok_x)])
                p.dve(lambda e: e.tensor_reduce(
                    scol[:, 1:2], xsq[:, :], AX.X, ALU.add), waits=[("act", tok_sq)])
                tok_sq = p.act(lambda e: e.activation(xsq[:, :], xld[:, 512:1024], AF.Square))
                p.dve(lambda e: e.tensor_reduce(
                    scol[:, 6:7], xsq[:, :], AX.X, ALU.add), waits=[("act", tok_sq)])
                p.dve(lambda e: e.drain(), guard=False)
                tok_s2 = p.dve(lambda e: e.tensor_tensor(
                    scol[:, 7:8], scol[:, 1:2], scol[:, 6:7], ALU.add))
                ln_stats_cols(tok_mu, tok_s2)
                if tc == 0 and i == 0 and "scol0" in dbg:
                    p.gp_dma(lambda e: e.dma_start(out=dbg["scol0"].ap()[:, :], in_=scol[:, :]))
                tok_tn = p.dve(lambda e: e.tensor_scalar(
                    tnt[:, :], xld[:, :], scol[:, 2:3], scol[:, 11:12], ALU.subtract, ALU.mult))
                for dt in range(8):
                    tok_tp = p.pe(lambda e, dt=dt: e.transpose(
                        tp_ps[:, :], tnt[:, dt * 128:(dt + 1) * 128], c["idbf"][:, :]),
                        waits=[("dve", tok_tn)])
                    p.dve(lambda e, dt=dt, i=i: e.tensor_copy(
                        tnf[:, dt * 512 + i * 128:dt * 512 + (i + 1) * 128], tp_ps[:, :]),
                        waits=[("pe", tok_tp), ("gp", p.now("gp"))])
            tok_tnf = p.now("dve")
            p.gp_dma(lambda e, tc=tc: e.dma_start(
                out=d_scr_tn.ap()[:, tc * 4096:(tc + 1) * 4096], in_=tnf[:, :]),
                waits=[("dve", tok_tnf)])
            for fc in range(8):
                tok_k = None
                for dt in range(8):
                    tok_k = p.pe(lambda e, fc=fc, dt=dt: e.matmul(
                        k_ps[:, :], wmodkv[:, dt * 2048 + fc * 128:dt * 2048 + (fc + 1) * 128],
                        tnf[:, dt * 512:(dt + 1) * 512],
                        start=(dt == 0), stop=(dt == 7)), waits=[("dve", tok_tnf)])
                tok_ksb = p.act(lambda e, fc=fc: e.add(ksb[:, :], k_ps[:, :], t_betak[:, fc:fc + 1]),
                    waits=[("pe", tok_k)])
                tok_k2 = p.act(lambda e: e.activation(k2[:, :], ksb[:, :], AF.Square))
                tok_n2 = p.pe(lambda e: e.matmul(n2_ps[:, :], c["indh"][:, :], k2[:, :],
                                                 start=True, stop=True), waits=[("act", tok_k2)])
                tok_nr = p.act(lambda e: e.activation(nrow[:, :], n2_ps[:, :], AF.Sqrt),
                               waits=[("pe", tok_n2)])
                p.dve(lambda e: e.tensor_scalar(nrow2[:, :], nrow[:, :], NORM_EPS, None, ALU.max),
                      waits=[("act", tok_nr)])
                p.dve(lambda e: e.reciprocal(nrow[:, :], nrow2[:, :]))
                p.dve(lambda e: e.drain(), guard=False)
                tok_nb = p.dve(lambda e: e.tensor_copy(nrb[:, :], nrow[:, :]))
                tok_rb = p.pe(lambda e: e.matmul(rn_ps[:, :], c["ind2"][:, :], nrb[:, :],
                                                 start=True, stop=True), waits=[("dve", tok_nb)])
                p.dve(lambda e, fc=fc, tc=tc: e.tensor_tensor(
                    kn[:, fc * 4096 + tc * 512:fc * 4096 + (tc + 1) * 512],
                    ksb[:, :], rn_ps[:, :], ALU.mult), waits=[("act", tok_ksb), ("pe", tok_rb)])
            for i in range(4):
                s_slot = tc * 4 + i
                for nh in range(2):
                    for dt in range(8):
                        p.pe(lambda e, i=i, nh=nh, dt=dt: e.matmul(
                            v_ps[:, :], tnf[:, dt * 512 + i * 128:dt * 512 + (i + 1) * 128],
                            wmodkv[:, dt * 2048 + 1024 + nh * 512:dt * 2048 + 1024 + (nh + 1) * 512],
                            start=(dt == 0), stop=False), waits=[("dve", tok_tnf)])
                    tok_v = p.pe(lambda e, nh=nh: e.matmul(
                        v_ps[:, :], c["ones1x128"][:, :], t_betav[0:1, nh * 512:(nh + 1) * 512],
                        start=False, stop=True))
                    p.dve(lambda e, s_slot=s_slot, nh=nh: e.tensor_copy(
                        bass.AP(vaug, s_slot * 1040 + nh * 8 * 65, [[32 * 1040, 128], [65, 8], [1, 64]]),
                        bass.AP(v_ps, 0, [[512, 128], [64, 8], [1, 64]])),
                        waits=[("pe", tok_v)])
            if tc == 0 and "vaug_early" in dbg:
                p.gp_dma(lambda e: e.dma_start(out=dbg["vaug_early"].ap()[:, :],
                                               in_=vaug[:, 0:4 * 1040]))
            if tc == 0 and "wv" in dbg:
                p.gp_dma(lambda e: e.dma_start(out=dbg["wv"].ap()[:, :],
                                               in_=wmodkv[:, 0:2048]))

    if "kn" in dbg:
        p.gp_dma(lambda e: e.dma_start(out=dbg["kn"].ap()[:, :], in_=kn[:, :]))
    for nm, tns in (("xld", xld), ("scol", scol), ("tnt", tnt), ("tnf", tnf)):
        if nm in dbg:
            p.gp_dma(lambda e, nm=nm, tns=tns: e.dma_start(out=dbg[nm].ap()[:, :], in_=tns[:, :]))
    if "vaug" in dbg:
        p.gp_dma(lambda e: e.dma_start(out=dbg["vaug"].ap()[:, 0:16640], in_=vaug[:, 0:16640]))
        p.gp_dma(lambda e: e.dma_start(out=dbg["vaug"].ap()[:, 16640:33280], in_=vaug[:, 16640:33280]))
    if "tnorm" in dbg:
        p.gp_dma(lambda e: e.dma_start(out=dbg["tnorm"].ap()[:, :], in_=d_scr_tn.ap()[:, :]))

    if phases < 2:
        bctx = None
    if phases >= 2:
      with ExitStack() as bctx:
        bsb = lambda n, shp, dt: bctx.enter_context(nc.sbuf_tensor(n, shp, dt))
        wmodq = bsb("wmodq", [128, 8 * 1024], BF16)
        c["tri"] = bsb("sc_tri", list(CON["tri"].shape), BF16)
        p.sp_dma(lambda e: e.dma_start(out=c["tri"][:, :], in_=d_con["tri"].ap()[:, :]))
        p.set_floor("sp", p.now("sp"))
        prep_wpart(bctx, 0, wmodq, True, t_betaq, None, "q")

        qn = bsb("qn", [128, 8 * 1024], BF16)
        tnr = bsb("tnr", [128, 8 * 512], BF16)
        qsb = bsb("qsb", [128, 512], F32)
        q2t = bsb("q2t", [128, 512], BF16)
        nrowq = bsb("nrowq", [2, 512], F32)
        nrowq2 = bsb("nrowq2", [2, 512], F32)
        nrbq = bsb("nrbq", [2, 512], BF16)
        p_sb = bsb("p_sb", [128, 4 * 512], BF16)
        p_m = bsb("p_m", [128, 2 * 512], BF16)

        bc_sb = bsb("bc_sb", [128, 512], F32)
        on_sb = bsb("on_sb", [128, 2 * 512], BF16)
        oo_tmp = bsb("oo_tmp", [128, 512], BF16)
        oo_t = bsb("oo_t", [128, 512], BF16)

        q_ps = bctx.enter_context(nc.psum_tensor("q_ps", [128, 512], F32))
        n2q_ps = bctx.enter_context(nc.psum_tensor("n2q_ps", [2, 512], F32))
        rnq_ps = bctx.enter_context(nc.psum_tensor("rnq_ps", [128, 512], F32))
        sc_ps = [bctx.enter_context(nc.psum_tensor(f"sc_ps{j}", [128, 512], F32)) for j in range(2)]
        o_ps = [bctx.enter_context(nc.psum_tensor(f"o_ps{j}", [65, 512], F32)) for j in range(2)]
        bcb_ps = bctx.enter_context(nc.psum_tensor("bcb_ps", [128, 512], F32))

        if "knl0" in dbg:
            p.gp_dma(lambda e: e.dma_start(out=dbg["knl0"].ap()[:, :], in_=kn[:, 3584:4096]))
        for pi, (glo, ghi) in enumerate(PAIRS):
            if pi == 1 and "knl1" in dbg:
                p.gp_dma(lambda e: e.dma_start(out=dbg["knl1"].ap()[:, :], in_=kn[:, 3584:4096]))
            if pi == 2 and "knl2" in dbg:
                p.gp_dma(lambda e: e.dma_start(out=dbg["knl2"].ap()[:, :], in_=kn[:, 3584:4096]))
            for cl, gch in enumerate((glo, ghi)):
                tok_tnr = p.sp_dma(lambda e, gch=gch: e.dma_start(
                    out=tnr[:, :], in_=d_scr_tn.ap()[:, gch * 4096:(gch + 1) * 4096]))
                for fc in range(8):
                    tok_q = None
                    for dt in range(8):
                        tok_q = p.pe(lambda e, fc=fc, dt=dt: e.matmul(
                            q_ps[:, :], wmodq[:, dt * 1024 + fc * 128:dt * 1024 + (fc + 1) * 128],
                            tnr[:, dt * 512:(dt + 1) * 512],
                            start=(dt == 0), stop=(dt == 7)), waits=[("sp", tok_tnr)])
                    tok_qsb = p.act(lambda e, fc=fc: e.add(qsb[:, :], q_ps[:, :], t_betaq[:, fc:fc + 1]),
                        waits=[("pe", tok_q)])
                    tok_q2 = p.act(lambda e: e.activation(q2t[:, :], qsb[:, :], AF.Square))
                    tok_n2 = p.pe(lambda e: e.matmul(n2q_ps[:, :], c["indh"][:, :], q2t[:, :],
                                                     start=True, stop=True), waits=[("act", tok_q2)])
                    tok_nr = p.act(lambda e: e.activation(nrowq[:, :], n2q_ps[:, :], AF.Sqrt),
                                   waits=[("pe", tok_n2)])
                    p.dve(lambda e: e.tensor_scalar(nrowq2[:, :], nrowq[:, :], NORM_EPS, None, ALU.max),
                          waits=[("act", tok_nr)])
                    p.dve(lambda e: e.reciprocal(nrowq[:, :], nrowq2[:, :]))
                    p.dve(lambda e: e.drain(), guard=False)
                    p.dve(lambda e: e.tensor_scalar(nrowq2[:, :], nrowq[:, :], t_qks[0:2, 0:1],
                                                    None, ALU.mult))
                    tok_nb = p.dve(lambda e: e.tensor_copy(nrbq[:, :], nrowq2[:, :]))
                    tok_rb = p.pe(lambda e: e.matmul(rnq_ps[:, :], c["ind2"][:, :], nrbq[:, :],
                                                     start=True, stop=True), waits=[("dve", tok_nb)])
                    p.dve(lambda e, fc=fc, cl=cl: e.tensor_tensor(
                        qn[:, fc * 1024 + cl * 512:fc * 1024 + (cl + 1) * 512],
                        qsb[:, :], rnq_ps[:, :], ALU.mult), waits=[("act", tok_qsb), ("pe", tok_rb)])
            for t in range(8):
                for cl, gch in enumerate((glo, ghi)):
                    T = 4 * (gch + 1)
                    tok_av = None
                    for s in range(T):
                        buf = s % 2
                        p.pe(lambda e, t=t, s=s, cl=cl: e.matmul(
                            sc_ps[0][:, :],
                            kn[0:64, t * 4096 + s * 128:t * 4096 + (s + 1) * 128],
                            qn[0:64, t * 1024 + cl * 512:t * 1024 + (cl + 1) * 512],
                            start=True, stop=True))
                        tok_s1 = p.pe(lambda e, t=t, s=s, cl=cl: e.matmul(
                            sc_ps[1][:, :],
                            kn[64:128, t * 4096 + s * 128:t * 4096 + (s + 1) * 128],
                            qn[64:128, t * 1024 + cl * 512:t * 1024 + (cl + 1) * 512],
                            start=True, stop=True))
                        tok_e0 = p.act(lambda e, buf=buf: e.activation(
                            p_sb[:, buf * 512:(buf + 1) * 512], sc_ps[0][:, :], AF.Exp),
                            waits=[("pe", tok_s1)])
                        tok_e1 = p.act(lambda e, buf=buf: e.activation(
                            p_sb[:, (2 + buf) * 512:(3 + buf) * 512], sc_ps[1][:, :], AF.Exp))
                        if s >= T - 4:
                            mi = s - (T - 4)
                            tok_m0 = p.dve(lambda e, mi=mi, buf=buf: e.tensor_tensor(
                                p_m[:, 0:512], p_sb[:, buf * 512:(buf + 1) * 512],
                                c["tri"][:, mi * 512:(mi + 1) * 512], ALU.mult),
                                waits=[("act", tok_e0)])
                            tok_m1 = p.dve(lambda e, mi=mi, buf=buf: e.tensor_tensor(
                                p_m[:, 512:1024], p_sb[:, (2 + buf) * 512:(3 + buf) * 512],
                                c["tri"][:, mi * 512:(mi + 1) * 512], ALU.mult),
                                waits=[("act", tok_e1)])
                            u0 = p_m[:, 0:512]
                            u1 = p_m[:, 512:1024]
                            wv0 = [("dve", tok_m0)]
                            wv1 = [("dve", tok_m1)]
                        else:
                            u0 = p_sb[:, buf * 512:(buf + 1) * 512]
                            u1 = p_sb[:, (2 + buf) * 512:(3 + buf) * 512]
                            wv0 = [("act", tok_e0)]
                            wv1 = [("act", tok_e1)]
                        p.pe(lambda e, t=t, s=s, u=u0, T=T: e.matmul(
                            o_ps[0][:, :],
                            vaug[:, s * 1040 + (2 * t) * 65:s * 1040 + (2 * t) * 65 + 65],
                            u, start=(s == 0), stop=(s == T - 1)), waits=wv0)
                        tok_av = p.pe(lambda e, t=t, s=s, u=u1, T=T: e.matmul(
                            o_ps[1][:, :],
                            vaug[:, s * 1040 + (2 * t + 1) * 65:s * 1040 + (2 * t + 1) * 65 + 65],
                            u, start=(s == 0), stop=(s == T - 1)), waits=wv1)
                    rb0 = tnr[0:1, 0:512]
                    rb1 = tnr[0:1, 512:1024]
                    p.dve(lambda e: e.reciprocal(qsb[0:1, 0:512], o_ps[0][64:65, :]),
                          waits=[("pe", tok_av)])
                    p.dve(lambda e: e.drain(), guard=False)
                    p.dve(lambda e: e.tensor_copy(rb0, qsb[0:1, 0:512]))
                    p.dve(lambda e: e.reciprocal(qsb[0:1, 0:512], o_ps[1][64:65, :]))
                    p.dve(lambda e: e.drain(), guard=False)
                    tok_rcb = p.dve(lambda e: e.tensor_copy(rb1, qsb[0:1, 0:512]))
                    p.pe(lambda e: e.matmul(bcb_ps[:, :], c["ind_lo"][:, :], rb0,
                                            start=True, stop=False), waits=[("dve", tok_rcb)])
                    tok_bc = p.pe(lambda e: e.matmul(bcb_ps[:, :], c["ind_hi"][:, :], rb1,
                                                     start=False, stop=True))
                    p.dve(lambda e: e.tensor_copy(bc_sb[:, :], bcb_ps[:, :]), waits=[("pe", tok_bc)])
                    p.dve(lambda e, cl=cl: e.tensor_tensor(
                        on_sb[0:64, cl * 512:(cl + 1) * 512], o_ps[0][0:64, :],
                        bc_sb[0:64, :], ALU.mult))
                    p.dve(lambda e, cl=cl: e.tensor_tensor(
                        on_sb[64:128, cl * 512:(cl + 1) * 512], o_ps[1][0:64, :],
                        bc_sb[64:128, :], ALU.mult))
                p.dve(lambda e: e.tensor_scalar(
                    oo_tmp[:, :], on_sb[:, 512:1024], t_sel[:, 1:2], None, ALU.mult))
                tok_sel = p.dve(lambda e: e.scalar_tensor_tensor(
                    oo_t[:, :], on_sb[:, 0:512], t_sel[:, 0:1], oo_tmp[:, :], ALU.mult, ALU.add))
                p.gp_dma(lambda e, t=t, pi=pi: e.dma_start(
                    out=d_scr_oo.ap()[:, t * 2048 + pi * 512:t * 2048 + (pi + 1) * 512],
                    in_=oo_t[:, :]), waits=[("dve", tok_sel)])

    if phases >= 2 and "oown" in dbg:
        p.gp_dma(lambda e: e.dma_start(out=dbg["oown"].ap()[:, :], in_=d_scr_oo.ap()[:, :]))

    knctx.close()

    x1 = sb("x1", [128, 8 * 2048], F32)
    c["onehot"] = sb("sc_onehot", list(CON["onehot"].shape), BF16)
    t_wrhl = sb("t_wrhl", [128, 2 * 64], BF16)
    t_b1a = sb("t_b1a", [128, NEXP * 32], F32)
    t_b1g = sb("t_b1g", [128, NEXP * 32], F32)
    t_b2b = sb("t_b2b", [8, E_MODEL], BF16)
    t_mrows = sb("t_mrows", [8, 2048], BF16)
    for (dst2, src2) in [(c["onehot"], d_con["onehot"]), (t_b1a, d_b1a), (t_b1g, d_b1g)]:
        p.sp_dma(lambda e, dst2=dst2, src2=src2: e.dma_start(out=dst2[:, :], in_=src2.ap()[:, :]))
    p.set_floor("sp", p.now("sp"))
    if phases >= 3:
      with ExitStack() as cctx:
        csb = lambda n, shp, dt: cctx.enter_context(nc.sbuf_tensor(n, shp, dt))
        wmodo = csb("wmodo", [128, 8 * 1024], BF16)
        wldo = csb("wldo", [128, 1024], F32)
        for dt in range(8):
            tok_ld = p.sp_dma(lambda e, dt=dt: e.dma_start(
                out=wldo[:, :], in_=d_wout.ap()[dt * 128:(dt + 1) * 128, :]))
            p.dve(lambda e, dt=dt: e.tensor_copy(
                wmodo[:, dt * 1024:(dt + 1) * 1024], wldo[:, :]), waits=[("sp", tok_ld)])
        wr_f = csb("wr_f", [128, 64], F32)
        wr_t = csb("wr_t", [128, 64], F32)
        tok_wr = p.sp_dma(lambda e: e.dma_start(out=wr_f[:, :], in_=d_rw.ap()[:, :]))
        p.dve(lambda e: e.tensor_copy(t_wrhl[:, 0:64], wr_f[:, :]), waits=[("sp", tok_wr)])
        p.dve(lambda e: e.tensor_tensor(wr_t[:, :], wr_f[:, :], t_wrhl[:, 0:64], ALU.subtract))
        p.dve(lambda e: e.tensor_copy(t_wrhl[:, 64:128], wr_t[:, :]))
        b2f = csb("b2f", [8, 1024], F32)
        tok_b2 = p.sp_dma(lambda e: e.dma_start(out=b2f[:, :], in_=d_b2.ap()[:, :]))
        p.dve(lambda e: e.tensor_copy(t_b2b[:, :], b2f[:, :]), waits=[("sp", tok_b2)])

        oo_rd = csb("oo_rd", [128, 8 * 512], BF16)
        xo = csb("xo", [128, 4 * 1024], F32)
        xt_sb = csb("xt_sb", [128, 512], F32)
        x1b = csb("x1b", [128, 512], BF16)
        x1sq = csb("x1sq", [128, 512], BF16)
        rowst = csb("rowst", [1, 8 * 512], F32)
        rwb = csb("rwb", [1, 4 * 512], BF16)
        t1f = csb("t1f", [128, 512], F32)
        t1t = csb("t1t", [128, 512], BF16)
        t1lo = csb("t1lo", [128, 8 * 512], BF16)
        tmpf = csb("tmpf", [128, 512], F32)
        lg_sb = csb("lg_sb", [8, 512], F32)
        mxc = csb("mxc", [128, 8], F32)
        mk_sb = csb("mk_sb", [128, 8], F32)
        ltp_s = csb("ltp_s", [128, 8], F32)

        x1_ps = cctx.enter_context(nc.psum_tensor("x1_ps", [128, 512], F32))
        xt_ps = cctx.enter_context(nc.psum_tensor("xt_ps", [128, 512], F32))
        sts_ps = cctx.enter_context(nc.psum_tensor("sts_ps", [1, 512], F32))
        stq_ps = cctx.enter_context(nc.psum_tensor("stq_ps", [1, 512], F32))
        mub_ps = cctx.enter_context(nc.psum_tensor("mub_ps", [128, 512], F32))
        rb_ps = cctx.enter_context(nc.psum_tensor("rb_ps", [128, 512], F32))
        rt_ps = cctx.enter_context(nc.psum_tensor("rt_ps", [8, 512], F32))
        lt_ps = xt_ps
        mr_ps = xt_ps

        row = lambda j: rowst[0:1, j * 512:(j + 1) * 512]
        for tc in range(4):
            tok_oo = None
            for t in range(8):
                tok_oo = p.sp_dma(lambda e, t=t, tc=tc: e.dma_start(
                    out=oo_rd[:, t * 512:(t + 1) * 512],
                    in_=d_scr_oo.ap()[:, t * 2048 + tc * 512:t * 2048 + (tc + 1) * 512]))
            tok_xo = None
            for i in range(4):
                r0 = tc * 512 + i * 128
                tok_xo = p.sp_dma(lambda e, i=i, r0=r0: e.dma_start(
                    out=xo[:, i * 1024:(i + 1) * 1024], in_=d_xown.ap()[r0:r0 + 128, :]))
            for dc in range(8):
                for i in range(4):
                    tok_tp = p.pe(lambda e, i=i, dc=dc: e.matmul(
                        xt_ps[:, i * 128:(i + 1) * 128],
                        xo[:, i * 1024 + dc * 128:i * 1024 + (dc + 1) * 128],
                        c["idf32"][:, :], start=True, stop=True, is_transpose=True),
                        waits=[("sp", tok_xo)])
                tok_xts = p.dve(lambda e: e.tensor_copy(xt_sb[:, :], xt_ps[:, :]),
                                waits=[("pe", tok_tp)])
                tok_mm = None
                for t in range(8):
                    tok_mm = p.pe(lambda e, t=t, dc=dc: e.matmul(
                        x1_ps[:, :], wmodo[:, t * 1024 + dc * 128:t * 1024 + (dc + 1) * 128],
                        oo_rd[:, t * 512:(t + 1) * 512], start=(t == 0), stop=(t == 7)),
                        waits=[("sp", tok_oo)])
                tok_x1 = p.dve(lambda e, dc=dc, tc=tc: e.tensor_tensor(
                    x1[:, dc * 2048 + tc * 512:dc * 2048 + (tc + 1) * 512],
                    x1_ps[:, :], xt_sb[:, :], ALU.add),
                    waits=[("pe", tok_mm), ("dve", tok_xts)])
                tok_x1b = p.act(lambda e, dc=dc, tc=tc: e.activation(
                    x1b[:, :], x1[:, dc * 2048 + tc * 512:dc * 2048 + (tc + 1) * 512], AF.Copy),
                    waits=[("dve", tok_x1)])
                tok_x1q = p.act(lambda e, dc=dc, tc=tc: e.activation(
                    x1sq[:, :], x1[:, dc * 2048 + tc * 512:dc * 2048 + (tc + 1) * 512], AF.Square))
                p.pe(lambda e, dc=dc: e.matmul(sts_ps[:, :], c["onesd"][:, :], x1b[:, :],
                                               start=(dc == 0), stop=(dc == 7)),
                     waits=[("act", tok_x1b)])
                tok_st = p.pe(lambda e, dc=dc: e.matmul(stq_ps[:, :], c["onesd"][:, :], x1sq[:, :],
                                                        start=(dc == 0), stop=(dc == 7)),
                              waits=[("act", tok_x1q)])
            p.dve(lambda e: e.tensor_scalar(row(0), sts_ps[:, :], 1.0 / 1024, None, ALU.mult),
                  waits=[("pe", tok_st)])
            p.dve(lambda e: e.tensor_scalar(row(1), stq_ps[:, :], 1.0 / 1024, None, ALU.mult))
            p.dve(lambda e: e.tensor_tensor(row(2), row(0), row(0), ALU.mult))
            p.dve(lambda e: e.tensor_tensor(row(3), row(1), row(2), ALU.subtract))
            tk = p.act(lambda e: e.activation(row(4), row(3), AF.Sqrt, bias=LN_EPS))
            p.dve(lambda e: e.reciprocal(row(5), row(4)), waits=[("act", tk)])
            p.dve(lambda e: e.drain(), guard=False)
            p.dve(lambda e: e.tensor_copy(rwb[0:1, 0:512], row(0)))
            p.dve(lambda e: e.tensor_copy(row(6), rwb[0:1, 0:512]))
            p.dve(lambda e: e.tensor_tensor(row(7), row(0), row(6), ALU.subtract))
            p.dve(lambda e: e.tensor_copy(rwb[0:1, 512:1024], row(7)))
            p.dve(lambda e: e.tensor_copy(rwb[0:1, 1024:1536], row(5)))
            p.dve(lambda e: e.tensor_copy(row(6), rwb[0:1, 1024:1536]))
            p.dve(lambda e: e.tensor_tensor(row(7), row(5), row(6), ALU.subtract))
            tok_rw = p.dve(lambda e: e.tensor_copy(rwb[0:1, 1536:2048], row(7)))
            p.pe(lambda e: e.matmul(mub_ps[:, :], c["ones1x128"][:, :], rwb[0:1, 0:512],
                                    start=True, stop=False), waits=[("dve", tok_rw)])
            tok_mub = p.pe(lambda e: e.matmul(mub_ps[:, :], c["ones1x128"][:, :],
                                              rwb[0:1, 512:1024], start=False, stop=True))
            p.pe(lambda e: e.matmul(rb_ps[:, :], c["ones1x128"][:, :], rwb[0:1, 1024:1536],
                                    start=True, stop=False))
            tok_rbb = p.pe(lambda e: e.matmul(rb_ps[:, :], c["ones1x128"][:, :],
                                              rwb[0:1, 1536:2048], start=False, stop=True))
            for dc in range(8):
                p.dve(lambda e, dc=dc, tc=tc: e.tensor_tensor(
                    tmpf[:, :], x1[:, dc * 2048 + tc * 512:dc * 2048 + (tc + 1) * 512],
                    mub_ps[:, :], ALU.subtract), waits=[("pe", tok_mub), ("pe", tok_rbb)])
                p.dve(lambda e: e.tensor_tensor(t1f[:, :], tmpf[:, :], rb_ps[:, :], ALU.mult))
                tok_t1 = p.dve(lambda e: e.tensor_copy(t1t[:, :], t1f[:, :]))
                tok_t1l = p.dve(lambda e, dc=dc: e.tensor_tensor(
                    t1lo[:, dc * 512:(dc + 1) * 512], t1f[:, :], t1t[:, :], ALU.subtract))
                p.gp_dma(lambda e, dc=dc, tc=tc: e.dma_start(
                    out=d_scr_t1.ap()[:, dc * 2048 + tc * 512:dc * 2048 + (tc + 1) * 512],
                    in_=t1t[:, :]), waits=[("dve", tok_t1)])
                p.pe(lambda e, dc=dc: e.matmul(
                    rt_ps[:, :], t_wrhl[:, dc * 8:(dc + 1) * 8], t1t[:, :],
                    start=(dc == 0), stop=False), waits=[("dve", tok_t1)])
                p.pe(lambda e, dc=dc: e.matmul(
                    rt_ps[:, :], t_wrhl[:, 64 + dc * 8:64 + (dc + 1) * 8], t1t[:, :],
                    start=False, stop=False))
                tok_rt = p.pe(lambda e, dc=dc: e.matmul(
                    rt_ps[:, :], t_wrhl[:, dc * 8:(dc + 1) * 8], t1lo[:, dc * 512:(dc + 1) * 512],
                    start=False, stop=(dc == 7)), waits=[("dve", tok_t1l)])
            tok_lg = p.act(lambda e: e.add(lg_sb[:, :], rt_ps[:, :], t_rb[0:8, 0:1]), waits=[("pe", tok_rt)])
            for j in range(4):
                tok_lt = p.pe(lambda e, j=j: e.transpose(
                    lt_ps[:, 0:8], lg_sb[:, j * 128:(j + 1) * 128], c["idf32"][0:8, 0:8]),
                    waits=[("act", tok_lg)])
                p.dve(lambda e: e.tensor_copy(ltp_s[:, :], lt_ps[:, 0:8]), waits=[("pe", tok_lt)])
                p.dve(lambda e: e.drain(), guard=False)
                p.dve(lambda e: e.tensor_reduce(mxc[:, 0:1], ltp_s[:, :], AX.X, ALU.max))
                p.dve(lambda e: e.drain(), guard=False)
                p.dve(lambda e: e.memset(mxc[:, 1:2], 0.0))
                p.dve(lambda e: e.drain(), guard=False)
                for ee in range(8):
                    tcur, tnxt = (1, 5) if ee % 2 == 0 else (5, 1)
                    p.dve(lambda e, ee=ee: e.tensor_tensor(
                        mxc[:, 2:3], ltp_s[:, ee:ee + 1], mxc[:, 0:1], ALU.is_ge))
                    p.dve(lambda e: e.drain(), guard=False)
                    p.dve(lambda e, tcur=tcur: e.tensor_scalar(
                        mxc[:, 3:4], mxc[:, tcur:tcur + 1], -1.0, 1.0, ALU.mult, ALU.add))
                    p.dve(lambda e: e.drain(), guard=False)
                    p.dve(lambda e: e.tensor_tensor(mxc[:, 4:5], mxc[:, 2:3], mxc[:, 3:4], ALU.mult))
                    p.dve(lambda e: e.drain(), guard=False)
                    p.dve(lambda e, tcur=tcur, tnxt=tnxt: e.tensor_tensor(
                        mxc[:, tnxt:tnxt + 1], mxc[:, tcur:tcur + 1], mxc[:, 4:5], ALU.add))
                    p.dve(lambda e, ee=ee: e.tensor_copy(mk_sb[:, ee:ee + 1], mxc[:, 4:5]))
                    p.dve(lambda e: e.drain(), guard=False)
                tok_mk = p.now("dve")
                tok_mt = p.pe(lambda e: e.transpose(mr_ps[0:8, 128:256], mk_sb[:, :], c["idf32"][:, :]),
                              waits=[("dve", tok_mk)])
                p.dve(lambda e, tc=tc, j=j: e.tensor_copy(
                    t_mrows[:, tc * 512 + j * 128:tc * 512 + (j + 1) * 128], mr_ps[0:8, 128:256]),
                    waits=[("pe", tok_mt)])

    if phases >= 3 and "x1" in dbg:
        p.gp_dma(lambda e: e.dma_start(out=dbg["x1"].ap()[:, :], in_=x1[:, :]))
    if "mrows" in dbg:
        p.gp_dma(lambda e: e.dma_start(out=dbg["mrows"].ap()[:, :], in_=t_mrows[:, :]))
    if "t1" in dbg:
        p.gp_dma(lambda e: e.dma_start(out=dbg["t1"].ap()[:, :], in_=d_scr_t1.ap()[:, :]))

    if phases >= 4:
      with ExitStack() as dctx:
        dsb = lambda n, shp, dt: dctx.enter_context(nc.sbuf_tensor(n, shp, dt))
        u_sb = dsb("u_sb", [128, 32 * 1024], BF16)
        t1rd = dsb("t1rd", [128, 8 * 1024], BF16)
        apiece = dsb("apiece", [128, 8 * 256], F32)
        gpiece = dsb("gpiece", [128, 8 * 256], F32)
        apb = dsb("apb", [128, 8 * 256], BF16)
        gpb = dsb("gpb", [128, 8 * 256], BF16)
        w2c = dsb("w2c", [128, 128], F32)
        w2b = dsb("w2b", [128, 128], BF16)
        sg_t = dsb("sg_t", [128, 512], F32)
        aa_t = dsb("aa_t", [128, 512], F32)
        mb_sb = dsb("mb_sb", [128, 512], BF16)
        yt = dsb("yt", [128, 512], F32)

        a_ps = dctx.enter_context(nc.psum_tensor("a_ps", [128, 512], F32))
        g_ps = dctx.enter_context(nc.psum_tensor("g_ps", [128, 512], F32))
        y_ps = [dctx.enter_context(nc.psum_tensor(f"y_ps{j}", [128, 512], F32)) for j in range(2)]
        mb_ps = dctx.enter_context(nc.psum_tensor("mb_ps", [128, 512], F32))

        for e_i in range(NEXP):
            if e_i == 2 and "x1e0" in dbg:
                p.gp_dma(lambda e: e.dma_start(out=dbg["x1e0"].ap()[:, :], in_=x1[:, :]))
            for fh in range(2):
                tok_t1r = None
                for dc in range(8):
                    tok_t1r = p.sp_dma(lambda e, dc=dc, fh=fh: e.dma_start(
                        out=t1rd[:, dc * 1024:(dc + 1) * 1024],
                        in_=d_scr_t1.ap()[:, dc * 2048 + fh * 1024:dc * 2048 + (fh + 1) * 1024]))
                for p4 in range(16):
                    tok_ap = None
                    for dt in range(8):
                        p.sp_dma(lambda e, dt=dt, p4=p4, e_i=e_i: e.dma_start(
                            out=apiece[:, dt * 256:(dt + 1) * 256],
                            in_=d_w1.ap()[e_i, dt * 128:(dt + 1) * 128, p4 * 256:(p4 + 1) * 256]))
                        tok_ap = p.sp_dma(lambda e, dt=dt, p4=p4, e_i=e_i: e.dma_start(
                            out=gpiece[:, dt * 256:(dt + 1) * 256],
                            in_=d_w1.ap()[e_i, dt * 128:(dt + 1) * 128,
                                          4096 + p4 * 256:4096 + (p4 + 1) * 256]))
                    p.dve(lambda e: e.tensor_copy(apb[:, :], apiece[:, :]), waits=[("sp", tok_ap)])
                    tok_cb = p.dve(lambda e: e.tensor_copy(gpb[:, :], gpiece[:, :]))
                    for hc2 in range(2):
                        hc = p4 * 2 + hc2
                        for qq in range(2):
                            for dt in range(8):
                                p.pe(lambda e, dt=dt, hc2=hc2, qq=qq: e.matmul(
                                    a_ps[:, :],
                                    apb[:, dt * 256 + hc2 * 128:dt * 256 + (hc2 + 1) * 128],
                                    t1rd[:, dt * 1024 + qq * 512:dt * 1024 + (qq + 1) * 512],
                                    start=(dt == 0), stop=(dt == 7)),
                                    waits=[("dve", tok_cb), ("sp", tok_t1r)])
                                tok_a = p.pe(lambda e, dt=dt, hc2=hc2, qq=qq: e.matmul(
                                    g_ps[:, :],
                                    gpb[:, dt * 256 + hc2 * 128:dt * 256 + (hc2 + 1) * 128],
                                    t1rd[:, dt * 1024 + qq * 512:dt * 1024 + (qq + 1) * 512],
                                    start=(dt == 0), stop=(dt == 7)))
                            tok_sg = p.act(lambda e, hc=hc, e_i=e_i: e.activation(
                                sg_t[:, :], g_ps[:, :], AF.Sigmoid,
                                bias=t_b1g[:, e_i * 32 + hc:e_i * 32 + hc + 1]),
                                waits=[("pe", tok_a)])
                            tok_aa = p.act(lambda e, hc=hc, e_i=e_i: e.add(aa_t[:, :], a_ps[:, :], t_b1a[:, e_i * 32 + hc:e_i * 32 + hc + 1]))
                            p.dve(lambda e, hc=hc, qq=qq: e.tensor_tensor(
                                u_sb[:, hc * 1024 + qq * 512:hc * 1024 + (qq + 1) * 512],
                                aa_t[:, :], sg_t[:, :], ALU.mult),
                                waits=[("act", tok_sg), ("act", tok_aa)])
                tok_u = p.now("dve")
                if e_i == 1 and fh == 0 and "u0" in dbg:
                    p.gp_dma(lambda e: e.dma_start(out=dbg["u0"].ap()[:, 0:16384], in_=u_sb[:, 0:16384]))
                    p.gp_dma(lambda e: e.dma_start(out=dbg["u0"].ap()[:, 16384:32768], in_=u_sb[:, 16384:32768]))
                for qq in range(2):
                    tcq = fh * 1024 + qq * 512
                    tok_mb = p.pe(lambda e, e_i=e_i, tcq=tcq: e.matmul(
                        mb_ps[:, :], c["onehot"][:, e_i * 128:(e_i + 1) * 128],
                        t_mrows[0:8, tcq:tcq + 512], start=True, stop=True),
                        waits=[("dve", tok_u)])
                    tok_mbs = p.dve(lambda e: e.tensor_copy(mb_sb[:, :], mb_ps[:, :]),
                                    waits=[("pe", tok_mb)])
                    if e_i == 1 and fh == 0 and qq == 0 and "mb0" in dbg:
                        p.gp_dma(lambda e: e.dma_start(out=dbg["mb0"].ap()[:, :], in_=mb_sb[:, :]),
                                 waits=[("dve", tok_mbs)])
                    for dc in range(8):
                        jb = dc % 2
                        for hc in range(32):
                            tok_wld = p.sp_dma(lambda e, e_i=e_i, hc=hc, dc=dc: e.dma_start(
                                out=w2c[:, :],
                                in_=d_w2.ap()[e_i, hc * 128:(hc + 1) * 128, dc * 128:(dc + 1) * 128]))
                            tok_wb = p.dve(lambda e: e.tensor_copy(w2b[:, :], w2c[:, :]),
                                           waits=[("sp", tok_wld)])
                            tok_y = p.pe(lambda e, hc=hc, qq=qq, jb=jb: e.matmul(
                                y_ps[jb][:, :], w2b[:, :],
                                u_sb[:, hc * 1024 + qq * 512:hc * 1024 + (qq + 1) * 512],
                                start=(hc == 0), stop=False), waits=[("dve", tok_wb)])
                        tok_y = p.pe(lambda e, dc=dc, tcq=tcq, jb=jb: e.matmul(
                            y_ps[jb][:, :], t_b2b[:, dc * 128:(dc + 1) * 128],
                            t_mrows[0:8, tcq:tcq + 512], start=False, stop=True))
                        p.dve(lambda e, jb=jb: e.tensor_tensor(
                            yt[:, :], y_ps[jb][:, :], mb_sb[:, :], ALU.mult),
                            waits=[("pe", tok_y), ("dve", tok_mbs)])
                        p.dve(lambda e, dc=dc, tcq=tcq: e.tensor_tensor(
                            x1[:, dc * 2048 + tcq:dc * 2048 + tcq + 512],
                            x1[:, dc * 2048 + tcq:dc * 2048 + tcq + 512], yt[:, :], ALU.add))
                        if e_i == 1 and fh == 0 and qq == 0 and dc == 0 and "yt0" in dbg:
                            p.gp_dma(lambda e: e.dma_start(out=dbg["yt0"].ap()[:, :], in_=yt[:, :]))


    if phases >= 4 and "mrend" in dbg:
        p.gp_dma(lambda e: e.dma_start(out=dbg["mrend"].ap()[:, :], in_=t_mrows[:, :]))

    if phases >= 5:
      with ExitStack() as ectx:
        osb = ectx.enter_context(nc.sbuf_tensor("osb", [128, 1024], F32))
        o_ps2 = ectx.enter_context(nc.psum_tensor("o_ps2", [128, 1024], F32))
        for tt in range(16):
            for dc in range(8):
                p.pe(lambda e, tt=tt, dc=dc: e.matmul(
                    o_ps2[:, dc * 128:(dc + 1) * 128],
                    x1[:, dc * 2048 + tt * 128:dc * 2048 + (tt + 1) * 128],
                    c["idf32"][:, :], start=True, stop=True, is_transpose=True))
            tok_cp = p.act(lambda e: e.activation(osb[:, :], o_ps2[:, :], AF.Copy),
                           waits=[("pe", p.now("pe")), ("gp", p.now("gp"))])
            p.gp_dma(lambda e, tt=tt: e.dma_start(
                out=d_y.ap()[tt * 128:(tt + 1) * 128, :], in_=osb[:, :]),
                waits=[("act", tok_cp)])

    p.gp(lambda e: e.nop(), waits=[("gp", p.now("gp"))], guard=False, inc=0)

    with ExitStack() as sctx:
        sems = {e: sctx.enter_context(nc.semaphore(f"sem_{e}")) for e in ENGINES}
        with nc.Block() as block:
            p.replay(block, sems)
    ctx.close()
    return nc


def own_chunks(pp):
    return [PAIRS[pi][pp] for pi in range(4)]


def stage_inputs(inputs, core):
    b, pp = core // 2, core % 2
    x = np.asarray(inputs["L0_attn_x"], dtype=np.float32)
    own = own_chunks(pp)
    x_own = np.concatenate([x[b, g * 512:(g + 1) * 512] for g in own], axis=0)
    gamma = np.asarray(inputs["L0_attn_ln_gamma"], np.float32).reshape(8, 128).T.copy()
    beta = np.asarray(inputs["L0_attn_ln_beta"], np.float32).reshape(8, 128).T.copy()
    qks = np.broadcast_to(np.asarray(inputs["L0_attn_qkscale"], np.float32).reshape(1, 1),
                          (128, 1)).copy()
    sel = np.zeros((128, 2), np.float32)
    sel[:, pp] = 1.0
    b1 = np.asarray(inputs["L0_ffn_b1"], np.float32)
    b1a = np.concatenate([b1[e, :4096].reshape(32, 128).T for e in range(8)], axis=1)
    b1g = np.concatenate([b1[e, 4096:].reshape(32, 128).T for e in range(8)], axis=1)
    m = {
        "x_batch": np.ascontiguousarray(x[b]),
        "x_own": np.ascontiguousarray(x_own),
        "w_qkv": np.ascontiguousarray(np.asarray(inputs["L0_attn_w_qkv"], np.float32)),
        "w_out": np.ascontiguousarray(np.asarray(inputs["L0_attn_w_out"], np.float32)),
        "gamma": gamma, "beta": beta, "qkscale": qks, "sel": sel,
        "router_w": np.ascontiguousarray(
            np.asarray(inputs["L0_ffn_router_w"], np.float32).reshape(8, 128, 8)
            .transpose(1, 0, 2).reshape(128, 64)),
        "router_b": np.asarray(inputs["L0_ffn_router_b"], np.float32).reshape(8, 1).copy(),
        "w1": np.ascontiguousarray(np.asarray(inputs["L0_ffn_w1"], np.float32)),
        "b1a": np.ascontiguousarray(b1a), "b1g": np.ascontiguousarray(b1g),
        "w2": np.ascontiguousarray(np.asarray(inputs["L0_ffn_w2"], np.float32)),
        "b2": np.ascontiguousarray(np.asarray(inputs["L0_ffn_b2"], np.float32)),
    }
    for k, v in host_constants().items():
        m["c_" + k] = np.ascontiguousarray(v)
    return m


def kernel(**inputs):
    nc = build_program()
    in_maps = [stage_inputs(inputs, cc) for cc in range(8)]
    res = run_bass_kernel_spmd(nc, in_maps, core_ids=list(range(8)))
    out = np.zeros((4, 4096, 1024), dtype=np.float32)
    for cc in range(8):
        b, pp = cc // 2, cc % 2
        y = np.asarray(res.results[cc]["y"], np.float32)
        for pi, g in enumerate(own_chunks(pp)):
            out[b, g * 512:(g + 1) * 512] = y[pi * 512:(pi + 1) * 512]
    return out


# version 03
# speedup vs baseline: 2.4663x; 2.4663x over previous
import numpy as np
import ml_dtypes
from contextlib import ExitStack

import concourse.bass as bass
import concourse.mybir as mybir
from concourse.bass_utils import run_bass_kernel_spmd

F32 = mybir.dt.float32
BF16 = mybir.dt.bfloat16
AF = mybir.ActivationFunctionType
ALU = mybir.AluOpType
AX = mybir.AxisListType

E_MODEL = 1024
S = 4096
HIDDEN = 4096
NEXP = 8
LN_EPS = 1e-5
NORM_EPS = 1e-12
ROPE_BASE = 10000.0

ENGINES = ("pe", "act", "dve", "sp", "gp")
PAIRS = [(0, 7), (1, 6), (2, 5), (3, 4)]


def host_constants():
    e = E_MODEL
    inv = ROPE_BASE ** (-(np.arange(0, e, 2, dtype=np.float64) / e))
    emb = np.repeat(inv, 2)
    sin_v = np.sin(S * emb).astype(np.float32)
    cos_v = np.cos(S * emb).astype(np.float32)
    s1 = sin_v.copy()
    s1[: e // 2] = -s1[: e // 2]
    cosB = np.broadcast_to(cos_v, (128, e)).copy()
    s1B = np.broadcast_to(s1, (128, e)).copy()
    tri = np.zeros((128, 4 * 512), dtype=ml_dtypes.bfloat16)
    pp = np.arange(128)[:, None]
    ff = np.arange(512)[None, :]
    for i in range(4):
        tri[:, i * 512:(i + 1) * 512] = (ff >= pp + 128 * i).astype(ml_dtypes.bfloat16)
    idf32 = np.eye(128, dtype=np.float32)
    idbf = np.eye(128, dtype=ml_dtypes.bfloat16)
    ones1x128 = np.ones((1, 128), dtype=ml_dtypes.bfloat16)
    onesd = np.ones((128, 1), dtype=ml_dtypes.bfloat16)
    indh = np.zeros((128, 2), dtype=ml_dtypes.bfloat16)
    indh[0:64, 0] = 1
    indh[64:128, 1] = 1
    ind2 = np.zeros((2, 128), dtype=ml_dtypes.bfloat16)
    ind2[0, 0:64] = 1
    ind2[1, 64:128] = 1
    ind_lo = np.zeros((1, 128), dtype=ml_dtypes.bfloat16)
    ind_lo[0, 0:64] = 1
    ind_hi = np.zeros((1, 128), dtype=ml_dtypes.bfloat16)
    ind_hi[0, 64:128] = 1
    onehot = np.zeros((8, 8 * 128), dtype=ml_dtypes.bfloat16)
    for ee in range(8):
        onehot[ee, ee * 128:(ee + 1) * 128] = 1
    return dict(cosB=cosB, s1B=s1B, tri=tri, idf32=idf32, idbf=idbf,
                ones1x128=ones1x128, onesd=onesd, indh=indh, ind2=ind2,
                ind_lo=ind_lo, ind_hi=ind_hi, onehot=onehot)


class Prog:

    GUARDS = {
        "pe": ("act", "dve"),
        "act": ("pe", "dve"),
        "dve": ("pe", "act"),
        "sp": ("pe", "act", "dve", "gp"),
        "gp": ("pe", "act", "dve", "sp"),
    }

    EXTRA_SEMS = ("w2a", "w2b", "w1s", "t1s")

    def __init__(self):
        self.ops = []
        self.cnt = {e: 0 for e in ENGINES}
        for s in self.EXTRA_SEMS:
            self.cnt[s] = 0
        self.floors = {}

    def set_floor(self, eng, val):
        if val:
            self.floors[eng] = max(self.floors.get(eng, 0), val)

    def op(self, eng, fn, inc=0, waits=(), guard=True, sem=None):
        w = dict()
        for (fe, fv) in self.floors.items():
            if fe != eng and fv:
                w[fe] = fv
        if guard:
            for ge in self.GUARDS[eng]:
                if self.cnt[ge]:
                    w[ge] = self.cnt[ge]
        for (we, wv) in waits:
            if wv is None:
                continue
            assert wv <= self.cnt[we], f"wait on future {we}:{wv} > {self.cnt[we]}"
            w[we] = max(w.get(we, 0), wv)
        key = sem if sem is not None else eng
        self.ops.append((eng, fn, inc, sorted(w.items()), key))
        if inc:
            self.cnt[key] += inc
            return self.cnt[key]
        return None

    def pe(self, fn, waits=(), guard=True):
        return self.op("pe", fn, 1, waits, guard)

    def act(self, fn, waits=(), guard=True):
        return self.op("act", fn, 1, waits, guard)

    def dve(self, fn, waits=(), guard=True):
        return self.op("dve", fn, 1, waits, guard)

    def sp_dma(self, fn, waits=(), guard=True, sem=None):
        return self.op("sp", fn, 16, waits, guard, sem)

    def gp_dma(self, fn, waits=(), guard=True):
        return self.op("gp", fn, 16, waits, guard)

    def gp(self, fn, waits=(), guard=True, inc=1):
        return self.op("gp", fn, inc, waits, guard)

    def now(self, eng):
        return self.cnt[eng]

    def replay(self, block, sems):
        def make(engname):
            def emit(eng):
                seen = {e: 0 for e in self.cnt}
                for (e, fn, inc, waits, key) in self.ops:
                    if e != engname:
                        continue
                    for (we, wv) in waits:
                        if wv > seen[we]:
                            eng.wait_ge(sems[we], wv)
                            seen[we] = wv
                    inst = fn(eng)
                    if inc:
                        assert inst is not None
                        inst.then_inc(sems[key], inc)
            return emit
        block.tensor(make("pe"))
        block.scalar(make("act"))
        block.vector(make("dve"))
        block.sync(make("sp"))
        block.gpsimd(make("gp"))


def build_program(debug=(), phases=5):
    nc = bass.Bass("TRN2", target_bir_lowering=False)
    p = Prog()

    d_x = nc.dram_tensor("x_batch", [S, E_MODEL], F32, kind="ExternalInput")
    d_xown = nc.dram_tensor("x_own", [2048, E_MODEL], F32, kind="ExternalInput")
    d_wqkv = nc.dram_tensor("w_qkv", [E_MODEL, 3 * E_MODEL], F32, kind="ExternalInput")
    d_wout = nc.dram_tensor("w_out", [E_MODEL, E_MODEL], F32, kind="ExternalInput")
    d_gamma = nc.dram_tensor("gamma", [128, 8], F32, kind="ExternalInput")
    d_beta = nc.dram_tensor("beta", [128, 8], F32, kind="ExternalInput")
    d_qks = nc.dram_tensor("qkscale", [128, 1], F32, kind="ExternalInput")
    d_sel = nc.dram_tensor("sel", [128, 2], F32, kind="ExternalInput")
    d_rw = nc.dram_tensor("router_w", [128, 64], F32, kind="ExternalInput")
    d_rb = nc.dram_tensor("router_b", [8, 1], F32, kind="ExternalInput")
    d_w1 = nc.dram_tensor("w1", [NEXP, E_MODEL, 2 * HIDDEN], F32, kind="ExternalInput")
    d_b1a = nc.dram_tensor("b1a", [128, NEXP * 32], F32, kind="ExternalInput")
    d_b1g = nc.dram_tensor("b1g", [128, NEXP * 32], F32, kind="ExternalInput")
    d_w2 = nc.dram_tensor("w2", [NEXP, HIDDEN, E_MODEL], F32, kind="ExternalInput")
    d_b2 = nc.dram_tensor("b2", [8, E_MODEL], F32, kind="ExternalInput")
    CON = host_constants()
    d_con = {k: nc.dram_tensor("c_" + k, list(CON[k].shape),
                               BF16 if CON[k].dtype == ml_dtypes.bfloat16 else F32,
                               kind="ExternalInput") for k in CON}
    d_y = nc.dram_tensor("y", [2048, E_MODEL], F32, kind="ExternalOutput")
    d_scr_tn = nc.dram_tensor("scr_tn", [128, 8 * 4096], BF16, kind="Internal")
    d_scr_oo = nc.dram_tensor("scr_oo", [128, 8 * 2048], BF16, kind="Internal")
    d_scr_t1 = nc.dram_tensor("scr_t1", [128, 8 * 2048], BF16, kind="Internal")

    dbg = {}
    for name, shape, dt in debug:
        dbg[name] = nc.dram_tensor("dbg_" + name, shape, dt, kind="ExternalOutput")

    ctx = ExitStack()
    sb = lambda name, shape, dt: ctx.enter_context(nc.sbuf_tensor(name, shape, dt))

    c = {}
    for k in CON:
        if k in ("tri", "onehot"):
            continue
        shp = list(CON[k].shape)
        c[k] = sb("sc_" + k, shp, BF16 if CON[k].dtype == ml_dtypes.bfloat16 else F32)
    t_gamma = sb("t_gamma", [128, 8], F32)
    t_beta = sb("t_beta", [128, 8], F32)
    t_qks = sb("t_qks", [128, 1], F32)
    t_sel = sb("t_sel", [128, 2], F32)
    t_rb = sb("t_rb", [8, 1], F32)
    t_eps = sb("t_eps", [128, 1], F32)
    nc.const_aps.register(F32, LN_EPS, t_eps[:, 0:1])
    t_betak = sb("t_betak", [128, 8], F32)
    t_betaq = sb("t_betaq", [128, 8], F32)
    t_betav = sb("t_betav", [1, 1024], BF16)

    for (dst, src) in ([(c[k], d_con[k]) for k in CON if k not in ("tri", "onehot")] +
                       [(t_gamma, d_gamma), (t_beta, d_beta), (t_qks, d_qks),
                        (t_sel, d_sel), (t_rb, d_rb)]):
        p.sp_dma(lambda e, dst=dst, src=src: e.dma_start(out=dst[:, :], in_=src.ap()[:, :]),
                 guard=False)
    p.set_floor("sp", p.now("sp"))
    p.dve(lambda e: e.memset(t_eps[:, 0:1], LN_EPS), guard=False)
    p.set_floor("dve", p.now("dve"))

    def prep_wpart(stk, cols_off, dest, do_rope, beta_cols, beta_row, tag):
        with ExitStack() as ps:
            asbp = lambda n, shp, dt: ps.enter_context(nc.sbuf_tensor(n + "_" + tag, shp, dt))
            wld = asbp("wld", [128, 1024], F32)
            wg = asbp("wg", [128, 1024], F32)
            prm = asbp("prm", [128, 1024], F32)
            m1 = asbp("m1", [128, 1024], F32)
            brow = asbp("brow", [1, 1024], F32)
            brr = asbp("brr", [1, 1024], F32)
            bp1 = ps.enter_context(nc.psum_tensor("bp1_" + tag, [1, 512], F32))
            bp2 = ps.enter_context(nc.psum_tensor("bp2_" + tag, [1, 512], F32))
            btp = ps.enter_context(nc.psum_tensor("btp_" + tag, [128, 8], F32))
            for dt in range(8):
                tok_ld = p.sp_dma(lambda e, dt=dt: e.dma_start(
                    out=wld[:, :],
                    in_=d_wqkv.ap()[dt * 128:(dt + 1) * 128, cols_off:cols_off + 1024]))
                for half in range(2):
                    p.pe(lambda e, dt=dt, half=half: e.matmul(
                        (bp1 if half == 0 else bp2)[:, :], t_beta[:, dt:dt + 1],
                        wld[:, half * 512:(half + 1) * 512],
                        start=(dt == 0), stop=(dt == 7)), waits=[("sp", tok_ld)])
                p.dve(lambda e, dt=dt: e.tensor_scalar(
                    wg[:, :], wld[:, :], t_gamma[:, dt:dt + 1], None, ALU.mult),
                    waits=[("sp", tok_ld)])
                if do_rope:
                    p.dve(lambda e: e.tensor_copy(prm[:, 0:512], bass.AP(wg, 1, [[1024, 128], [2, 512]])))
                    p.dve(lambda e: e.tensor_copy(prm[:, 512:1024], bass.AP(wg, 0, [[1024, 128], [2, 512]])))
                    p.dve(lambda e: e.tensor_tensor(m1[:, :], wg[:, :], c["cosB"][:, :], ALU.mult))
                    p.dve(lambda e: e.tensor_tensor(prm[:, :], prm[:, :], c["s1B"][:, :], ALU.mult))
                    p.dve(lambda e, dt=dt: e.tensor_tensor(
                        dest[:, dt * 1024:(dt + 1) * 1024], m1[:, :], prm[:, :], ALU.add))
                else:
                    p.dve(lambda e, dt=dt: e.tensor_copy(dest[:, dt * 1024:(dt + 1) * 1024], wg[:, :]))
            p.dve(lambda e: e.tensor_copy(brow[0:1, 0:512], bp1[:, :]))
            tok = p.dve(lambda e: e.tensor_copy(brow[0:1, 512:1024], bp2[:, :]))
            if do_rope:
                p.dve(lambda e: e.tensor_copy(brr[0:1, 0:512], bass.AP(brow, 1, [[1024, 1], [2, 512]])))
                p.dve(lambda e: e.tensor_copy(brr[0:1, 512:1024], bass.AP(brow, 0, [[1024, 1], [2, 512]])))
                p.dve(lambda e: e.tensor_tensor(wg[0:1, :], brow[0:1, :], c["cosB"][0:1, :], ALU.mult))
                p.dve(lambda e: e.tensor_tensor(wld[0:1, :], brr[0:1, :], c["s1B"][0:1, :], ALU.mult))
                tok = p.dve(lambda e: e.tensor_tensor(brow[0:1, :], wg[0:1, :], wld[0:1, :], ALU.add))
            if beta_row is not None:
                p.dve(lambda e: e.tensor_copy(beta_row[0:1, :], brow[0:1, :]))
            if beta_cols is not None:
                for fc in range(8):
                    p.pe(lambda e, fc=fc: e.transpose(
                        btp[:, fc:fc + 1], brow[0:1, fc * 128:(fc + 1) * 128], c["idf32"][0:1, 0:1]),
                        waits=[("dve", tok)])
                p.dve(lambda e: e.tensor_copy(beta_cols[:, :], btp[:, :]))

    knctx = ExitStack()
    kn = knctx.enter_context(nc.sbuf_tensor("kn", [128, 8 * 4096], BF16))
    vaug = knctx.enter_context(nc.sbuf_tensor("vaug", [128, 32 * 1040], BF16))

    with ExitStack() as actx:
        asb = lambda n, shp, dt: actx.enter_context(nc.sbuf_tensor(n, shp, dt))
        wmodkv = asb("wmodkv", [128, 8 * 2048], BF16)
        with ExitStack() as ps:
            wld = ps.enter_context(nc.sbuf_tensor("wld_kv", [128, 1024], F32))
            wg = ps.enter_context(nc.sbuf_tensor("wg_kv", [128, 1024], F32))
            prm = ps.enter_context(nc.sbuf_tensor("prm_kv", [128, 1024], F32))
            m1 = ps.enter_context(nc.sbuf_tensor("m1_kv", [128, 1024], F32))
            bk1 = ps.enter_context(nc.psum_tensor("bk1", [1, 512], F32))
            bk2 = ps.enter_context(nc.psum_tensor("bk2", [1, 512], F32))
            bv1 = ps.enter_context(nc.psum_tensor("bv1", [1, 512], F32))
            bv2 = ps.enter_context(nc.psum_tensor("bv2", [1, 512], F32))
            btp = ps.enter_context(nc.psum_tensor("btp_kv", [128, 8], F32))
            brow = m1[0:1, 0:1024]
            brr = prm[0:1, 0:1024]
            for dt in range(8):
                tok_ld = p.sp_dma(lambda e, dt=dt: e.dma_start(
                    out=wld[:, :], in_=d_wqkv.ap()[dt * 128:(dt + 1) * 128, 1024:2048]))
                for half in range(2):
                    p.pe(lambda e, dt=dt, half=half: e.matmul(
                        (bk1 if half == 0 else bk2)[:, :], t_beta[:, dt:dt + 1],
                        wld[:, half * 512:(half + 1) * 512],
                        start=(dt == 0), stop=(dt == 7)), waits=[("sp", tok_ld)])
                p.dve(lambda e, dt=dt: e.tensor_scalar(
                    wg[:, :], wld[:, :], t_gamma[:, dt:dt + 1], None, ALU.mult),
                    waits=[("sp", tok_ld)])
                p.dve(lambda e: e.tensor_copy(prm[:, 0:512], bass.AP(wg, 1, [[1024, 128], [2, 512]])))
                p.dve(lambda e: e.tensor_copy(prm[:, 512:1024], bass.AP(wg, 0, [[1024, 128], [2, 512]])))
                p.dve(lambda e: e.tensor_tensor(m1[:, :], wg[:, :], c["cosB"][:, :], ALU.mult))
                p.dve(lambda e: e.tensor_tensor(prm[:, :], prm[:, :], c["s1B"][:, :], ALU.mult))
                p.dve(lambda e, dt=dt: e.tensor_tensor(
                    wmodkv[:, dt * 2048:dt * 2048 + 1024], m1[:, :], prm[:, :], ALU.add))
                tok_ld = p.sp_dma(lambda e, dt=dt: e.dma_start(
                    out=wld[:, :], in_=d_wqkv.ap()[dt * 128:(dt + 1) * 128, 2048:3072]))
                for half in range(2):
                    p.pe(lambda e, dt=dt, half=half: e.matmul(
                        (bv1 if half == 0 else bv2)[:, :], t_beta[:, dt:dt + 1],
                        wld[:, half * 512:(half + 1) * 512],
                        start=(dt == 0), stop=(dt == 7)), waits=[("sp", tok_ld)])
                p.dve(lambda e, dt=dt: e.tensor_scalar(
                    wmodkv[:, dt * 2048 + 1024:(dt + 1) * 2048], wld[:, :],
                    t_gamma[:, dt:dt + 1], None, ALU.mult), waits=[("sp", tok_ld)])
            p.dve(lambda e: e.tensor_copy(brr[0:1, :], bk1[:, :]) if False else e.tensor_copy(brow[0:1, 0:512], bk1[:, :]))
            p.dve(lambda e: e.tensor_copy(brow[0:1, 512:1024], bk2[:, :]))
            p.dve(lambda e: e.tensor_copy(brr[0:1, 0:512], bass.AP(m1, 1, [[1024, 1], [2, 512]])))
            p.dve(lambda e: e.tensor_copy(brr[0:1, 512:1024], bass.AP(m1, 0, [[1024, 1], [2, 512]])))
            p.dve(lambda e: e.tensor_tensor(wg[0:1, :], brow[0:1, :], c["cosB"][0:1, :], ALU.mult))
            p.dve(lambda e: e.tensor_tensor(wld[0:1, :], brr[0:1, :], c["s1B"][0:1, :], ALU.mult))
            tok = p.dve(lambda e: e.tensor_tensor(brow[0:1, :], wg[0:1, :], wld[0:1, :], ALU.add))
            for fc in range(8):
                p.pe(lambda e, fc=fc: e.transpose(
                    btp[:, fc:fc + 1], brow[0:1, fc * 128:(fc + 1) * 128], c["idf32"][0:1, 0:1]),
                    waits=[("dve", tok)])
            p.dve(lambda e: e.tensor_copy(t_betak[:, :], btp[:, :]))
            p.dve(lambda e: e.tensor_copy(brow[0:1, 0:512], bv1[:, :]))
            p.dve(lambda e: e.tensor_copy(brow[0:1, 512:1024], bv2[:, :]))
            p.dve(lambda e: e.tensor_copy(t_betav[0:1, :], brow[0:1, :]))

        p.dve(lambda e: e.memset(
            bass.AP(vaug, 64, [[32 * 1040, 128], [1040, 32], [65, 16], [1, 1]]), 1.0))

        xld = asb("xld", [128, 1024], F32)
        xsq = asb("xsq", [128, 512], F32)
        scol = asb("scol", [128, 12], F32)
        tnt = asb("tnt", [128, 1024], BF16)
        tnf = asb("tnf", [128, 8 * 512], BF16)
        ksb = asb("ksb", [128, 512], F32)
        k2 = asb("k2", [128, 512], BF16)
        nrow = asb("nrow", [2, 512], F32)
        nrow2 = asb("nrow2", [2, 512], F32)
        nrb = asb("nrb", [2, 512], BF16)
        tp_ps = actx.enter_context(nc.psum_tensor("tp_ps", [128, 128], BF16))
        k_ps = actx.enter_context(nc.psum_tensor("k_ps", [128, 512], F32))
        v_ps = actx.enter_context(nc.psum_tensor("v_ps", [128, 512], F32))
        n2_ps = actx.enter_context(nc.psum_tensor("n2_ps", [2, 512], F32))
        rn_ps = actx.enter_context(nc.psum_tensor("rn_ps", [128, 512], F32))

        def ln_stats_cols(tok_mu, tok_s2):
            p.dve(lambda e: e.tensor_scalar(scol[:, 2:3], scol[:, 0:1], 1.0 / 1024, None, ALU.mult),
                  waits=[("dve", tok_mu)])
            p.dve(lambda e: e.tensor_scalar(scol[:, 3:4], scol[:, 7:8], 1.0 / 1024, None, ALU.mult),
                  waits=[("dve", tok_s2)])
            p.dve(lambda e: e.tensor_tensor(scol[:, 4:5], scol[:, 2:3], scol[:, 2:3], ALU.mult))
            p.dve(lambda e: e.tensor_tensor(scol[:, 5:6], scol[:, 3:4], scol[:, 4:5], ALU.subtract))
            tk = p.act(lambda e: e.activation(scol[:, 9:10], scol[:, 5:6], AF.Sqrt, bias=LN_EPS))
            p.dve(lambda e: e.reciprocal(scol[:, 8:9], scol[:, 9:10]), waits=[("act", tk)])
            p.dve(lambda e: e.drain(), guard=False)
            p.dve(lambda e: e.tensor_tensor(scol[:, 9:10], scol[:, 5:6], scol[:, 8:9], ALU.mult))
            p.dve(lambda e: e.tensor_tensor(scol[:, 10:11], scol[:, 9:10], scol[:, 8:9], ALU.mult))
            tk = p.act(lambda e: e.activation(scol[:, 9:10], scol[:, 10:11], AF.Sqrt, bias=LN_EPS))
            p.dve(lambda e: e.reciprocal(scol[:, 10:11], scol[:, 9:10]), waits=[("act", tk)])
            p.dve(lambda e: e.drain(), guard=False)
            p.dve(lambda e: e.tensor_tensor(scol[:, 11:12], scol[:, 10:11], scol[:, 8:9], ALU.mult))

        for tc in range(8):
            for i in range(4):
                r0 = tc * 512 + i * 128
                tok_x = p.sp_dma(lambda e, r0=r0: e.dma_start(
                    out=xld[:, :], in_=d_x.ap()[r0:r0 + 128, :]))
                tok_mu = p.dve(lambda e: e.tensor_reduce(
                    scol[:, 0:1], xld[:, :], AX.X, ALU.add), waits=[("sp", tok_x)])
                tok_sq = p.act(lambda e: e.activation(xsq[:, :], xld[:, 0:512], AF.Square),
                               waits=[("sp", tok_x)])
                p.dve(lambda e: e.tensor_reduce(
                    scol[:, 1:2], xsq[:, :], AX.X, ALU.add), waits=[("act", tok_sq)])
                tok_sq = p.act(lambda e: e.activation(xsq[:, :], xld[:, 512:1024], AF.Square))
                p.dve(lambda e: e.tensor_reduce(
                    scol[:, 6:7], xsq[:, :], AX.X, ALU.add), waits=[("act", tok_sq)])
                p.dve(lambda e: e.drain(), guard=False)
                tok_s2 = p.dve(lambda e: e.tensor_tensor(
                    scol[:, 7:8], scol[:, 1:2], scol[:, 6:7], ALU.add))
                ln_stats_cols(tok_mu, tok_s2)
                if tc == 0 and i == 0 and "scol0" in dbg:
                    p.gp_dma(lambda e: e.dma_start(out=dbg["scol0"].ap()[:, :], in_=scol[:, :]))
                tok_tn = p.dve(lambda e: e.tensor_scalar(
                    tnt[:, :], xld[:, :], scol[:, 2:3], scol[:, 11:12], ALU.subtract, ALU.mult))
                for dt in range(8):
                    tok_tp = p.pe(lambda e, dt=dt: e.transpose(
                        tp_ps[:, :], tnt[:, dt * 128:(dt + 1) * 128], c["idbf"][:, :]),
                        waits=[("dve", tok_tn)])
                    p.dve(lambda e, dt=dt, i=i: e.tensor_copy(
                        tnf[:, dt * 512 + i * 128:dt * 512 + (i + 1) * 128], tp_ps[:, :]),
                        waits=[("pe", tok_tp), ("gp", p.now("gp"))])
            tok_tnf = p.now("dve")
            p.gp_dma(lambda e, tc=tc: e.dma_start(
                out=d_scr_tn.ap()[:, tc * 4096:(tc + 1) * 4096], in_=tnf[:, :]),
                waits=[("dve", tok_tnf)])
            for fc in range(8):
                tok_k = None
                for dt in range(8):
                    tok_k = p.pe(lambda e, fc=fc, dt=dt: e.matmul(
                        k_ps[:, :], wmodkv[:, dt * 2048 + fc * 128:dt * 2048 + (fc + 1) * 128],
                        tnf[:, dt * 512:(dt + 1) * 512],
                        start=(dt == 0), stop=(dt == 7)), waits=[("dve", tok_tnf)])
                tok_ksb = p.act(lambda e, fc=fc: e.add(ksb[:, :], k_ps[:, :], t_betak[:, fc:fc + 1]),
                    waits=[("pe", tok_k)])
                tok_k2 = p.act(lambda e: e.activation(k2[:, :], ksb[:, :], AF.Square))
                tok_n2 = p.pe(lambda e: e.matmul(n2_ps[:, :], c["indh"][:, :], k2[:, :],
                                                 start=True, stop=True), waits=[("act", tok_k2)])
                tok_nr = p.act(lambda e: e.activation(nrow[:, :], n2_ps[:, :], AF.Sqrt),
                               waits=[("pe", tok_n2)])
                p.dve(lambda e: e.tensor_scalar(nrow2[:, :], nrow[:, :], NORM_EPS, None, ALU.max),
                      waits=[("act", tok_nr)])
                p.dve(lambda e: e.reciprocal(nrow[:, :], nrow2[:, :]))
                p.dve(lambda e: e.drain(), guard=False)
                tok_nb = p.dve(lambda e: e.tensor_copy(nrb[:, :], nrow[:, :]))
                tok_rb = p.pe(lambda e: e.matmul(rn_ps[:, :], c["ind2"][:, :], nrb[:, :],
                                                 start=True, stop=True), waits=[("dve", tok_nb)])
                p.dve(lambda e, fc=fc, tc=tc: e.tensor_tensor(
                    kn[:, fc * 4096 + tc * 512:fc * 4096 + (tc + 1) * 512],
                    ksb[:, :], rn_ps[:, :], ALU.mult), waits=[("act", tok_ksb), ("pe", tok_rb)])
            for i in range(4):
                s_slot = tc * 4 + i
                for nh in range(2):
                    for dt in range(8):
                        p.pe(lambda e, i=i, nh=nh, dt=dt: e.matmul(
                            v_ps[:, :], tnf[:, dt * 512 + i * 128:dt * 512 + (i + 1) * 128],
                            wmodkv[:, dt * 2048 + 1024 + nh * 512:dt * 2048 + 1024 + (nh + 1) * 512],
                            start=(dt == 0), stop=False), waits=[("dve", tok_tnf)])
                    tok_v = p.pe(lambda e, nh=nh: e.matmul(
                        v_ps[:, :], c["ones1x128"][:, :], t_betav[0:1, nh * 512:(nh + 1) * 512],
                        start=False, stop=True))
                    p.dve(lambda e, s_slot=s_slot, nh=nh: e.tensor_copy(
                        bass.AP(vaug, s_slot * 1040 + nh * 8 * 65, [[32 * 1040, 128], [65, 8], [1, 64]]),
                        bass.AP(v_ps, 0, [[512, 128], [64, 8], [1, 64]])),
                        waits=[("pe", tok_v)])
            if tc == 0 and "vaug_early" in dbg:
                p.gp_dma(lambda e: e.dma_start(out=dbg["vaug_early"].ap()[:, :],
                                               in_=vaug[:, 0:4 * 1040]))
            if tc == 0 and "wv" in dbg:
                p.gp_dma(lambda e: e.dma_start(out=dbg["wv"].ap()[:, :],
                                               in_=wmodkv[:, 0:2048]))

    if "kn" in dbg:
        p.gp_dma(lambda e: e.dma_start(out=dbg["kn"].ap()[:, :], in_=kn[:, :]))
    for nm, tns in (("xld", xld), ("scol", scol), ("tnt", tnt), ("tnf", tnf)):
        if nm in dbg:
            p.gp_dma(lambda e, nm=nm, tns=tns: e.dma_start(out=dbg[nm].ap()[:, :], in_=tns[:, :]))
    if "vaug" in dbg:
        p.gp_dma(lambda e: e.dma_start(out=dbg["vaug"].ap()[:, 0:16640], in_=vaug[:, 0:16640]))
        p.gp_dma(lambda e: e.dma_start(out=dbg["vaug"].ap()[:, 16640:33280], in_=vaug[:, 16640:33280]))
    if "tnorm" in dbg:
        p.gp_dma(lambda e: e.dma_start(out=dbg["tnorm"].ap()[:, :], in_=d_scr_tn.ap()[:, :]))

    if phases < 2:
        bctx = None
    if phases >= 2:
      with ExitStack() as bctx:
        bsb = lambda n, shp, dt: bctx.enter_context(nc.sbuf_tensor(n, shp, dt))
        wmodq = bsb("wmodq", [128, 8 * 1024], BF16)
        c["tri"] = bsb("sc_tri", list(CON["tri"].shape), BF16)
        p.sp_dma(lambda e: e.dma_start(out=c["tri"][:, :], in_=d_con["tri"].ap()[:, :]))
        p.set_floor("sp", p.now("sp"))
        prep_wpart(bctx, 0, wmodq, True, t_betaq, None, "q")

        qn = bsb("qn", [128, 8 * 1024], BF16)
        tnr = bsb("tnr", [128, 8 * 512], BF16)
        qsb = bsb("qsb", [128, 512], F32)
        q2t = bsb("q2t", [128, 512], BF16)
        nrowq = bsb("nrowq", [2, 512], F32)
        nrowq2 = bsb("nrowq2", [2, 512], F32)
        nrbq = bsb("nrbq", [2, 512], BF16)
        p_sb = bsb("p_sb", [128, 4 * 512], BF16)
        p_m = bsb("p_m", [128, 2 * 512], BF16)

        bc_sb = bsb("bc_sb", [128, 512], F32)
        on_sb = bsb("on_sb", [128, 2 * 512], BF16)
        oo_tmp = bsb("oo_tmp", [128, 512], BF16)
        oo_t = bsb("oo_t", [128, 512], BF16)

        q_ps = bctx.enter_context(nc.psum_tensor("q_ps", [128, 512], F32))
        n2q_ps = bctx.enter_context(nc.psum_tensor("n2q_ps", [2, 512], F32))
        rnq_ps = bctx.enter_context(nc.psum_tensor("rnq_ps", [128, 512], F32))
        sc_ps = [bctx.enter_context(nc.psum_tensor(f"sc_ps{j}", [128, 512], F32)) for j in range(2)]
        o_ps = [bctx.enter_context(nc.psum_tensor(f"o_ps{j}", [65, 512], F32)) for j in range(2)]
        bcb_ps = bctx.enter_context(nc.psum_tensor("bcb_ps", [128, 512], F32))

        if "knl0" in dbg:
            p.gp_dma(lambda e: e.dma_start(out=dbg["knl0"].ap()[:, :], in_=kn[:, 3584:4096]))
        for pi, (glo, ghi) in enumerate(PAIRS):
            if pi == 1 and "knl1" in dbg:
                p.gp_dma(lambda e: e.dma_start(out=dbg["knl1"].ap()[:, :], in_=kn[:, 3584:4096]))
            if pi == 2 and "knl2" in dbg:
                p.gp_dma(lambda e: e.dma_start(out=dbg["knl2"].ap()[:, :], in_=kn[:, 3584:4096]))
            for cl, gch in enumerate((glo, ghi)):
                tok_tnr = p.sp_dma(lambda e, gch=gch: e.dma_start(
                    out=tnr[:, :], in_=d_scr_tn.ap()[:, gch * 4096:(gch + 1) * 4096]))
                for fc in range(8):
                    tok_q = None
                    for dt in range(8):
                        tok_q = p.pe(lambda e, fc=fc, dt=dt: e.matmul(
                            q_ps[:, :], wmodq[:, dt * 1024 + fc * 128:dt * 1024 + (fc + 1) * 128],
                            tnr[:, dt * 512:(dt + 1) * 512],
                            start=(dt == 0), stop=(dt == 7)), waits=[("sp", tok_tnr)])
                    tok_qsb = p.act(lambda e, fc=fc: e.add(qsb[:, :], q_ps[:, :], t_betaq[:, fc:fc + 1]),
                        waits=[("pe", tok_q)])
                    tok_q2 = p.act(lambda e: e.activation(q2t[:, :], qsb[:, :], AF.Square))
                    tok_n2 = p.pe(lambda e: e.matmul(n2q_ps[:, :], c["indh"][:, :], q2t[:, :],
                                                     start=True, stop=True), waits=[("act", tok_q2)])
                    tok_nr = p.act(lambda e: e.activation(nrowq[:, :], n2q_ps[:, :], AF.Sqrt),
                                   waits=[("pe", tok_n2)])
                    p.dve(lambda e: e.tensor_scalar(nrowq2[:, :], nrowq[:, :], NORM_EPS, None, ALU.max),
                          waits=[("act", tok_nr)])
                    p.dve(lambda e: e.reciprocal(nrowq[:, :], nrowq2[:, :]))
                    p.dve(lambda e: e.drain(), guard=False)
                    p.dve(lambda e: e.tensor_scalar(nrowq2[:, :], nrowq[:, :], t_qks[0:2, 0:1],
                                                    None, ALU.mult))
                    tok_nb = p.dve(lambda e: e.tensor_copy(nrbq[:, :], nrowq2[:, :]))
                    tok_rb = p.pe(lambda e: e.matmul(rnq_ps[:, :], c["ind2"][:, :], nrbq[:, :],
                                                     start=True, stop=True), waits=[("dve", tok_nb)])
                    p.dve(lambda e, fc=fc, cl=cl: e.tensor_tensor(
                        qn[:, fc * 1024 + cl * 512:fc * 1024 + (cl + 1) * 512],
                        qsb[:, :], rnq_ps[:, :], ALU.mult), waits=[("act", tok_qsb), ("pe", tok_rb)])
            for t in range(8):
                for cl, gch in enumerate((glo, ghi)):
                    T = 4 * (gch + 1)
                    tok_av = None
                    for s in range(T):
                        buf = s % 2
                        p.pe(lambda e, t=t, s=s, cl=cl: e.matmul(
                            sc_ps[0][:, :],
                            kn[0:64, t * 4096 + s * 128:t * 4096 + (s + 1) * 128],
                            qn[0:64, t * 1024 + cl * 512:t * 1024 + (cl + 1) * 512],
                            start=True, stop=True))
                        tok_s1 = p.pe(lambda e, t=t, s=s, cl=cl: e.matmul(
                            sc_ps[1][:, :],
                            kn[64:128, t * 4096 + s * 128:t * 4096 + (s + 1) * 128],
                            qn[64:128, t * 1024 + cl * 512:t * 1024 + (cl + 1) * 512],
                            start=True, stop=True))
                        tok_e0 = p.act(lambda e, buf=buf: e.activation(
                            p_sb[:, buf * 512:(buf + 1) * 512], sc_ps[0][:, :], AF.Exp),
                            waits=[("pe", tok_s1)])
                        tok_e1 = p.act(lambda e, buf=buf: e.activation(
                            p_sb[:, (2 + buf) * 512:(3 + buf) * 512], sc_ps[1][:, :], AF.Exp))
                        if s >= T - 4:
                            mi = s - (T - 4)
                            tok_m0 = p.dve(lambda e, mi=mi, buf=buf: e.tensor_tensor(
                                p_m[:, 0:512], p_sb[:, buf * 512:(buf + 1) * 512],
                                c["tri"][:, mi * 512:(mi + 1) * 512], ALU.mult),
                                waits=[("act", tok_e0)])
                            tok_m1 = p.dve(lambda e, mi=mi, buf=buf: e.tensor_tensor(
                                p_m[:, 512:1024], p_sb[:, (2 + buf) * 512:(3 + buf) * 512],
                                c["tri"][:, mi * 512:(mi + 1) * 512], ALU.mult),
                                waits=[("act", tok_e1)])
                            u0 = p_m[:, 0:512]
                            u1 = p_m[:, 512:1024]
                            wv0 = [("dve", tok_m0)]
                            wv1 = [("dve", tok_m1)]
                        else:
                            u0 = p_sb[:, buf * 512:(buf + 1) * 512]
                            u1 = p_sb[:, (2 + buf) * 512:(3 + buf) * 512]
                            wv0 = [("act", tok_e0)]
                            wv1 = [("act", tok_e1)]
                        p.pe(lambda e, t=t, s=s, u=u0, T=T: e.matmul(
                            o_ps[0][:, :],
                            vaug[:, s * 1040 + (2 * t) * 65:s * 1040 + (2 * t) * 65 + 65],
                            u, start=(s == 0), stop=(s == T - 1)), waits=wv0)
                        tok_av = p.pe(lambda e, t=t, s=s, u=u1, T=T: e.matmul(
                            o_ps[1][:, :],
                            vaug[:, s * 1040 + (2 * t + 1) * 65:s * 1040 + (2 * t + 1) * 65 + 65],
                            u, start=(s == 0), stop=(s == T - 1)), waits=wv1)
                    rb0 = tnr[0:1, 0:512]
                    rb1 = tnr[0:1, 512:1024]
                    p.dve(lambda e: e.reciprocal(qsb[0:1, 0:512], o_ps[0][64:65, :]),
                          waits=[("pe", tok_av)])
                    p.dve(lambda e: e.drain(), guard=False)
                    p.dve(lambda e: e.tensor_copy(rb0, qsb[0:1, 0:512]))
                    p.dve(lambda e: e.reciprocal(qsb[0:1, 0:512], o_ps[1][64:65, :]))
                    p.dve(lambda e: e.drain(), guard=False)
                    tok_rcb = p.dve(lambda e: e.tensor_copy(rb1, qsb[0:1, 0:512]))
                    p.pe(lambda e: e.matmul(bcb_ps[:, :], c["ind_lo"][:, :], rb0,
                                            start=True, stop=False), waits=[("dve", tok_rcb)])
                    tok_bc = p.pe(lambda e: e.matmul(bcb_ps[:, :], c["ind_hi"][:, :], rb1,
                                                     start=False, stop=True))
                    p.dve(lambda e: e.tensor_copy(bc_sb[:, :], bcb_ps[:, :]), waits=[("pe", tok_bc)])
                    p.dve(lambda e, cl=cl: e.tensor_tensor(
                        on_sb[0:64, cl * 512:(cl + 1) * 512], o_ps[0][0:64, :],
                        bc_sb[0:64, :], ALU.mult))
                    p.dve(lambda e, cl=cl: e.tensor_tensor(
                        on_sb[64:128, cl * 512:(cl + 1) * 512], o_ps[1][0:64, :],
                        bc_sb[64:128, :], ALU.mult))
                p.dve(lambda e: e.tensor_scalar(
                    oo_tmp[:, :], on_sb[:, 512:1024], t_sel[:, 1:2], None, ALU.mult))
                tok_sel = p.dve(lambda e: e.scalar_tensor_tensor(
                    oo_t[:, :], on_sb[:, 0:512], t_sel[:, 0:1], oo_tmp[:, :], ALU.mult, ALU.add))
                p.gp_dma(lambda e, t=t, pi=pi: e.dma_start(
                    out=d_scr_oo.ap()[:, t * 2048 + pi * 512:t * 2048 + (pi + 1) * 512],
                    in_=oo_t[:, :]), waits=[("dve", tok_sel)])

    if phases >= 2 and "oown" in dbg:
        p.gp_dma(lambda e: e.dma_start(out=dbg["oown"].ap()[:, :], in_=d_scr_oo.ap()[:, :]))

    knctx.close()

    x1 = sb("x1", [128, 8 * 2048], F32)
    c["onehot"] = sb("sc_onehot", list(CON["onehot"].shape), BF16)
    t_wrhl = sb("t_wrhl", [128, 2 * 64], BF16)
    t_b1a = sb("t_b1a", [128, NEXP * 32], F32)
    t_b1g = sb("t_b1g", [128, NEXP * 32], F32)
    t_b2b = sb("t_b2b", [8, E_MODEL], BF16)
    t_mrows = sb("t_mrows", [8, 2048], BF16)
    for (dst2, src2) in [(c["onehot"], d_con["onehot"]), (t_b1a, d_b1a), (t_b1g, d_b1g)]:
        p.sp_dma(lambda e, dst2=dst2, src2=src2: e.dma_start(out=dst2[:, :], in_=src2.ap()[:, :]))
    p.set_floor("sp", p.now("sp"))
    if phases >= 3:
      with ExitStack() as cctx:
        csb = lambda n, shp, dt: cctx.enter_context(nc.sbuf_tensor(n, shp, dt))
        wmodo = csb("wmodo", [128, 8 * 1024], BF16)
        wldo = csb("wldo", [128, 1024], F32)
        for dt in range(8):
            tok_ld = p.sp_dma(lambda e, dt=dt: e.dma_start(
                out=wldo[:, :], in_=d_wout.ap()[dt * 128:(dt + 1) * 128, :]))
            p.dve(lambda e, dt=dt: e.tensor_copy(
                wmodo[:, dt * 1024:(dt + 1) * 1024], wldo[:, :]), waits=[("sp", tok_ld)])
        wr_f = csb("wr_f", [128, 64], F32)
        wr_t = csb("wr_t", [128, 64], F32)
        tok_wr = p.sp_dma(lambda e: e.dma_start(out=wr_f[:, :], in_=d_rw.ap()[:, :]))
        p.dve(lambda e: e.tensor_copy(t_wrhl[:, 0:64], wr_f[:, :]), waits=[("sp", tok_wr)])
        p.dve(lambda e: e.tensor_tensor(wr_t[:, :], wr_f[:, :], t_wrhl[:, 0:64], ALU.subtract))
        p.dve(lambda e: e.tensor_copy(t_wrhl[:, 64:128], wr_t[:, :]))
        b2f = csb("b2f", [8, 1024], F32)
        tok_b2 = p.sp_dma(lambda e: e.dma_start(out=b2f[:, :], in_=d_b2.ap()[:, :]))
        p.dve(lambda e: e.tensor_copy(t_b2b[:, :], b2f[:, :]), waits=[("sp", tok_b2)])

        oo_rd = csb("oo_rd", [128, 8 * 512], BF16)
        xo = csb("xo", [128, 4 * 1024], F32)
        xt_sb = csb("xt_sb", [128, 512], F32)
        x1b = csb("x1b", [128, 512], BF16)
        x1sq = csb("x1sq", [128, 512], BF16)
        rowst = csb("rowst", [1, 8 * 512], F32)
        rwb = csb("rwb", [1, 4 * 512], BF16)
        t1f = csb("t1f", [128, 512], F32)
        t1t = csb("t1t", [128, 512], BF16)
        t1lo = csb("t1lo", [128, 8 * 512], BF16)
        tmpf = csb("tmpf", [128, 512], F32)
        lg_sb = csb("lg_sb", [8, 512], F32)
        mxc = csb("mxc", [128, 8], F32)
        mk_sb = csb("mk_sb", [128, 8], F32)
        ltp_s = csb("ltp_s", [128, 8], F32)

        x1_ps = cctx.enter_context(nc.psum_tensor("x1_ps", [128, 512], F32))
        xt_ps = cctx.enter_context(nc.psum_tensor("xt_ps", [128, 512], F32))
        sts_ps = cctx.enter_context(nc.psum_tensor("sts_ps", [1, 512], F32))
        stq_ps = cctx.enter_context(nc.psum_tensor("stq_ps", [1, 512], F32))
        mub_ps = cctx.enter_context(nc.psum_tensor("mub_ps", [128, 512], F32))
        rb_ps = cctx.enter_context(nc.psum_tensor("rb_ps", [128, 512], F32))
        rt_ps = cctx.enter_context(nc.psum_tensor("rt_ps", [8, 512], F32))
        lt_ps = xt_ps
        mr_ps = xt_ps

        row = lambda j: rowst[0:1, j * 512:(j + 1) * 512]
        for tc in range(4):
            tok_oo = None
            for t in range(8):
                tok_oo = p.sp_dma(lambda e, t=t, tc=tc: e.dma_start(
                    out=oo_rd[:, t * 512:(t + 1) * 512],
                    in_=d_scr_oo.ap()[:, t * 2048 + tc * 512:t * 2048 + (tc + 1) * 512]))
            tok_xo = None
            for i in range(4):
                r0 = tc * 512 + i * 128
                tok_xo = p.sp_dma(lambda e, i=i, r0=r0: e.dma_start(
                    out=xo[:, i * 1024:(i + 1) * 1024], in_=d_xown.ap()[r0:r0 + 128, :]))
            for dc in range(8):
                for i in range(4):
                    tok_tp = p.pe(lambda e, i=i, dc=dc: e.matmul(
                        xt_ps[:, i * 128:(i + 1) * 128],
                        xo[:, i * 1024 + dc * 128:i * 1024 + (dc + 1) * 128],
                        c["idf32"][:, :], start=True, stop=True, is_transpose=True),
                        waits=[("sp", tok_xo)])
                tok_xts = p.dve(lambda e: e.tensor_copy(xt_sb[:, :], xt_ps[:, :]),
                                waits=[("pe", tok_tp)])
                tok_mm = None
                for t in range(8):
                    tok_mm = p.pe(lambda e, t=t, dc=dc: e.matmul(
                        x1_ps[:, :], wmodo[:, t * 1024 + dc * 128:t * 1024 + (dc + 1) * 128],
                        oo_rd[:, t * 512:(t + 1) * 512], start=(t == 0), stop=(t == 7)),
                        waits=[("sp", tok_oo)])
                tok_x1 = p.dve(lambda e, dc=dc, tc=tc: e.tensor_tensor(
                    x1[:, dc * 2048 + tc * 512:dc * 2048 + (tc + 1) * 512],
                    x1_ps[:, :], xt_sb[:, :], ALU.add),
                    waits=[("pe", tok_mm), ("dve", tok_xts)])
                tok_x1b = p.act(lambda e, dc=dc, tc=tc: e.activation(
                    x1b[:, :], x1[:, dc * 2048 + tc * 512:dc * 2048 + (tc + 1) * 512], AF.Copy),
                    waits=[("dve", tok_x1)])
                tok_x1q = p.act(lambda e, dc=dc, tc=tc: e.activation(
                    x1sq[:, :], x1[:, dc * 2048 + tc * 512:dc * 2048 + (tc + 1) * 512], AF.Square))
                p.pe(lambda e, dc=dc: e.matmul(sts_ps[:, :], c["onesd"][:, :], x1b[:, :],
                                               start=(dc == 0), stop=(dc == 7)),
                     waits=[("act", tok_x1b)])
                tok_st = p.pe(lambda e, dc=dc: e.matmul(stq_ps[:, :], c["onesd"][:, :], x1sq[:, :],
                                                        start=(dc == 0), stop=(dc == 7)),
                              waits=[("act", tok_x1q)])
            p.dve(lambda e: e.tensor_scalar(row(0), sts_ps[:, :], 1.0 / 1024, None, ALU.mult),
                  waits=[("pe", tok_st)])
            p.dve(lambda e: e.tensor_scalar(row(1), stq_ps[:, :], 1.0 / 1024, None, ALU.mult))
            p.dve(lambda e: e.tensor_tensor(row(2), row(0), row(0), ALU.mult))
            p.dve(lambda e: e.tensor_tensor(row(3), row(1), row(2), ALU.subtract))
            tk = p.act(lambda e: e.activation(row(4), row(3), AF.Sqrt, bias=LN_EPS))
            p.dve(lambda e: e.reciprocal(row(5), row(4)), waits=[("act", tk)])
            p.dve(lambda e: e.drain(), guard=False)
            p.dve(lambda e: e.tensor_copy(rwb[0:1, 0:512], row(0)))
            p.dve(lambda e: e.tensor_copy(row(6), rwb[0:1, 0:512]))
            p.dve(lambda e: e.tensor_tensor(row(7), row(0), row(6), ALU.subtract))
            p.dve(lambda e: e.tensor_copy(rwb[0:1, 512:1024], row(7)))
            p.dve(lambda e: e.tensor_copy(rwb[0:1, 1024:1536], row(5)))
            p.dve(lambda e: e.tensor_copy(row(6), rwb[0:1, 1024:1536]))
            p.dve(lambda e: e.tensor_tensor(row(7), row(5), row(6), ALU.subtract))
            tok_rw = p.dve(lambda e: e.tensor_copy(rwb[0:1, 1536:2048], row(7)))
            p.pe(lambda e: e.matmul(mub_ps[:, :], c["ones1x128"][:, :], rwb[0:1, 0:512],
                                    start=True, stop=False), waits=[("dve", tok_rw)])
            tok_mub = p.pe(lambda e: e.matmul(mub_ps[:, :], c["ones1x128"][:, :],
                                              rwb[0:1, 512:1024], start=False, stop=True))
            p.pe(lambda e: e.matmul(rb_ps[:, :], c["ones1x128"][:, :], rwb[0:1, 1024:1536],
                                    start=True, stop=False))
            tok_rbb = p.pe(lambda e: e.matmul(rb_ps[:, :], c["ones1x128"][:, :],
                                              rwb[0:1, 1536:2048], start=False, stop=True))
            for dc in range(8):
                p.dve(lambda e, dc=dc, tc=tc: e.tensor_tensor(
                    tmpf[:, :], x1[:, dc * 2048 + tc * 512:dc * 2048 + (tc + 1) * 512],
                    mub_ps[:, :], ALU.subtract), waits=[("pe", tok_mub), ("pe", tok_rbb)])
                p.dve(lambda e: e.tensor_tensor(t1f[:, :], tmpf[:, :], rb_ps[:, :], ALU.mult))
                tok_t1 = p.dve(lambda e: e.tensor_copy(t1t[:, :], t1f[:, :]))
                tok_t1l = p.dve(lambda e, dc=dc: e.tensor_tensor(
                    t1lo[:, dc * 512:(dc + 1) * 512], t1f[:, :], t1t[:, :], ALU.subtract))
                p.gp_dma(lambda e, dc=dc, tc=tc: e.dma_start(
                    out=d_scr_t1.ap()[:, dc * 2048 + tc * 512:dc * 2048 + (tc + 1) * 512],
                    in_=t1t[:, :]), waits=[("dve", tok_t1)])
                p.pe(lambda e, dc=dc: e.matmul(
                    rt_ps[:, :], t_wrhl[:, dc * 8:(dc + 1) * 8], t1t[:, :],
                    start=(dc == 0), stop=False), waits=[("dve", tok_t1)])
                p.pe(lambda e, dc=dc: e.matmul(
                    rt_ps[:, :], t_wrhl[:, 64 + dc * 8:64 + (dc + 1) * 8], t1t[:, :],
                    start=False, stop=False))
                tok_rt = p.pe(lambda e, dc=dc: e.matmul(
                    rt_ps[:, :], t_wrhl[:, dc * 8:(dc + 1) * 8], t1lo[:, dc * 512:(dc + 1) * 512],
                    start=False, stop=(dc == 7)), waits=[("dve", tok_t1l)])
            tok_lg = p.act(lambda e: e.add(lg_sb[:, :], rt_ps[:, :], t_rb[0:8, 0:1]), waits=[("pe", tok_rt)])
            for j in range(4):
                tok_lt = p.pe(lambda e, j=j: e.transpose(
                    lt_ps[:, 0:8], lg_sb[:, j * 128:(j + 1) * 128], c["idf32"][0:8, 0:8]),
                    waits=[("act", tok_lg)])
                p.dve(lambda e: e.tensor_copy(ltp_s[:, :], lt_ps[:, 0:8]), waits=[("pe", tok_lt)])
                p.dve(lambda e: e.drain(), guard=False)
                p.dve(lambda e: e.tensor_reduce(mxc[:, 0:1], ltp_s[:, :], AX.X, ALU.max))
                p.dve(lambda e: e.drain(), guard=False)
                p.dve(lambda e: e.memset(mxc[:, 1:2], 0.0))
                p.dve(lambda e: e.drain(), guard=False)
                for ee in range(8):
                    tcur, tnxt = (1, 5) if ee % 2 == 0 else (5, 1)
                    p.dve(lambda e, ee=ee: e.tensor_tensor(
                        mxc[:, 2:3], ltp_s[:, ee:ee + 1], mxc[:, 0:1], ALU.is_ge))
                    p.dve(lambda e: e.drain(), guard=False)
                    p.dve(lambda e, tcur=tcur: e.tensor_scalar(
                        mxc[:, 3:4], mxc[:, tcur:tcur + 1], -1.0, 1.0, ALU.mult, ALU.add))
                    p.dve(lambda e: e.drain(), guard=False)
                    p.dve(lambda e: e.tensor_tensor(mxc[:, 4:5], mxc[:, 2:3], mxc[:, 3:4], ALU.mult))
                    p.dve(lambda e: e.drain(), guard=False)
                    p.dve(lambda e, tcur=tcur, tnxt=tnxt: e.tensor_tensor(
                        mxc[:, tnxt:tnxt + 1], mxc[:, tcur:tcur + 1], mxc[:, 4:5], ALU.add))
                    p.dve(lambda e, ee=ee: e.tensor_copy(mk_sb[:, ee:ee + 1], mxc[:, 4:5]))
                    p.dve(lambda e: e.drain(), guard=False)
                tok_mk = p.now("dve")
                tok_mt = p.pe(lambda e: e.transpose(mr_ps[0:8, 128:256], mk_sb[:, :], c["idf32"][:, :]),
                              waits=[("dve", tok_mk)])
                p.dve(lambda e, tc=tc, j=j: e.tensor_copy(
                    t_mrows[:, tc * 512 + j * 128:tc * 512 + (j + 1) * 128], mr_ps[0:8, 128:256]),
                    waits=[("pe", tok_mt)])

    if phases >= 3 and "x1" in dbg:
        p.gp_dma(lambda e: e.dma_start(out=dbg["x1"].ap()[:, :], in_=x1[:, :]))
    if "mrows" in dbg:
        p.gp_dma(lambda e: e.dma_start(out=dbg["mrows"].ap()[:, :], in_=t_mrows[:, :]))
    if "t1" in dbg:
        p.gp_dma(lambda e: e.dma_start(out=dbg["t1"].ap()[:, :], in_=d_scr_t1.ap()[:, :]))

    if phases >= 4:
      with ExitStack() as dctx:
        dsb = lambda n, shp, dt: dctx.enter_context(nc.sbuf_tensor(n, shp, dt))
        u_sb = dsb("u_sb", [128, 32 * 1024], BF16)
        t1rd = dsb("t1rd", [128, 8 * 1024], BF16)
        w1ld = dsb("w1ld", [128, 8 * 256], F32)
        apb = [dsb(f"apb{j}", [128, 8 * 256], BF16) for j in range(2)]
        gpb = [dsb(f"gpb{j}", [128, 8 * 256], BF16) for j in range(2)]
        w2c = [dsb(f"w2c{j}", [128, 512], F32) for j in range(2)]
        w2b = [dsb(f"w2b{j}", [128, 512], BF16) for j in range(2)]
        sg_t = [dsb(f"sg_t{j}", [128, 512], F32) for j in range(2)]
        aa_t = [dsb(f"aa_t{j}", [128, 512], F32) for j in range(2)]
        mb_sb = dsb("mb_sb", [128, 2 * 512], BF16)
        yt = sg_t[0]

        tk_conv = {0: None, 1: None}
        tk_mm_piece = {0: None, 1: None}
        tk_w2mm = {0: None, 1: None}
        tk_w2ld = {0: None, 1: None}
        tk_w2cv = {0: None, 1: None}
        tk_ag_evac = {0: None, 1: None}
        tk_umul = {0: None, 1: None}
        tk_ap_dma = None
        tk_t1r = None
        tk_x1add = None

        W1B = E_MODEL * 2 * HIDDEN
        for e_i in range(NEXP):
            if e_i == 1 and "x1e0" in dbg:
                p.gp_dma(lambda e: e.dma_start(out=dbg["x1e0"].ap()[:, :], in_=x1[:, :]),
                         waits=[("dve", p.now("dve"))])
            with ExitStack() as uctx:
                ag_ps = [uctx.enter_context(nc.psum_tensor(f"ag_ps{j}_{e_i}", [128, 512], F32))
                         for j in range(4)]
                for fh in range(2):
                    tk_t1r = p.sp_dma(lambda e, fh=fh: e.dma_start(
                        out=t1rd[:, :],
                        in_=bass.AP(d_scr_t1, fh * 1024, [[16384, 128], [2048, 8], [1, 1024]])),
                        waits=[("pe", tk_mm_piece[0]), ("pe", tk_mm_piece[1]),
                               ("pe", p.now("pe") if (e_i == 0 and fh == 0) else None)],
                        guard=(e_i == 0 and fh == 0), sem="t1s")
                    gidx = 0
                    for p4 in range(16):
                        j = p4 % 2
                        tk_ap_dma = p.sp_dma(lambda e, e_i=e_i, p4=p4: e.dma_start(
                            out=w1ld[:, :],
                            in_=bass.AP(d_w1, e_i * W1B + p4 * 256,
                                        [[2 * HIDDEN, 128], [128 * 2 * HIDDEN, 8], [1, 256]])),
                            waits=[("dve", tk_conv[1 - j])], guard=False, sem="w1s")
                        tok_ca = p.dve(lambda e, j=j: e.tensor_copy(apb[j][:, :], w1ld[:, :]),
                                       waits=[("w1s", tk_ap_dma), ("pe", tk_mm_piece[j])],
                                       guard=False)
                        tk_ap_dma = p.sp_dma(lambda e, e_i=e_i, p4=p4: e.dma_start(
                            out=w1ld[:, :],
                            in_=bass.AP(d_w1, e_i * W1B + HIDDEN + p4 * 256,
                                        [[2 * HIDDEN, 128], [128 * 2 * HIDDEN, 8], [1, 256]])),
                            waits=[("dve", tok_ca)], guard=False, sem="w1s")
                        tk_conv[j] = p.dve(lambda e, j=j: e.tensor_copy(gpb[j][:, :], w1ld[:, :]),
                                           waits=[("w1s", tk_ap_dma)], guard=False)
                        for hc2 in range(2):
                            hc = p4 * 2 + hc2
                            for qq in range(2):
                                bj = gidx % 2
                                gidx += 1
                                for dt in range(8):
                                    p.pe(lambda e, j=j, dt=dt, hc2=hc2, qq=qq, bj=bj: e.matmul(
                                        ag_ps[bj][:, :],
                                        apb[j][:, dt * 256 + hc2 * 128:dt * 256 + (hc2 + 1) * 128],
                                        t1rd[:, dt * 1024 + qq * 512:dt * 1024 + (qq + 1) * 512],
                                        start=(dt == 0), stop=(dt == 7)),
                                        waits=([("dve", tk_conv[j]), ("t1s", tk_t1r),
                                                ("act", tk_ag_evac[bj])] if dt == 0 else []),
                                        guard=False)
                                    tk_mm_piece[j] = p.pe(lambda e, j=j, dt=dt, hc2=hc2, qq=qq, bj=bj: e.matmul(
                                        ag_ps[2 + bj][:, :],
                                        gpb[j][:, dt * 256 + hc2 * 128:dt * 256 + (hc2 + 1) * 128],
                                        t1rd[:, dt * 1024 + qq * 512:dt * 1024 + (qq + 1) * 512],
                                        start=(dt == 0), stop=(dt == 7)), guard=False)
                                tok_sg = p.act(lambda e, hc=hc, e_i=e_i, bj=bj: e.activation(
                                    sg_t[bj][:, :], ag_ps[2 + bj][:, :], AF.Sigmoid,
                                    bias=t_b1g[:, e_i * 32 + hc:e_i * 32 + hc + 1]),
                                    waits=[("pe", tk_mm_piece[j]), ("dve", tk_umul[bj]),
                                           ("dve", tk_x1add)],
                                    guard=False)
                                tk_ag_evac[bj] = p.act(lambda e, hc=hc, e_i=e_i, bj=bj: e.add(
                                    aa_t[bj][:, :], ag_ps[bj][:, :],
                                    t_b1a[:, e_i * 32 + hc:e_i * 32 + hc + 1]), guard=False)
                                tk_umul[bj] = p.dve(lambda e, hc=hc, qq=qq, bj=bj: e.tensor_tensor(
                                    u_sb[:, hc * 1024 + qq * 512:hc * 1024 + (qq + 1) * 512],
                                    aa_t[bj][:, :], sg_t[bj][:, :], ALU.mult),
                                    waits=[("act", tk_ag_evac[bj])], guard=False)
                    tok_u = p.now("dve")
                    if e_i == 7 and fh == 1 and "u0" in dbg:
                        p.gp_dma(lambda e: e.dma_start(out=dbg["u0"].ap()[:, 0:16384],
                                                       in_=u_sb[:, 0:16384]), waits=[("dve", tok_u)])
                        p.gp_dma(lambda e: e.dma_start(out=dbg["u0"].ap()[:, 16384:32768],
                                                       in_=u_sb[:, 16384:32768]), waits=[("dve", tok_u)])
                    tk_mb = []
                    for qq in range(2):
                        tcq = fh * 1024 + qq * 512
                        t_mbm = p.pe(lambda e, e_i=e_i, tcq=tcq: e.matmul(
                            ag_ps[0][:, :], c["onehot"][:, e_i * 128:(e_i + 1) * 128],
                            t_mrows[0:8, tcq:tcq + 512], start=True, stop=True),
                            waits=[("act", tk_ag_evac[0]), ("dve", tk_umul[0])] +
                                  ([("dve", tk_mb[0])] if qq == 1 else []), guard=False)
                        tk_mb.append(p.dve(lambda e, qq=qq: e.tensor_copy(
                            mb_sb[:, qq * 512:(qq + 1) * 512], ag_ps[0][:, :]),
                            waits=[("pe", t_mbm)], guard=False))
                    with ExitStack() as wctx:
                        y_ps = [wctx.enter_context(nc.psum_tensor(f"y_ps{j}_{e_i}_{fh}", [128, 512], F32))
                                for j in range(4)]
                        for qq in range(2):
                            tcq = fh * 1024 + qq * 512
                            for dh in range(2):
                                tok_ystop = {}
                                for hc in range(32):
                                    wj = hc % 2
                                    wsem = "w2a" if wj == 0 else "w2b"
                                    tk_w2ld[wj] = p.sp_dma(lambda e, e_i=e_i, hc=hc, dh=dh, wj=wj: e.dma_start(
                                        out=w2c[wj][:, :],
                                        in_=d_w2.ap()[e_i, hc * 128:(hc + 1) * 128,
                                                      dh * 512:(dh + 1) * 512]),
                                        waits=[("dve", tk_w2cv[wj])], guard=False, sem=wsem)
                                    tk_w2cv[wj] = p.dve(lambda e, wj=wj: e.tensor_copy(
                                        w2b[wj][:, :], w2c[wj][:, :]),
                                        waits=[(wsem, tk_w2ld[wj]), ("pe", tk_w2mm[wj])], guard=False)
                                    for dc4 in range(4):
                                        dc = dh * 4 + dc4
                                        tk_w2mm[wj] = p.pe(lambda e, hc=hc, qq=qq, dc4=dc4, wj=wj: e.matmul(
                                            y_ps[dc4][:, :], w2b[wj][:, dc4 * 128:(dc4 + 1) * 128],
                                            u_sb[:, hc * 1024 + qq * 512:hc * 1024 + (qq + 1) * 512],
                                            start=(hc == 0), stop=False),
                                            waits=([("dve", tk_w2cv[wj])] if dc4 == 0 else []) +
                                                  ([("dve", tok_u), ("dve", tk_x1add)] if hc == 0 else []),
                                            guard=False)
                                for dc4 in range(4):
                                    dc = dh * 4 + dc4
                                    tok_ystop[dc4] = p.pe(lambda e, dc=dc, dc4=dc4, tcq=tcq: e.matmul(
                                        y_ps[dc4][:, :], t_b2b[:, dc * 128:(dc + 1) * 128],
                                        t_mrows[0:8, tcq:tcq + 512], start=False, stop=True),
                                        guard=False)
                                for dc4 in range(4):
                                    dc = dh * 4 + dc4
                                    p.dve(lambda e, dc4=dc4, qq=qq: e.tensor_tensor(
                                        yt[:, :], y_ps[dc4][:, :], mb_sb[:, qq * 512:(qq + 1) * 512],
                                        ALU.mult),
                                        waits=[("pe", tok_ystop[dc4]), ("dve", tk_mb[qq])], guard=False)
                                    tk_x1add = p.dve(lambda e, dc=dc, tcq=tcq: e.tensor_tensor(
                                        x1[:, dc * 2048 + tcq:dc * 2048 + tcq + 512],
                                        x1[:, dc * 2048 + tcq:dc * 2048 + tcq + 512], yt[:, :],
                                        ALU.add), guard=False)

    if phases >= 4 and "mrend" in dbg:
        p.gp_dma(lambda e: e.dma_start(out=dbg["mrend"].ap()[:, :], in_=t_mrows[:, :]))

    if phases >= 5:
      with ExitStack() as ectx:
        osb = ectx.enter_context(nc.sbuf_tensor("osb", [128, 1024], F32))
        o_ps2 = ectx.enter_context(nc.psum_tensor("o_ps2", [128, 1024], F32))
        for tt in range(16):
            for dc in range(8):
                p.pe(lambda e, tt=tt, dc=dc: e.matmul(
                    o_ps2[:, dc * 128:(dc + 1) * 128],
                    x1[:, dc * 2048 + tt * 128:dc * 2048 + (tt + 1) * 128],
                    c["idf32"][:, :], start=True, stop=True, is_transpose=True))
            tok_cp = p.act(lambda e: e.activation(osb[:, :], o_ps2[:, :], AF.Copy),
                           waits=[("pe", p.now("pe")), ("gp", p.now("gp"))])
            p.gp_dma(lambda e, tt=tt: e.dma_start(
                out=d_y.ap()[tt * 128:(tt + 1) * 128, :], in_=osb[:, :]),
                waits=[("act", tok_cp)])

    p.gp(lambda e: e.nop(), waits=[("gp", p.now("gp"))], guard=False, inc=0)

    with ExitStack() as sctx:
        sems = {e: sctx.enter_context(nc.semaphore(f"sem_{e}")) for e in
                list(ENGINES) + list(Prog.EXTRA_SEMS)}
        with nc.Block() as block:
            p.replay(block, sems)
    ctx.close()
    return nc


def own_chunks(pp):
    return [PAIRS[pi][pp] for pi in range(4)]


def stage_inputs(inputs, core):
    b, pp = core // 2, core % 2
    x = np.asarray(inputs["L0_attn_x"], dtype=np.float32)
    own = own_chunks(pp)
    x_own = np.concatenate([x[b, g * 512:(g + 1) * 512] for g in own], axis=0)
    gamma = np.asarray(inputs["L0_attn_ln_gamma"], np.float32).reshape(8, 128).T.copy()
    beta = np.asarray(inputs["L0_attn_ln_beta"], np.float32).reshape(8, 128).T.copy()
    qks = np.broadcast_to(np.asarray(inputs["L0_attn_qkscale"], np.float32).reshape(1, 1),
                          (128, 1)).copy()
    sel = np.zeros((128, 2), np.float32)
    sel[:, pp] = 1.0
    b1 = np.asarray(inputs["L0_ffn_b1"], np.float32)
    b1a = np.concatenate([b1[e, :4096].reshape(32, 128).T for e in range(8)], axis=1)
    b1g = np.concatenate([b1[e, 4096:].reshape(32, 128).T for e in range(8)], axis=1)
    m = {
        "x_batch": np.ascontiguousarray(x[b]),
        "x_own": np.ascontiguousarray(x_own),
        "w_qkv": np.ascontiguousarray(np.asarray(inputs["L0_attn_w_qkv"], np.float32)),
        "w_out": np.ascontiguousarray(np.asarray(inputs["L0_attn_w_out"], np.float32)),
        "gamma": gamma, "beta": beta, "qkscale": qks, "sel": sel,
        "router_w": np.ascontiguousarray(
            np.asarray(inputs["L0_ffn_router_w"], np.float32).reshape(8, 128, 8)
            .transpose(1, 0, 2).reshape(128, 64)),
        "router_b": np.asarray(inputs["L0_ffn_router_b"], np.float32).reshape(8, 1).copy(),
        "w1": np.ascontiguousarray(np.asarray(inputs["L0_ffn_w1"], np.float32)),
        "b1a": np.ascontiguousarray(b1a), "b1g": np.ascontiguousarray(b1g),
        "w2": np.ascontiguousarray(np.asarray(inputs["L0_ffn_w2"], np.float32)),
        "b2": np.ascontiguousarray(np.asarray(inputs["L0_ffn_b2"], np.float32)),
    }
    for k, v in host_constants().items():
        m["c_" + k] = np.ascontiguousarray(v)
    return m


def kernel(**inputs):
    nc = build_program()
    in_maps = [stage_inputs(inputs, cc) for cc in range(8)]
    res = run_bass_kernel_spmd(nc, in_maps, core_ids=list(range(8)))
    out = np.zeros((4, 4096, 1024), dtype=np.float32)
    for cc in range(8):
        b, pp = cc // 2, cc % 2
        y = np.asarray(res.results[cc]["y"], np.float32)
        for pi, g in enumerate(own_chunks(pp)):
            out[b, g * 512:(g + 1) * 512] = y[pi * 512:(pi + 1) * 512]
    return out


# version 04 (final)
# speedup vs baseline: 2.8393x; 1.1512x over previous
import numpy as np
import ml_dtypes
from contextlib import ExitStack

import concourse.bass as bass
import concourse.mybir as mybir
from concourse.bass_utils import run_bass_kernel_spmd

F32 = mybir.dt.float32
BF16 = mybir.dt.bfloat16
AF = mybir.ActivationFunctionType
ALU = mybir.AluOpType
AX = mybir.AxisListType

E_MODEL = 1024
S = 4096
HIDDEN = 4096
NEXP = 8
LN_EPS = 1e-5
NORM_EPS = 1e-12
ROPE_BASE = 10000.0

ENGINES = ("pe", "act", "dve", "sp", "gp")
PAIRS = [(0, 7), (1, 6), (2, 5), (3, 4)]


def host_constants():
    e = E_MODEL
    inv = ROPE_BASE ** (-(np.arange(0, e, 2, dtype=np.float64) / e))
    emb = np.repeat(inv, 2)
    sin_v = np.sin(S * emb).astype(np.float32)
    cos_v = np.cos(S * emb).astype(np.float32)
    s1 = sin_v.copy()
    s1[: e // 2] = -s1[: e // 2]
    cosB = np.broadcast_to(cos_v, (128, e)).copy()
    s1B = np.broadcast_to(s1, (128, e)).copy()
    tri = np.zeros((128, 4 * 512), dtype=ml_dtypes.bfloat16)
    pp = np.arange(128)[:, None]
    ff = np.arange(512)[None, :]
    for i in range(4):
        tri[:, i * 512:(i + 1) * 512] = (ff >= pp + 128 * i).astype(ml_dtypes.bfloat16)
    idf32 = np.eye(128, dtype=np.float32)
    idbf = np.eye(128, dtype=ml_dtypes.bfloat16)
    ones1x128 = np.ones((1, 128), dtype=ml_dtypes.bfloat16)
    onesd = np.ones((128, 1), dtype=ml_dtypes.bfloat16)
    indh = np.zeros((128, 2), dtype=ml_dtypes.bfloat16)
    indh[0:64, 0] = 1
    indh[64:128, 1] = 1
    ind2 = np.zeros((2, 128), dtype=ml_dtypes.bfloat16)
    ind2[0, 0:64] = 1
    ind2[1, 64:128] = 1
    ind_lo = np.zeros((1, 128), dtype=ml_dtypes.bfloat16)
    ind_lo[0, 0:64] = 1
    ind_hi = np.zeros((1, 128), dtype=ml_dtypes.bfloat16)
    ind_hi[0, 64:128] = 1
    onehot = np.zeros((8, 8 * 128), dtype=ml_dtypes.bfloat16)
    for ee in range(8):
        onehot[ee, ee * 128:(ee + 1) * 128] = 1
    return dict(cosB=cosB, s1B=s1B, tri=tri, idf32=idf32, idbf=idbf,
                ones1x128=ones1x128, onesd=onesd, indh=indh, ind2=ind2,
                ind_lo=ind_lo, ind_hi=ind_hi, onehot=onehot)


class Prog:

    GUARDS = {
        "pe": ("act", "dve"),
        "act": ("pe", "dve"),
        "dve": ("pe", "act"),
        "sp": ("pe", "act", "dve", "gp"),
        "gp": ("pe", "act", "dve", "sp"),
    }

    EXTRA_SEMS = ("w2a", "w2b", "w1s", "t1s")

    def __init__(self):
        self.ops = []
        self.cnt = {e: 0 for e in ENGINES}
        for s in self.EXTRA_SEMS:
            self.cnt[s] = 0
        self.floors = {}

    def set_floor(self, eng, val):
        if val:
            self.floors[eng] = max(self.floors.get(eng, 0), val)

    def op(self, eng, fn, inc=0, waits=(), guard=True, sem=None):
        w = dict()
        for (fe, fv) in self.floors.items():
            if fe != eng and fv:
                w[fe] = fv
        if guard:
            for ge in self.GUARDS[eng]:
                if self.cnt[ge]:
                    w[ge] = self.cnt[ge]
        for (we, wv) in waits:
            if wv is None:
                continue
            assert wv <= self.cnt[we], f"wait on future {we}:{wv} > {self.cnt[we]}"
            w[we] = max(w.get(we, 0), wv)
        key = sem if sem is not None else eng
        self.ops.append((eng, fn, inc, sorted(w.items()), key))
        if inc:
            self.cnt[key] += inc
            return self.cnt[key]
        return None

    def pe(self, fn, waits=(), guard=True):
        return self.op("pe", fn, 1, waits, guard)

    def act(self, fn, waits=(), guard=True):
        return self.op("act", fn, 1, waits, guard)

    def dve(self, fn, waits=(), guard=True):
        return self.op("dve", fn, 1, waits, guard)

    def sp_dma(self, fn, waits=(), guard=True, sem=None):
        return self.op("sp", fn, 16, waits, guard, sem)

    def gp_dma(self, fn, waits=(), guard=True):
        return self.op("gp", fn, 16, waits, guard)

    def gp(self, fn, waits=(), guard=True, inc=1):
        return self.op("gp", fn, inc, waits, guard)

    def now(self, eng):
        return self.cnt[eng]

    def replay(self, block, sems):
        def make(engname):
            def emit(eng):
                seen = {e: 0 for e in self.cnt}
                for (e, fn, inc, waits, key) in self.ops:
                    if e != engname:
                        continue
                    for (we, wv) in waits:
                        if wv > seen[we]:
                            eng.wait_ge(sems[we], wv)
                            seen[we] = wv
                    inst = fn(eng)
                    if inc:
                        assert inst is not None
                        inst.then_inc(sems[key], inc)
            return emit
        block.tensor(make("pe"))
        block.scalar(make("act"))
        block.vector(make("dve"))
        block.sync(make("sp"))
        block.gpsimd(make("gp"))


def build_program(debug=(), phases=5):
    nc = bass.Bass("TRN2", target_bir_lowering=False)
    p = Prog()

    d_x = nc.dram_tensor("x_batch", [S, E_MODEL], F32, kind="ExternalInput")
    d_xown = nc.dram_tensor("x_own", [2048, E_MODEL], F32, kind="ExternalInput")
    d_wqkv = nc.dram_tensor("w_qkv", [E_MODEL, 3 * E_MODEL], F32, kind="ExternalInput")
    d_wout = nc.dram_tensor("w_out", [E_MODEL, E_MODEL], F32, kind="ExternalInput")
    d_gamma = nc.dram_tensor("gamma", [128, 8], F32, kind="ExternalInput")
    d_beta = nc.dram_tensor("beta", [128, 8], F32, kind="ExternalInput")
    d_qks = nc.dram_tensor("qkscale", [128, 1], F32, kind="ExternalInput")
    d_sel = nc.dram_tensor("sel", [128, 2], F32, kind="ExternalInput")
    d_rw = nc.dram_tensor("router_w", [128, 64], F32, kind="ExternalInput")
    d_rb = nc.dram_tensor("router_b", [8, 1], F32, kind="ExternalInput")
    d_w1 = nc.dram_tensor("w1", [NEXP, E_MODEL, 2 * HIDDEN], F32, kind="ExternalInput")
    d_b1a = nc.dram_tensor("b1a", [128, NEXP * 32], F32, kind="ExternalInput")
    d_b1g = nc.dram_tensor("b1g", [128, NEXP * 32], F32, kind="ExternalInput")
    d_w2 = nc.dram_tensor("w2", [NEXP, HIDDEN, E_MODEL], F32, kind="ExternalInput")
    d_b2 = nc.dram_tensor("b2", [8, E_MODEL], F32, kind="ExternalInput")
    CON = host_constants()
    d_con = {k: nc.dram_tensor("c_" + k, list(CON[k].shape),
                               BF16 if CON[k].dtype == ml_dtypes.bfloat16 else F32,
                               kind="ExternalInput") for k in CON}
    d_y = nc.dram_tensor("y", [2048, E_MODEL], F32, kind="ExternalOutput")
    d_scr_tn = nc.dram_tensor("scr_tn", [128, 8 * 4096], BF16, kind="Internal")
    d_scr_oo = nc.dram_tensor("scr_oo", [128, 8 * 2048], BF16, kind="Internal")
    d_scr_t1 = nc.dram_tensor("scr_t1", [128, 8 * 2048], BF16, kind="Internal")

    dbg = {}
    for name, shape, dt in debug:
        dbg[name] = nc.dram_tensor("dbg_" + name, shape, dt, kind="ExternalOutput")

    ctx = ExitStack()
    sb = lambda name, shape, dt: ctx.enter_context(nc.sbuf_tensor(name, shape, dt))

    c = {}
    for k in CON:
        if k in ("tri", "onehot"):
            continue
        shp = list(CON[k].shape)
        c[k] = sb("sc_" + k, shp, BF16 if CON[k].dtype == ml_dtypes.bfloat16 else F32)
    t_gamma = sb("t_gamma", [128, 8], F32)
    t_beta = sb("t_beta", [128, 8], F32)
    t_qks = sb("t_qks", [128, 1], F32)
    t_sel = sb("t_sel", [128, 2], F32)
    t_rb = sb("t_rb", [8, 1], F32)
    t_eps = sb("t_eps", [128, 1], F32)
    nc.const_aps.register(F32, LN_EPS, t_eps[:, 0:1])
    t_betak = sb("t_betak", [128, 8], F32)
    t_betaq = sb("t_betaq", [128, 8], F32)
    t_betav = sb("t_betav", [1, 1024], BF16)

    for (dst, src) in ([(c[k], d_con[k]) for k in CON if k not in ("tri", "onehot")] +
                       [(t_gamma, d_gamma), (t_beta, d_beta), (t_qks, d_qks),
                        (t_sel, d_sel), (t_rb, d_rb)]):
        p.sp_dma(lambda e, dst=dst, src=src: e.dma_start(out=dst[:, :], in_=src.ap()[:, :]),
                 guard=False)
    p.set_floor("sp", p.now("sp"))
    p.dve(lambda e: e.memset(t_eps[:, 0:1], LN_EPS), guard=False)
    p.set_floor("dve", p.now("dve"))

    def prep_wpart(stk, cols_off, dest, do_rope, beta_cols, beta_row, tag):
        with ExitStack() as ps:
            asbp = lambda n, shp, dt: ps.enter_context(nc.sbuf_tensor(n + "_" + tag, shp, dt))
            wld = asbp("wld", [128, 1024], F32)
            wg = asbp("wg", [128, 1024], F32)
            prm = asbp("prm", [128, 1024], F32)
            m1 = asbp("m1", [128, 1024], F32)
            brow = asbp("brow", [1, 1024], F32)
            brr = asbp("brr", [1, 1024], F32)
            bp1 = ps.enter_context(nc.psum_tensor("bp1_" + tag, [1, 512], F32))
            bp2 = ps.enter_context(nc.psum_tensor("bp2_" + tag, [1, 512], F32))
            btp = ps.enter_context(nc.psum_tensor("btp_" + tag, [128, 8], F32))
            for dt in range(8):
                tok_ld = p.sp_dma(lambda e, dt=dt: e.dma_start(
                    out=wld[:, :],
                    in_=d_wqkv.ap()[dt * 128:(dt + 1) * 128, cols_off:cols_off + 1024]))
                for half in range(2):
                    p.pe(lambda e, dt=dt, half=half: e.matmul(
                        (bp1 if half == 0 else bp2)[:, :], t_beta[:, dt:dt + 1],
                        wld[:, half * 512:(half + 1) * 512],
                        start=(dt == 0), stop=(dt == 7)), waits=[("sp", tok_ld)])
                p.dve(lambda e, dt=dt: e.tensor_scalar(
                    wg[:, :], wld[:, :], t_gamma[:, dt:dt + 1], None, ALU.mult),
                    waits=[("sp", tok_ld)])
                if do_rope:
                    p.dve(lambda e: e.tensor_copy(prm[:, 0:512], bass.AP(wg, 1, [[1024, 128], [2, 512]])))
                    p.dve(lambda e: e.tensor_copy(prm[:, 512:1024], bass.AP(wg, 0, [[1024, 128], [2, 512]])))
                    p.dve(lambda e: e.tensor_tensor(m1[:, :], wg[:, :], c["cosB"][:, :], ALU.mult))
                    p.dve(lambda e: e.tensor_tensor(prm[:, :], prm[:, :], c["s1B"][:, :], ALU.mult))
                    p.dve(lambda e, dt=dt: e.tensor_tensor(
                        dest[:, dt * 1024:(dt + 1) * 1024], m1[:, :], prm[:, :], ALU.add))
                else:
                    p.dve(lambda e, dt=dt: e.tensor_copy(dest[:, dt * 1024:(dt + 1) * 1024], wg[:, :]))
            p.dve(lambda e: e.tensor_copy(brow[0:1, 0:512], bp1[:, :]))
            tok = p.dve(lambda e: e.tensor_copy(brow[0:1, 512:1024], bp2[:, :]))
            if do_rope:
                p.dve(lambda e: e.tensor_copy(brr[0:1, 0:512], bass.AP(brow, 1, [[1024, 1], [2, 512]])))
                p.dve(lambda e: e.tensor_copy(brr[0:1, 512:1024], bass.AP(brow, 0, [[1024, 1], [2, 512]])))
                p.dve(lambda e: e.tensor_tensor(wg[0:1, :], brow[0:1, :], c["cosB"][0:1, :], ALU.mult))
                p.dve(lambda e: e.tensor_tensor(wld[0:1, :], brr[0:1, :], c["s1B"][0:1, :], ALU.mult))
                tok = p.dve(lambda e: e.tensor_tensor(brow[0:1, :], wg[0:1, :], wld[0:1, :], ALU.add))
            if beta_row is not None:
                p.dve(lambda e: e.tensor_copy(beta_row[0:1, :], brow[0:1, :]))
            if beta_cols is not None:
                for fc in range(8):
                    p.pe(lambda e, fc=fc: e.transpose(
                        btp[:, fc:fc + 1], brow[0:1, fc * 128:(fc + 1) * 128], c["idf32"][0:1, 0:1]),
                        waits=[("dve", tok)])
                p.dve(lambda e: e.tensor_copy(beta_cols[:, :], btp[:, :]))

    knctx = ExitStack()
    kn = knctx.enter_context(nc.sbuf_tensor("kn", [128, 8 * 4096], BF16))
    vaug = knctx.enter_context(nc.sbuf_tensor("vaug", [128, 32 * 1040], BF16))

    with ExitStack() as actx:
        asb = lambda n, shp, dt: actx.enter_context(nc.sbuf_tensor(n, shp, dt))
        wmodkv = asb("wmodkv", [128, 8 * 2048], BF16)
        with ExitStack() as ps:
            wld = ps.enter_context(nc.sbuf_tensor("wld_kv", [128, 1024], F32))
            wg = ps.enter_context(nc.sbuf_tensor("wg_kv", [128, 1024], F32))
            prm = ps.enter_context(nc.sbuf_tensor("prm_kv", [128, 1024], F32))
            m1 = ps.enter_context(nc.sbuf_tensor("m1_kv", [128, 1024], F32))
            bk1 = ps.enter_context(nc.psum_tensor("bk1", [1, 512], F32))
            bk2 = ps.enter_context(nc.psum_tensor("bk2", [1, 512], F32))
            bv1 = ps.enter_context(nc.psum_tensor("bv1", [1, 512], F32))
            bv2 = ps.enter_context(nc.psum_tensor("bv2", [1, 512], F32))
            btp = ps.enter_context(nc.psum_tensor("btp_kv", [128, 8], F32))
            brow = m1[0:1, 0:1024]
            brr = prm[0:1, 0:1024]
            for dt in range(8):
                tok_ld = p.sp_dma(lambda e, dt=dt: e.dma_start(
                    out=wld[:, :], in_=d_wqkv.ap()[dt * 128:(dt + 1) * 128, 1024:2048]))
                for half in range(2):
                    p.pe(lambda e, dt=dt, half=half: e.matmul(
                        (bk1 if half == 0 else bk2)[:, :], t_beta[:, dt:dt + 1],
                        wld[:, half * 512:(half + 1) * 512],
                        start=(dt == 0), stop=(dt == 7)), waits=[("sp", tok_ld)])
                p.dve(lambda e, dt=dt: e.tensor_scalar(
                    wg[:, :], wld[:, :], t_gamma[:, dt:dt + 1], None, ALU.mult),
                    waits=[("sp", tok_ld)])
                p.dve(lambda e: e.tensor_copy(prm[:, 0:512], bass.AP(wg, 1, [[1024, 128], [2, 512]])))
                p.dve(lambda e: e.tensor_copy(prm[:, 512:1024], bass.AP(wg, 0, [[1024, 128], [2, 512]])))
                p.dve(lambda e: e.tensor_tensor(m1[:, :], wg[:, :], c["cosB"][:, :], ALU.mult))
                p.dve(lambda e: e.tensor_tensor(prm[:, :], prm[:, :], c["s1B"][:, :], ALU.mult))
                p.dve(lambda e, dt=dt: e.tensor_tensor(
                    wmodkv[:, dt * 2048:dt * 2048 + 1024], m1[:, :], prm[:, :], ALU.add))
                tok_ld = p.sp_dma(lambda e, dt=dt: e.dma_start(
                    out=wld[:, :], in_=d_wqkv.ap()[dt * 128:(dt + 1) * 128, 2048:3072]))
                for half in range(2):
                    p.pe(lambda e, dt=dt, half=half: e.matmul(
                        (bv1 if half == 0 else bv2)[:, :], t_beta[:, dt:dt + 1],
                        wld[:, half * 512:(half + 1) * 512],
                        start=(dt == 0), stop=(dt == 7)), waits=[("sp", tok_ld)])
                p.dve(lambda e, dt=dt: e.tensor_scalar(
                    wmodkv[:, dt * 2048 + 1024:(dt + 1) * 2048], wld[:, :],
                    t_gamma[:, dt:dt + 1], None, ALU.mult), waits=[("sp", tok_ld)])
            p.dve(lambda e: e.tensor_copy(brr[0:1, :], bk1[:, :]) if False else e.tensor_copy(brow[0:1, 0:512], bk1[:, :]))
            p.dve(lambda e: e.tensor_copy(brow[0:1, 512:1024], bk2[:, :]))
            p.dve(lambda e: e.tensor_copy(brr[0:1, 0:512], bass.AP(m1, 1, [[1024, 1], [2, 512]])))
            p.dve(lambda e: e.tensor_copy(brr[0:1, 512:1024], bass.AP(m1, 0, [[1024, 1], [2, 512]])))
            p.dve(lambda e: e.tensor_tensor(wg[0:1, :], brow[0:1, :], c["cosB"][0:1, :], ALU.mult))
            p.dve(lambda e: e.tensor_tensor(wld[0:1, :], brr[0:1, :], c["s1B"][0:1, :], ALU.mult))
            tok = p.dve(lambda e: e.tensor_tensor(brow[0:1, :], wg[0:1, :], wld[0:1, :], ALU.add))
            for fc in range(8):
                p.pe(lambda e, fc=fc: e.transpose(
                    btp[:, fc:fc + 1], brow[0:1, fc * 128:(fc + 1) * 128], c["idf32"][0:1, 0:1]),
                    waits=[("dve", tok)])
            p.dve(lambda e: e.tensor_copy(t_betak[:, :], btp[:, :]))
            p.dve(lambda e: e.tensor_copy(brow[0:1, 0:512], bv1[:, :]))
            p.dve(lambda e: e.tensor_copy(brow[0:1, 512:1024], bv2[:, :]))
            p.dve(lambda e: e.tensor_copy(t_betav[0:1, :], brow[0:1, :]))

        p.dve(lambda e: e.memset(
            bass.AP(vaug, 64, [[32 * 1040, 128], [1040, 32], [65, 16], [1, 1]]), 1.0))

        xld = asb("xld", [128, 1024], F32)
        xsq = asb("xsq", [128, 512], F32)
        scol = asb("scol", [128, 12], F32)
        tnt = asb("tnt", [128, 1024], BF16)
        tnf = asb("tnf", [128, 8 * 512], BF16)
        ksb = asb("ksb", [128, 512], F32)
        k2 = asb("k2", [128, 512], BF16)
        nrow = asb("nrow", [2, 512], F32)
        nrow2 = asb("nrow2", [2, 512], F32)
        nrb = asb("nrb", [2, 512], BF16)
        tp_ps = actx.enter_context(nc.psum_tensor("tp_ps", [128, 128], BF16))
        k_ps = actx.enter_context(nc.psum_tensor("k_ps", [128, 512], F32))
        v_ps = actx.enter_context(nc.psum_tensor("v_ps", [128, 512], F32))
        n2_ps = actx.enter_context(nc.psum_tensor("n2_ps", [2, 512], F32))
        rn_ps = actx.enter_context(nc.psum_tensor("rn_ps", [128, 512], F32))

        def ln_stats_cols(tok_mu, tok_s2):
            p.dve(lambda e: e.tensor_scalar(scol[:, 2:3], scol[:, 0:1], 1.0 / 1024, None, ALU.mult),
                  waits=[("dve", tok_mu)])
            p.dve(lambda e: e.tensor_scalar(scol[:, 3:4], scol[:, 7:8], 1.0 / 1024, None, ALU.mult),
                  waits=[("dve", tok_s2)])
            p.dve(lambda e: e.tensor_tensor(scol[:, 4:5], scol[:, 2:3], scol[:, 2:3], ALU.mult))
            p.dve(lambda e: e.tensor_tensor(scol[:, 5:6], scol[:, 3:4], scol[:, 4:5], ALU.subtract))
            tk = p.act(lambda e: e.activation(scol[:, 9:10], scol[:, 5:6], AF.Sqrt, bias=LN_EPS))
            p.dve(lambda e: e.reciprocal(scol[:, 8:9], scol[:, 9:10]), waits=[("act", tk)])
            p.dve(lambda e: e.drain(), guard=False)
            p.dve(lambda e: e.tensor_tensor(scol[:, 9:10], scol[:, 5:6], scol[:, 8:9], ALU.mult))
            p.dve(lambda e: e.tensor_tensor(scol[:, 10:11], scol[:, 9:10], scol[:, 8:9], ALU.mult))
            tk = p.act(lambda e: e.activation(scol[:, 9:10], scol[:, 10:11], AF.Sqrt, bias=LN_EPS))
            p.dve(lambda e: e.reciprocal(scol[:, 10:11], scol[:, 9:10]), waits=[("act", tk)])
            p.dve(lambda e: e.drain(), guard=False)
            p.dve(lambda e: e.tensor_tensor(scol[:, 11:12], scol[:, 10:11], scol[:, 8:9], ALU.mult))

        for tc in range(8):
            for i in range(4):
                r0 = tc * 512 + i * 128
                tok_x = p.sp_dma(lambda e, r0=r0: e.dma_start(
                    out=xld[:, :], in_=d_x.ap()[r0:r0 + 128, :]))
                tok_mu = p.dve(lambda e: e.tensor_reduce(
                    scol[:, 0:1], xld[:, :], AX.X, ALU.add), waits=[("sp", tok_x)])
                tok_sq = p.act(lambda e: e.activation(xsq[:, :], xld[:, 0:512], AF.Square),
                               waits=[("sp", tok_x)])
                p.dve(lambda e: e.tensor_reduce(
                    scol[:, 1:2], xsq[:, :], AX.X, ALU.add), waits=[("act", tok_sq)])
                tok_sq = p.act(lambda e: e.activation(xsq[:, :], xld[:, 512:1024], AF.Square))
                p.dve(lambda e: e.tensor_reduce(
                    scol[:, 6:7], xsq[:, :], AX.X, ALU.add), waits=[("act", tok_sq)])
                p.dve(lambda e: e.drain(), guard=False)
                tok_s2 = p.dve(lambda e: e.tensor_tensor(
                    scol[:, 7:8], scol[:, 1:2], scol[:, 6:7], ALU.add))
                ln_stats_cols(tok_mu, tok_s2)
                if tc == 0 and i == 0 and "scol0" in dbg:
                    p.gp_dma(lambda e: e.dma_start(out=dbg["scol0"].ap()[:, :], in_=scol[:, :]))
                tok_tn = p.dve(lambda e: e.tensor_scalar(
                    tnt[:, :], xld[:, :], scol[:, 2:3], scol[:, 11:12], ALU.subtract, ALU.mult))
                for dt in range(8):
                    tok_tp = p.pe(lambda e, dt=dt: e.transpose(
                        tp_ps[:, :], tnt[:, dt * 128:(dt + 1) * 128], c["idbf"][:, :]),
                        waits=[("dve", tok_tn)])
                    p.dve(lambda e, dt=dt, i=i: e.tensor_copy(
                        tnf[:, dt * 512 + i * 128:dt * 512 + (i + 1) * 128], tp_ps[:, :]),
                        waits=[("pe", tok_tp), ("gp", p.now("gp"))])
            tok_tnf = p.now("dve")
            p.gp_dma(lambda e, tc=tc: e.dma_start(
                out=d_scr_tn.ap()[:, tc * 4096:(tc + 1) * 4096], in_=tnf[:, :]),
                waits=[("dve", tok_tnf)])
            for fc in range(8):
                tok_k = None
                for dt in range(8):
                    tok_k = p.pe(lambda e, fc=fc, dt=dt: e.matmul(
                        k_ps[:, :], wmodkv[:, dt * 2048 + fc * 128:dt * 2048 + (fc + 1) * 128],
                        tnf[:, dt * 512:(dt + 1) * 512],
                        start=(dt == 0), stop=(dt == 7)), waits=[("dve", tok_tnf)])
                tok_ksb = p.act(lambda e, fc=fc: e.add(ksb[:, :], k_ps[:, :], t_betak[:, fc:fc + 1]),
                    waits=[("pe", tok_k)])
                tok_k2 = p.act(lambda e: e.activation(k2[:, :], ksb[:, :], AF.Square))
                tok_n2 = p.pe(lambda e: e.matmul(n2_ps[:, :], c["indh"][:, :], k2[:, :],
                                                 start=True, stop=True), waits=[("act", tok_k2)])
                tok_nr = p.act(lambda e: e.activation(nrow[:, :], n2_ps[:, :], AF.Sqrt),
                               waits=[("pe", tok_n2)])
                p.dve(lambda e: e.tensor_scalar(nrow2[:, :], nrow[:, :], NORM_EPS, None, ALU.max),
                      waits=[("act", tok_nr)])
                p.dve(lambda e: e.reciprocal(nrow[:, :], nrow2[:, :]))
                p.dve(lambda e: e.drain(), guard=False)
                tok_nb = p.dve(lambda e: e.tensor_copy(nrb[:, :], nrow[:, :]))
                tok_rb = p.pe(lambda e: e.matmul(rn_ps[:, :], c["ind2"][:, :], nrb[:, :],
                                                 start=True, stop=True), waits=[("dve", tok_nb)])
                p.dve(lambda e, fc=fc, tc=tc: e.tensor_tensor(
                    kn[:, fc * 4096 + tc * 512:fc * 4096 + (tc + 1) * 512],
                    ksb[:, :], rn_ps[:, :], ALU.mult), waits=[("act", tok_ksb), ("pe", tok_rb)])
            for i in range(4):
                s_slot = tc * 4 + i
                for nh in range(2):
                    for dt in range(8):
                        p.pe(lambda e, i=i, nh=nh, dt=dt: e.matmul(
                            v_ps[:, :], tnf[:, dt * 512 + i * 128:dt * 512 + (i + 1) * 128],
                            wmodkv[:, dt * 2048 + 1024 + nh * 512:dt * 2048 + 1024 + (nh + 1) * 512],
                            start=(dt == 0), stop=False), waits=[("dve", tok_tnf)])
                    tok_v = p.pe(lambda e, nh=nh: e.matmul(
                        v_ps[:, :], c["ones1x128"][:, :], t_betav[0:1, nh * 512:(nh + 1) * 512],
                        start=False, stop=True))
                    p.dve(lambda e, s_slot=s_slot, nh=nh: e.tensor_copy(
                        bass.AP(vaug, s_slot * 1040 + nh * 8 * 65, [[32 * 1040, 128], [65, 8], [1, 64]]),
                        bass.AP(v_ps, 0, [[512, 128], [64, 8], [1, 64]])),
                        waits=[("pe", tok_v)])
            if tc == 0 and "vaug_early" in dbg:
                p.gp_dma(lambda e: e.dma_start(out=dbg["vaug_early"].ap()[:, :],
                                               in_=vaug[:, 0:4 * 1040]))
            if tc == 0 and "wv" in dbg:
                p.gp_dma(lambda e: e.dma_start(out=dbg["wv"].ap()[:, :],
                                               in_=wmodkv[:, 0:2048]))

    if "kn" in dbg:
        p.gp_dma(lambda e: e.dma_start(out=dbg["kn"].ap()[:, :], in_=kn[:, :]))
    for nm, tns in (("xld", xld), ("scol", scol), ("tnt", tnt), ("tnf", tnf)):
        if nm in dbg:
            p.gp_dma(lambda e, nm=nm, tns=tns: e.dma_start(out=dbg[nm].ap()[:, :], in_=tns[:, :]))
    if "vaug" in dbg:
        p.gp_dma(lambda e: e.dma_start(out=dbg["vaug"].ap()[:, 0:16640], in_=vaug[:, 0:16640]))
        p.gp_dma(lambda e: e.dma_start(out=dbg["vaug"].ap()[:, 16640:33280], in_=vaug[:, 16640:33280]))
    if "tnorm" in dbg:
        p.gp_dma(lambda e: e.dma_start(out=dbg["tnorm"].ap()[:, :], in_=d_scr_tn.ap()[:, :]))

    if phases < 2:
        bctx = None
    if phases >= 2:
      with ExitStack() as bctx:
        bsb = lambda n, shp, dt: bctx.enter_context(nc.sbuf_tensor(n, shp, dt))
        wmodq = bsb("wmodq", [128, 8 * 1024], BF16)
        c["tri"] = bsb("sc_tri", list(CON["tri"].shape), BF16)
        p.sp_dma(lambda e: e.dma_start(out=c["tri"][:, :], in_=d_con["tri"].ap()[:, :]))
        p.set_floor("sp", p.now("sp"))
        prep_wpart(bctx, 0, wmodq, True, t_betaq, None, "q")

        qn = bsb("qn", [128, 8 * 1024], BF16)
        tnr = bsb("tnr", [128, 8 * 512], BF16)
        qsb = bsb("qsb", [128, 512], F32)
        q2t = bsb("q2t", [128, 512], BF16)
        nrowq = bsb("nrowq", [2, 512], F32)
        nrowq2 = bsb("nrowq2", [2, 512], F32)
        nrbq = bsb("nrbq", [2, 512], BF16)
        p_sb = bsb("p_sb", [128, 4 * 512], BF16)
        p_m = bsb("p_m", [128, 2 * 512], BF16)

        bc_sb = bsb("bc_sb", [128, 512], F32)
        on_sb = bsb("on_sb", [128, 2 * 512], BF16)
        oo_tmp = bsb("oo_tmp", [128, 512], BF16)
        oo_t = bsb("oo_t", [128, 512], BF16)

        q_ps = bctx.enter_context(nc.psum_tensor("q_ps", [128, 512], F32))
        n2q_ps = bctx.enter_context(nc.psum_tensor("n2q_ps", [2, 512], F32))
        rnq_ps = bctx.enter_context(nc.psum_tensor("rnq_ps", [128, 512], F32))
        sc_x = [bctx.enter_context(nc.psum_tensor(f"sc_ps{j}", [128, 512], F32)) for j in range(3)]
        o_ps = [bctx.enter_context(nc.psum_tensor(f"o_ps{j}", [65, 512], F32)) for j in range(2)]
        sc_ps4 = [[sc_x[0], sc_x[1]], [sc_x[2], rnq_ps]]
        bcb_ps = q_ps

        if "knl0" in dbg:
            p.gp_dma(lambda e: e.dma_start(out=dbg["knl0"].ap()[:, :], in_=kn[:, 3584:4096]))
        for pi, (glo, ghi) in enumerate(PAIRS):
            if pi == 1 and "knl1" in dbg:
                p.gp_dma(lambda e: e.dma_start(out=dbg["knl1"].ap()[:, :], in_=kn[:, 3584:4096]))
            if pi == 2 and "knl2" in dbg:
                p.gp_dma(lambda e: e.dma_start(out=dbg["knl2"].ap()[:, :], in_=kn[:, 3584:4096]))
            for cl, gch in enumerate((glo, ghi)):
                tok_tnr = p.sp_dma(lambda e, gch=gch: e.dma_start(
                    out=tnr[:, :], in_=d_scr_tn.ap()[:, gch * 4096:(gch + 1) * 4096]))
                for fc in range(8):
                    tok_q = None
                    for dt in range(8):
                        tok_q = p.pe(lambda e, fc=fc, dt=dt: e.matmul(
                            q_ps[:, :], wmodq[:, dt * 1024 + fc * 128:dt * 1024 + (fc + 1) * 128],
                            tnr[:, dt * 512:(dt + 1) * 512],
                            start=(dt == 0), stop=(dt == 7)), waits=[("sp", tok_tnr)])
                    tok_qsb = p.act(lambda e, fc=fc: e.add(qsb[:, :], q_ps[:, :], t_betaq[:, fc:fc + 1]),
                        waits=[("pe", tok_q)])
                    tok_q2 = p.act(lambda e: e.activation(q2t[:, :], qsb[:, :], AF.Square))
                    tok_n2 = p.pe(lambda e: e.matmul(n2q_ps[:, :], c["indh"][:, :], q2t[:, :],
                                                     start=True, stop=True), waits=[("act", tok_q2)])
                    tok_nr = p.act(lambda e: e.activation(nrowq[:, :], n2q_ps[:, :], AF.Sqrt),
                                   waits=[("pe", tok_n2)])
                    p.dve(lambda e: e.tensor_scalar(nrowq2[:, :], nrowq[:, :], NORM_EPS, None, ALU.max),
                          waits=[("act", tok_nr)])
                    p.dve(lambda e: e.reciprocal(nrowq[:, :], nrowq2[:, :]))
                    p.dve(lambda e: e.drain(), guard=False)
                    p.dve(lambda e: e.tensor_scalar(nrowq2[:, :], nrowq[:, :], t_qks[0:2, 0:1],
                                                    None, ALU.mult))
                    tok_nb = p.dve(lambda e: e.tensor_copy(nrbq[:, :], nrowq2[:, :]))
                    tok_rb = p.pe(lambda e: e.matmul(rnq_ps[:, :], c["ind2"][:, :], nrbq[:, :],
                                                     start=True, stop=True), waits=[("dve", tok_nb)])
                    p.dve(lambda e, fc=fc, cl=cl: e.tensor_tensor(
                        qn[:, fc * 1024 + cl * 512:fc * 1024 + (cl + 1) * 512],
                        qsb[:, :], rnq_ps[:, :], ALU.mult), waits=[("act", tok_qsb), ("pe", tok_rb)])
            tok_qn = p.now("dve")
            for t in range(8):
                tk_sc = [[None, None], [None, None]]
                tk_exp = [[None, None], [None, None]]
                tk_avt = [[None, None], [None, None]]
                tk_pm = [None, None]
                tk_oev = [p.now("dve"), p.now("dve")]
                for cl, gch in enumerate((glo, ghi)):
                    T = 4 * (gch + 1)

                    def emit_scores(s, t=t, cl=cl):
                        buf = s % 2
                        for h in range(2):
                            tk_sc[buf][h] = p.pe(lambda e, t=t, s=s, cl=cl, h=h: e.matmul(
                                sc_ps4[buf][h][:, :],
                                kn[h * 64:(h + 1) * 64,
                                   t * 4096 + s * 128:t * 4096 + (s + 1) * 128],
                                qn[h * 64:(h + 1) * 64,
                                   t * 1024 + cl * 512:t * 1024 + (cl + 1) * 512],
                                start=True, stop=True),
                                waits=[("act", tk_exp[buf][h]),
                                       ("dve", tok_qn if s <= 1 else None)], guard=False)

                    emit_scores(0)
                    tok_av = None
                    for s in range(T):
                        buf = s % 2
                        if s + 1 < T:
                            emit_scores(s + 1)
                        use = [None, None]
                        wv = [None, None]
                        for h in range(2):
                            tk_exp[buf][h] = p.act(lambda e, buf=buf, h=h: e.activation(
                                p_sb[:, (2 * h + buf) * 512:(2 * h + buf + 1) * 512],
                                sc_ps4[buf][h][:, :], AF.Exp),
                                waits=[("pe", tk_sc[buf][h]), ("pe", tk_avt[buf][h])],
                                guard=False)
                            use[h] = p_sb[:, (2 * h + buf) * 512:(2 * h + buf + 1) * 512]
                            wv[h] = [("act", tk_exp[buf][h])]
                        if s >= T - 4:
                            mi = s - (T - 4)
                            for h in range(2):
                                tok_m = p.dve(lambda e, mi=mi, buf=buf, h=h: e.tensor_tensor(
                                    p_m[:, h * 512:(h + 1) * 512],
                                    p_sb[:, (2 * h + buf) * 512:(2 * h + buf + 1) * 512],
                                    c["tri"][:, mi * 512:(mi + 1) * 512], ALU.mult),
                                    waits=[("act", tk_exp[buf][h]), ("pe", tk_pm[h])],
                                    guard=False)
                                use[h] = p_m[:, h * 512:(h + 1) * 512]
                                wv[h] = [("dve", tok_m)]
                        for h in range(2):
                            tok_av = p.pe(lambda e, t=t, s=s, u=use[h], T=T, h=h: e.matmul(
                                o_ps[h][:, :],
                                vaug[:, s * 1040 + (2 * t + h) * 65:s * 1040 + (2 * t + h) * 65 + 65],
                                u, start=(s == 0), stop=(s == T - 1)),
                                waits=wv[h] + ([("dve", tk_oev[h])] if s == 0 else []),
                                guard=False)
                            tk_avt[buf][h] = tok_av
                            if s >= T - 4:
                                tk_pm[h] = tok_av
                    rb0 = tnr[0:1, 0:512]
                    rb1 = tnr[0:1, 512:1024]
                    p.dve(lambda e: e.reciprocal(qsb[0:1, 0:512], o_ps[0][64:65, :]),
                          waits=[("pe", tok_av)])
                    p.dve(lambda e: e.drain(), guard=False)
                    p.dve(lambda e: e.tensor_copy(rb0, qsb[0:1, 0:512]))
                    p.dve(lambda e: e.reciprocal(qsb[0:1, 0:512], o_ps[1][64:65, :]))
                    p.dve(lambda e: e.drain(), guard=False)
                    tok_rcb = p.dve(lambda e: e.tensor_copy(rb1, qsb[0:1, 0:512]))
                    p.pe(lambda e: e.matmul(bcb_ps[:, :], c["ind_lo"][:, :], rb0,
                                            start=True, stop=False), waits=[("dve", tok_rcb)])
                    tok_bc = p.pe(lambda e: e.matmul(bcb_ps[:, :], c["ind_hi"][:, :], rb1,
                                                     start=False, stop=True))
                    p.dve(lambda e: e.tensor_copy(bc_sb[:, :], bcb_ps[:, :]), waits=[("pe", tok_bc)])
                    p.dve(lambda e, cl=cl: e.tensor_tensor(
                        on_sb[0:64, cl * 512:(cl + 1) * 512], o_ps[0][0:64, :],
                        bc_sb[0:64, :], ALU.mult))
                    tk_oev[0] = p.now("dve")
                    p.dve(lambda e, cl=cl: e.tensor_tensor(
                        on_sb[64:128, cl * 512:(cl + 1) * 512], o_ps[1][0:64, :],
                        bc_sb[64:128, :], ALU.mult))
                    tk_oev[1] = p.now("dve")
                p.dve(lambda e: e.tensor_scalar(
                    oo_tmp[:, :], on_sb[:, 512:1024], t_sel[:, 1:2], None, ALU.mult))
                tok_sel = p.dve(lambda e: e.scalar_tensor_tensor(
                    oo_t[:, :], on_sb[:, 0:512], t_sel[:, 0:1], oo_tmp[:, :], ALU.mult, ALU.add))
                p.gp_dma(lambda e, t=t, pi=pi: e.dma_start(
                    out=d_scr_oo.ap()[:, t * 2048 + pi * 512:t * 2048 + (pi + 1) * 512],
                    in_=oo_t[:, :]), waits=[("dve", tok_sel)])

    if phases >= 2 and "oown" in dbg:
        p.gp_dma(lambda e: e.dma_start(out=dbg["oown"].ap()[:, :], in_=d_scr_oo.ap()[:, :]))

    knctx.close()

    x1 = sb("x1", [128, 8 * 2048], F32)
    c["onehot"] = sb("sc_onehot", list(CON["onehot"].shape), BF16)
    t_wrhl = sb("t_wrhl", [128, 2 * 64], BF16)
    t_b1a = sb("t_b1a", [128, NEXP * 32], F32)
    t_b1g = sb("t_b1g", [128, NEXP * 32], F32)
    t_b2b = sb("t_b2b", [8, E_MODEL], BF16)
    t_mrows = sb("t_mrows", [8, 2048], BF16)
    for (dst2, src2) in [(c["onehot"], d_con["onehot"]), (t_b1a, d_b1a), (t_b1g, d_b1g)]:
        p.sp_dma(lambda e, dst2=dst2, src2=src2: e.dma_start(out=dst2[:, :], in_=src2.ap()[:, :]))
    p.set_floor("sp", p.now("sp"))
    if phases >= 3:
      with ExitStack() as cctx:
        csb = lambda n, shp, dt: cctx.enter_context(nc.sbuf_tensor(n, shp, dt))
        wmodo = csb("wmodo", [128, 8 * 1024], BF16)
        wldo = csb("wldo", [128, 1024], F32)
        for dt in range(8):
            tok_ld = p.sp_dma(lambda e, dt=dt: e.dma_start(
                out=wldo[:, :], in_=d_wout.ap()[dt * 128:(dt + 1) * 128, :]))
            p.dve(lambda e, dt=dt: e.tensor_copy(
                wmodo[:, dt * 1024:(dt + 1) * 1024], wldo[:, :]), waits=[("sp", tok_ld)])
        wr_f = csb("wr_f", [128, 64], F32)
        wr_t = csb("wr_t", [128, 64], F32)
        tok_wr = p.sp_dma(lambda e: e.dma_start(out=wr_f[:, :], in_=d_rw.ap()[:, :]))
        p.dve(lambda e: e.tensor_copy(t_wrhl[:, 0:64], wr_f[:, :]), waits=[("sp", tok_wr)])
        p.dve(lambda e: e.tensor_tensor(wr_t[:, :], wr_f[:, :], t_wrhl[:, 0:64], ALU.subtract))
        p.dve(lambda e: e.tensor_copy(t_wrhl[:, 64:128], wr_t[:, :]))
        b2f = csb("b2f", [8, 1024], F32)
        tok_b2 = p.sp_dma(lambda e: e.dma_start(out=b2f[:, :], in_=d_b2.ap()[:, :]))
        p.dve(lambda e: e.tensor_copy(t_b2b[:, :], b2f[:, :]), waits=[("sp", tok_b2)])

        oo_rd = csb("oo_rd", [128, 8 * 512], BF16)
        xo = csb("xo", [128, 4 * 1024], F32)
        xt_sb = csb("xt_sb", [128, 512], F32)
        x1b = csb("x1b", [128, 512], BF16)
        x1sq = csb("x1sq", [128, 512], BF16)
        rowst = csb("rowst", [1, 8 * 512], F32)
        rwb = csb("rwb", [1, 4 * 512], BF16)
        t1f = csb("t1f", [128, 512], F32)
        t1t = csb("t1t", [128, 512], BF16)
        t1lo = csb("t1lo", [128, 8 * 512], BF16)
        tmpf = csb("tmpf", [128, 512], F32)
        lg_sb = csb("lg_sb", [8, 512], F32)
        mxc = csb("mxc", [128, 8], F32)
        mk_sb = csb("mk_sb", [128, 8], F32)
        ltp_s = csb("ltp_s", [128, 8], F32)

        x1_ps = cctx.enter_context(nc.psum_tensor("x1_ps", [128, 512], F32))
        xt_ps = cctx.enter_context(nc.psum_tensor("xt_ps", [128, 512], F32))
        sts_ps = cctx.enter_context(nc.psum_tensor("sts_ps", [1, 512], F32))
        stq_ps = cctx.enter_context(nc.psum_tensor("stq_ps", [1, 512], F32))
        mub_ps = cctx.enter_context(nc.psum_tensor("mub_ps", [128, 512], F32))
        rb_ps = cctx.enter_context(nc.psum_tensor("rb_ps", [128, 512], F32))
        rt_ps = cctx.enter_context(nc.psum_tensor("rt_ps", [8, 512], F32))
        lt_ps = xt_ps
        mr_ps = xt_ps

        row = lambda j: rowst[0:1, j * 512:(j + 1) * 512]
        for tc in range(4):
            tok_oo = None
            for t in range(8):
                tok_oo = p.sp_dma(lambda e, t=t, tc=tc: e.dma_start(
                    out=oo_rd[:, t * 512:(t + 1) * 512],
                    in_=d_scr_oo.ap()[:, t * 2048 + tc * 512:t * 2048 + (tc + 1) * 512]))
            tok_xo = None
            for i in range(4):
                r0 = tc * 512 + i * 128
                tok_xo = p.sp_dma(lambda e, i=i, r0=r0: e.dma_start(
                    out=xo[:, i * 1024:(i + 1) * 1024], in_=d_xown.ap()[r0:r0 + 128, :]))
            for dc in range(8):
                for i in range(4):
                    tok_tp = p.pe(lambda e, i=i, dc=dc: e.matmul(
                        xt_ps[:, i * 128:(i + 1) * 128],
                        xo[:, i * 1024 + dc * 128:i * 1024 + (dc + 1) * 128],
                        c["idf32"][:, :], start=True, stop=True, is_transpose=True),
                        waits=[("sp", tok_xo)])
                tok_xts = p.dve(lambda e: e.tensor_copy(xt_sb[:, :], xt_ps[:, :]),
                                waits=[("pe", tok_tp)])
                tok_mm = None
                for t in range(8):
                    tok_mm = p.pe(lambda e, t=t, dc=dc: e.matmul(
                        x1_ps[:, :], wmodo[:, t * 1024 + dc * 128:t * 1024 + (dc + 1) * 128],
                        oo_rd[:, t * 512:(t + 1) * 512], start=(t == 0), stop=(t == 7)),
                        waits=[("sp", tok_oo)])
                tok_x1 = p.dve(lambda e, dc=dc, tc=tc: e.tensor_tensor(
                    x1[:, dc * 2048 + tc * 512:dc * 2048 + (tc + 1) * 512],
                    x1_ps[:, :], xt_sb[:, :], ALU.add),
                    waits=[("pe", tok_mm), ("dve", tok_xts)])
                tok_x1b = p.act(lambda e, dc=dc, tc=tc: e.activation(
                    x1b[:, :], x1[:, dc * 2048 + tc * 512:dc * 2048 + (tc + 1) * 512], AF.Copy),
                    waits=[("dve", tok_x1)])
                tok_x1q = p.act(lambda e, dc=dc, tc=tc: e.activation(
                    x1sq[:, :], x1[:, dc * 2048 + tc * 512:dc * 2048 + (tc + 1) * 512], AF.Square))
                p.pe(lambda e, dc=dc: e.matmul(sts_ps[:, :], c["onesd"][:, :], x1b[:, :],
                                               start=(dc == 0), stop=(dc == 7)),
                     waits=[("act", tok_x1b)])
                tok_st = p.pe(lambda e, dc=dc: e.matmul(stq_ps[:, :], c["onesd"][:, :], x1sq[:, :],
                                                        start=(dc == 0), stop=(dc == 7)),
                              waits=[("act", tok_x1q)])
            p.dve(lambda e: e.tensor_scalar(row(0), sts_ps[:, :], 1.0 / 1024, None, ALU.mult),
                  waits=[("pe", tok_st)])
            p.dve(lambda e: e.tensor_scalar(row(1), stq_ps[:, :], 1.0 / 1024, None, ALU.mult))
            p.dve(lambda e: e.tensor_tensor(row(2), row(0), row(0), ALU.mult))
            p.dve(lambda e: e.tensor_tensor(row(3), row(1), row(2), ALU.subtract))
            tk = p.act(lambda e: e.activation(row(4), row(3), AF.Sqrt, bias=LN_EPS))
            p.dve(lambda e: e.reciprocal(row(5), row(4)), waits=[("act", tk)])
            p.dve(lambda e: e.drain(), guard=False)
            p.dve(lambda e: e.tensor_copy(rwb[0:1, 0:512], row(0)))
            p.dve(lambda e: e.tensor_copy(row(6), rwb[0:1, 0:512]))
            p.dve(lambda e: e.tensor_tensor(row(7), row(0), row(6), ALU.subtract))
            p.dve(lambda e: e.tensor_copy(rwb[0:1, 512:1024], row(7)))
            p.dve(lambda e: e.tensor_copy(rwb[0:1, 1024:1536], row(5)))
            p.dve(lambda e: e.tensor_copy(row(6), rwb[0:1, 1024:1536]))
            p.dve(lambda e: e.tensor_tensor(row(7), row(5), row(6), ALU.subtract))
            tok_rw = p.dve(lambda e: e.tensor_copy(rwb[0:1, 1536:2048], row(7)))
            p.pe(lambda e: e.matmul(mub_ps[:, :], c["ones1x128"][:, :], rwb[0:1, 0:512],
                                    start=True, stop=False), waits=[("dve", tok_rw)])
            tok_mub = p.pe(lambda e: e.matmul(mub_ps[:, :], c["ones1x128"][:, :],
                                              rwb[0:1, 512:1024], start=False, stop=True))
            p.pe(lambda e: e.matmul(rb_ps[:, :], c["ones1x128"][:, :], rwb[0:1, 1024:1536],
                                    start=True, stop=False))
            tok_rbb = p.pe(lambda e: e.matmul(rb_ps[:, :], c["ones1x128"][:, :],
                                              rwb[0:1, 1536:2048], start=False, stop=True))
            for dc in range(8):
                p.dve(lambda e, dc=dc, tc=tc: e.tensor_tensor(
                    tmpf[:, :], x1[:, dc * 2048 + tc * 512:dc * 2048 + (tc + 1) * 512],
                    mub_ps[:, :], ALU.subtract), waits=[("pe", tok_mub), ("pe", tok_rbb)])
                p.dve(lambda e: e.tensor_tensor(t1f[:, :], tmpf[:, :], rb_ps[:, :], ALU.mult))
                tok_t1 = p.dve(lambda e: e.tensor_copy(t1t[:, :], t1f[:, :]))
                tok_t1l = p.dve(lambda e, dc=dc: e.tensor_tensor(
                    t1lo[:, dc * 512:(dc + 1) * 512], t1f[:, :], t1t[:, :], ALU.subtract))
                p.gp_dma(lambda e, dc=dc, tc=tc: e.dma_start(
                    out=d_scr_t1.ap()[:, dc * 2048 + tc * 512:dc * 2048 + (tc + 1) * 512],
                    in_=t1t[:, :]), waits=[("dve", tok_t1)])
                p.pe(lambda e, dc=dc: e.matmul(
                    rt_ps[:, :], t_wrhl[:, dc * 8:(dc + 1) * 8], t1t[:, :],
                    start=(dc == 0), stop=False), waits=[("dve", tok_t1)])
                p.pe(lambda e, dc=dc: e.matmul(
                    rt_ps[:, :], t_wrhl[:, 64 + dc * 8:64 + (dc + 1) * 8], t1t[:, :],
                    start=False, stop=False))
                tok_rt = p.pe(lambda e, dc=dc: e.matmul(
                    rt_ps[:, :], t_wrhl[:, dc * 8:(dc + 1) * 8], t1lo[:, dc * 512:(dc + 1) * 512],
                    start=False, stop=(dc == 7)), waits=[("dve", tok_t1l)])
            tok_lg = p.act(lambda e: e.add(lg_sb[:, :], rt_ps[:, :], t_rb[0:8, 0:1]), waits=[("pe", tok_rt)])
            for j in range(4):
                tok_lt = p.pe(lambda e, j=j: e.transpose(
                    lt_ps[:, 0:8], lg_sb[:, j * 128:(j + 1) * 128], c["idf32"][0:8, 0:8]),
                    waits=[("act", tok_lg)])
                p.dve(lambda e: e.tensor_copy(ltp_s[:, :], lt_ps[:, 0:8]), waits=[("pe", tok_lt)])
                p.dve(lambda e: e.drain(), guard=False)
                p.dve(lambda e: e.tensor_reduce(mxc[:, 0:1], ltp_s[:, :], AX.X, ALU.max))
                p.dve(lambda e: e.drain(), guard=False)
                p.dve(lambda e: e.memset(mxc[:, 1:2], 0.0))
                p.dve(lambda e: e.drain(), guard=False)
                for ee in range(8):
                    tcur, tnxt = (1, 5) if ee % 2 == 0 else (5, 1)
                    p.dve(lambda e, ee=ee: e.tensor_tensor(
                        mxc[:, 2:3], ltp_s[:, ee:ee + 1], mxc[:, 0:1], ALU.is_ge))
                    p.dve(lambda e: e.drain(), guard=False)
                    p.dve(lambda e, tcur=tcur: e.tensor_scalar(
                        mxc[:, 3:4], mxc[:, tcur:tcur + 1], -1.0, 1.0, ALU.mult, ALU.add))
                    p.dve(lambda e: e.drain(), guard=False)
                    p.dve(lambda e: e.tensor_tensor(mxc[:, 4:5], mxc[:, 2:3], mxc[:, 3:4], ALU.mult))
                    p.dve(lambda e: e.drain(), guard=False)
                    p.dve(lambda e, tcur=tcur, tnxt=tnxt: e.tensor_tensor(
                        mxc[:, tnxt:tnxt + 1], mxc[:, tcur:tcur + 1], mxc[:, 4:5], ALU.add))
                    p.dve(lambda e, ee=ee: e.tensor_copy(mk_sb[:, ee:ee + 1], mxc[:, 4:5]))
                    p.dve(lambda e: e.drain(), guard=False)
                tok_mk = p.now("dve")
                tok_mt = p.pe(lambda e: e.transpose(mr_ps[0:8, 128:256], mk_sb[:, :], c["idf32"][:, :]),
                              waits=[("dve", tok_mk)])
                p.dve(lambda e, tc=tc, j=j: e.tensor_copy(
                    t_mrows[:, tc * 512 + j * 128:tc * 512 + (j + 1) * 128], mr_ps[0:8, 128:256]),
                    waits=[("pe", tok_mt)])

    if phases >= 3 and "x1" in dbg:
        p.gp_dma(lambda e: e.dma_start(out=dbg["x1"].ap()[:, :], in_=x1[:, :]))
    if "mrows" in dbg:
        p.gp_dma(lambda e: e.dma_start(out=dbg["mrows"].ap()[:, :], in_=t_mrows[:, :]))
    if "t1" in dbg:
        p.gp_dma(lambda e: e.dma_start(out=dbg["t1"].ap()[:, :], in_=d_scr_t1.ap()[:, :]))

    if phases >= 4:
      with ExitStack() as dctx:
        dsb = lambda n, shp, dt: dctx.enter_context(nc.sbuf_tensor(n, shp, dt))
        u_sb = dsb("u_sb", [128, 32 * 1024], BF16)
        t1rd = dsb("t1rd", [128, 8 * 1024], BF16)
        w1ld = dsb("w1ld", [128, 8 * 256], F32)
        apb = [dsb(f"apb{j}", [128, 8 * 256], BF16) for j in range(2)]
        gpb = [dsb(f"gpb{j}", [128, 8 * 256], BF16) for j in range(2)]
        w2c = [dsb(f"w2c{j}", [128, 512], F32) for j in range(2)]
        w2b = [dsb(f"w2b{j}", [128, 512], BF16) for j in range(2)]
        sg_t = [dsb(f"sg_t{j}", [128, 512], F32) for j in range(2)]
        aa_t = [dsb(f"aa_t{j}", [128, 512], F32) for j in range(2)]
        mb_sb = dsb("mb_sb", [128, 2 * 512], BF16)
        yt = sg_t[0]

        tk_conv = {0: None, 1: None}
        tk_mm_piece = {0: None, 1: None}
        tk_w2mm = {0: None, 1: None}
        tk_w2ld = {0: None, 1: None}
        tk_w2cv = {0: None, 1: None}
        tk_ag_evac = {0: None, 1: None}
        tk_umul = {0: None, 1: None}
        tk_ap_dma = None
        tk_t1r = None
        tk_x1add = None

        W1B = E_MODEL * 2 * HIDDEN
        for e_i in range(NEXP):
            if e_i == 1 and "x1e0" in dbg:
                p.gp_dma(lambda e: e.dma_start(out=dbg["x1e0"].ap()[:, :], in_=x1[:, :]),
                         waits=[("dve", p.now("dve"))])
            with ExitStack() as uctx:
                ag_ps = [uctx.enter_context(nc.psum_tensor(f"ag_ps{j}_{e_i}", [128, 512], F32))
                         for j in range(4)]
                for fh in range(2):
                    tk_t1r = p.sp_dma(lambda e, fh=fh: e.dma_start(
                        out=t1rd[:, :],
                        in_=bass.AP(d_scr_t1, fh * 1024, [[16384, 128], [2048, 8], [1, 1024]])),
                        waits=[("pe", tk_mm_piece[0]), ("pe", tk_mm_piece[1]),
                               ("pe", p.now("pe") if (e_i == 0 and fh == 0) else None)],
                        guard=(e_i == 0 and fh == 0), sem="t1s")
                    gidx = 0
                    for p4 in range(16):
                        j = p4 % 2
                        tk_ap_dma = p.sp_dma(lambda e, e_i=e_i, p4=p4: e.dma_start(
                            out=w1ld[:, :],
                            in_=bass.AP(d_w1, e_i * W1B + p4 * 256,
                                        [[2 * HIDDEN, 128], [128 * 2 * HIDDEN, 8], [1, 256]])),
                            waits=[("dve", tk_conv[1 - j])], guard=False, sem="w1s")
                        tok_ca = p.dve(lambda e, j=j: e.tensor_copy(apb[j][:, :], w1ld[:, :]),
                                       waits=[("w1s", tk_ap_dma), ("pe", tk_mm_piece[j])],
                                       guard=False)
                        tk_ap_dma = p.sp_dma(lambda e, e_i=e_i, p4=p4: e.dma_start(
                            out=w1ld[:, :],
                            in_=bass.AP(d_w1, e_i * W1B + HIDDEN + p4 * 256,
                                        [[2 * HIDDEN, 128], [128 * 2 * HIDDEN, 8], [1, 256]])),
                            waits=[("dve", tok_ca)], guard=False, sem="w1s")
                        tk_conv[j] = p.dve(lambda e, j=j: e.tensor_copy(gpb[j][:, :], w1ld[:, :]),
                                           waits=[("w1s", tk_ap_dma)], guard=False)
                        for hc2 in range(2):
                            hc = p4 * 2 + hc2
                            for qq in range(2):
                                bj = gidx % 2
                                gidx += 1
                                for dt in range(8):
                                    p.pe(lambda e, j=j, dt=dt, hc2=hc2, qq=qq, bj=bj: e.matmul(
                                        ag_ps[bj][:, :],
                                        apb[j][:, dt * 256 + hc2 * 128:dt * 256 + (hc2 + 1) * 128],
                                        t1rd[:, dt * 1024 + qq * 512:dt * 1024 + (qq + 1) * 512],
                                        start=(dt == 0), stop=(dt == 7)),
                                        waits=([("dve", tk_conv[j]), ("t1s", tk_t1r),
                                                ("act", tk_ag_evac[bj])] if dt == 0 else []),
                                        guard=False)
                                    tk_mm_piece[j] = p.pe(lambda e, j=j, dt=dt, hc2=hc2, qq=qq, bj=bj: e.matmul(
                                        ag_ps[2 + bj][:, :],
                                        gpb[j][:, dt * 256 + hc2 * 128:dt * 256 + (hc2 + 1) * 128],
                                        t1rd[:, dt * 1024 + qq * 512:dt * 1024 + (qq + 1) * 512],
                                        start=(dt == 0), stop=(dt == 7)), guard=False)
                                tok_sg = p.act(lambda e, hc=hc, e_i=e_i, bj=bj: e.activation(
                                    sg_t[bj][:, :], ag_ps[2 + bj][:, :], AF.Sigmoid,
                                    bias=t_b1g[:, e_i * 32 + hc:e_i * 32 + hc + 1]),
                                    waits=[("pe", tk_mm_piece[j]), ("dve", tk_umul[bj]),
                                           ("dve", tk_x1add)],
                                    guard=False)
                                tk_ag_evac[bj] = p.act(lambda e, hc=hc, e_i=e_i, bj=bj: e.add(
                                    aa_t[bj][:, :], ag_ps[bj][:, :],
                                    t_b1a[:, e_i * 32 + hc:e_i * 32 + hc + 1]), guard=False)
                                tk_umul[bj] = p.dve(lambda e, hc=hc, qq=qq, bj=bj: e.tensor_tensor(
                                    u_sb[:, hc * 1024 + qq * 512:hc * 1024 + (qq + 1) * 512],
                                    aa_t[bj][:, :], sg_t[bj][:, :], ALU.mult),
                                    waits=[("act", tk_ag_evac[bj])], guard=False)
                    tok_u = p.now("dve")
                    if e_i == 7 and fh == 1 and "u0" in dbg:
                        p.gp_dma(lambda e: e.dma_start(out=dbg["u0"].ap()[:, 0:16384],
                                                       in_=u_sb[:, 0:16384]), waits=[("dve", tok_u)])
                        p.gp_dma(lambda e: e.dma_start(out=dbg["u0"].ap()[:, 16384:32768],
                                                       in_=u_sb[:, 16384:32768]), waits=[("dve", tok_u)])
                    tk_mb = []
                    for qq in range(2):
                        tcq = fh * 1024 + qq * 512
                        t_mbm = p.pe(lambda e, e_i=e_i, tcq=tcq: e.matmul(
                            ag_ps[0][:, :], c["onehot"][:, e_i * 128:(e_i + 1) * 128],
                            t_mrows[0:8, tcq:tcq + 512], start=True, stop=True),
                            waits=[("act", tk_ag_evac[0]), ("dve", tk_umul[0])] +
                                  ([("dve", tk_mb[0])] if qq == 1 else []), guard=False)
                        tk_mb.append(p.dve(lambda e, qq=qq: e.tensor_copy(
                            mb_sb[:, qq * 512:(qq + 1) * 512], ag_ps[0][:, :]),
                            waits=[("pe", t_mbm)], guard=False))
                    with ExitStack() as wctx:
                        y_ps = [wctx.enter_context(nc.psum_tensor(f"y_ps{j}_{e_i}_{fh}", [128, 512], F32))
                                for j in range(4)]
                        for qq in range(2):
                            tcq = fh * 1024 + qq * 512
                            for dh in range(2):
                                tok_ystop = {}
                                for hc in range(32):
                                    wj = hc % 2
                                    wsem = "w2a" if wj == 0 else "w2b"
                                    tk_w2ld[wj] = p.sp_dma(lambda e, e_i=e_i, hc=hc, dh=dh, wj=wj: e.dma_start(
                                        out=w2c[wj][:, :],
                                        in_=d_w2.ap()[e_i, hc * 128:(hc + 1) * 128,
                                                      dh * 512:(dh + 1) * 512]),
                                        waits=[("dve", tk_w2cv[wj])], guard=False, sem=wsem)
                                    tk_w2cv[wj] = p.dve(lambda e, wj=wj: e.tensor_copy(
                                        w2b[wj][:, :], w2c[wj][:, :]),
                                        waits=[(wsem, tk_w2ld[wj]), ("pe", tk_w2mm[wj])], guard=False)
                                    for dc4 in range(4):
                                        dc = dh * 4 + dc4
                                        tk_w2mm[wj] = p.pe(lambda e, hc=hc, qq=qq, dc4=dc4, wj=wj: e.matmul(
                                            y_ps[dc4][:, :], w2b[wj][:, dc4 * 128:(dc4 + 1) * 128],
                                            u_sb[:, hc * 1024 + qq * 512:hc * 1024 + (qq + 1) * 512],
                                            start=(hc == 0), stop=False),
                                            waits=([("dve", tk_w2cv[wj])] if dc4 == 0 else []) +
                                                  ([("dve", tok_u), ("dve", tk_x1add)] if hc == 0 else []),
                                            guard=False)
                                for dc4 in range(4):
                                    dc = dh * 4 + dc4
                                    tok_ystop[dc4] = p.pe(lambda e, dc=dc, dc4=dc4, tcq=tcq: e.matmul(
                                        y_ps[dc4][:, :], t_b2b[:, dc * 128:(dc + 1) * 128],
                                        t_mrows[0:8, tcq:tcq + 512], start=False, stop=True),
                                        guard=False)
                                for dc4 in range(4):
                                    dc = dh * 4 + dc4
                                    p.dve(lambda e, dc4=dc4, qq=qq: e.tensor_tensor(
                                        yt[:, :], y_ps[dc4][:, :], mb_sb[:, qq * 512:(qq + 1) * 512],
                                        ALU.mult),
                                        waits=[("pe", tok_ystop[dc4]), ("dve", tk_mb[qq])], guard=False)
                                    tk_x1add = p.dve(lambda e, dc=dc, tcq=tcq: e.tensor_tensor(
                                        x1[:, dc * 2048 + tcq:dc * 2048 + tcq + 512],
                                        x1[:, dc * 2048 + tcq:dc * 2048 + tcq + 512], yt[:, :],
                                        ALU.add), guard=False)

    if phases >= 4 and "mrend" in dbg:
        p.gp_dma(lambda e: e.dma_start(out=dbg["mrend"].ap()[:, :], in_=t_mrows[:, :]))

    if phases >= 5:
      with ExitStack() as ectx:
        osb = ectx.enter_context(nc.sbuf_tensor("osb", [128, 1024], F32))
        o_ps2 = ectx.enter_context(nc.psum_tensor("o_ps2", [128, 1024], F32))
        for tt in range(16):
            for dc in range(8):
                p.pe(lambda e, tt=tt, dc=dc: e.matmul(
                    o_ps2[:, dc * 128:(dc + 1) * 128],
                    x1[:, dc * 2048 + tt * 128:dc * 2048 + (tt + 1) * 128],
                    c["idf32"][:, :], start=True, stop=True, is_transpose=True))
            tok_cp = p.act(lambda e: e.activation(osb[:, :], o_ps2[:, :], AF.Copy),
                           waits=[("pe", p.now("pe")), ("gp", p.now("gp"))])
            p.gp_dma(lambda e, tt=tt: e.dma_start(
                out=d_y.ap()[tt * 128:(tt + 1) * 128, :], in_=osb[:, :]),
                waits=[("act", tok_cp)])

    p.gp(lambda e: e.nop(), waits=[("gp", p.now("gp"))], guard=False, inc=0)

    with ExitStack() as sctx:
        sems = {e: sctx.enter_context(nc.semaphore(f"sem_{e}")) for e in
                list(ENGINES) + list(Prog.EXTRA_SEMS)}
        with nc.Block() as block:
            p.replay(block, sems)
    ctx.close()
    return nc


def own_chunks(pp):
    return [PAIRS[pi][pp] for pi in range(4)]


def stage_inputs(inputs, core):
    b, pp = core // 2, core % 2
    x = np.asarray(inputs["L0_attn_x"], dtype=np.float32)
    own = own_chunks(pp)
    x_own = np.concatenate([x[b, g * 512:(g + 1) * 512] for g in own], axis=0)
    gamma = np.asarray(inputs["L0_attn_ln_gamma"], np.float32).reshape(8, 128).T.copy()
    beta = np.asarray(inputs["L0_attn_ln_beta"], np.float32).reshape(8, 128).T.copy()
    qks = np.broadcast_to(np.asarray(inputs["L0_attn_qkscale"], np.float32).reshape(1, 1),
                          (128, 1)).copy()
    sel = np.zeros((128, 2), np.float32)
    sel[:, pp] = 1.0
    b1 = np.asarray(inputs["L0_ffn_b1"], np.float32)
    b1a = np.concatenate([b1[e, :4096].reshape(32, 128).T for e in range(8)], axis=1)
    b1g = np.concatenate([b1[e, 4096:].reshape(32, 128).T for e in range(8)], axis=1)
    m = {
        "x_batch": np.ascontiguousarray(x[b]),
        "x_own": np.ascontiguousarray(x_own),
        "w_qkv": np.ascontiguousarray(np.asarray(inputs["L0_attn_w_qkv"], np.float32)),
        "w_out": np.ascontiguousarray(np.asarray(inputs["L0_attn_w_out"], np.float32)),
        "gamma": gamma, "beta": beta, "qkscale": qks, "sel": sel,
        "router_w": np.ascontiguousarray(
            np.asarray(inputs["L0_ffn_router_w"], np.float32).reshape(8, 128, 8)
            .transpose(1, 0, 2).reshape(128, 64)),
        "router_b": np.asarray(inputs["L0_ffn_router_b"], np.float32).reshape(8, 1).copy(),
        "w1": np.ascontiguousarray(np.asarray(inputs["L0_ffn_w1"], np.float32)),
        "b1a": np.ascontiguousarray(b1a), "b1g": np.ascontiguousarray(b1g),
        "w2": np.ascontiguousarray(np.asarray(inputs["L0_ffn_w2"], np.float32)),
        "b2": np.ascontiguousarray(np.asarray(inputs["L0_ffn_b2"], np.float32)),
    }
    for k, v in host_constants().items():
        m["c_" + k] = np.ascontiguousarray(v)
    return m


def kernel(**inputs):
    nc = build_program()
    in_maps = [stage_inputs(inputs, cc) for cc in range(8)]
    res = run_bass_kernel_spmd(nc, in_maps, core_ids=list(range(8)))
    out = np.zeros((4, 4096, 1024), dtype=np.float32)
    for cc in range(8):
        b, pp = cc // 2, cc % 2
        y = np.asarray(res.results[cc]["y"], np.float32)
        for pi, g in enumerate(own_chunks(pp)):
            out[b, g * 512:(g + 1) * 512] = y[pi * 512:(pi + 1) * 512]
    return out
